# version 01
# baseline (speedup 1.0000x reference)
import numpy as np
import ml_dtypes
import concourse.bass as bass
import concourse.mybir as mybir
from concourse.alu_op_type import AluOpType as OP
from concourse.bass_utils import run_bass_kernel_spmd

F32 = mybir.dt.float32
BF16 = mybir.dt.bfloat16
AF = mybir.ActivationFunctionType
AX = mybir.AxisListType.X

N = 2048
C = 1024
H = 8
NT = N // 128
NC = C // 128
LAMBDA_INIT = 0.5


def host_consts():
    d = {}
    d["c_ident"] = np.eye(128, dtype=np.float32)
    e2 = np.zeros((2, 128), np.float32); e2[0] = 1.0
    d["c_e2"] = e2
    op_ = np.zeros((128, 2), np.float32)
    op_[:64, 0] = 1.0; op_[64:, 1] = 1.0
    d["c_onespair"] = op_
    d["c_opT"] = np.ascontiguousarray(op_.T)
    d["c_w128"] = np.full((128, 1), 1.0 / 128, np.float32)
    d["c_sign"] = np.concatenate([np.ones(64), -np.ones(64)]).astype(np.float32)[:, None]
    return d


class Ctx:
    def __init__(self, nc):
        self.nc = nc
        self._sems = {}
        self.cnt = {}

    def sem(self, name):
        if name not in self._sems:
            self._sems[name] = self.nc.alloc_semaphore(name)
            self.cnt[name] = 0
        return self._sems[name]

    def inc(self, instr, name, n=1):
        instr.then_inc(self.sem(name), n)
        self.cnt[name] += n

    def wait_all(self, eng, name):
        eng.wait_ge(self.sem(name), self.cnt[name])


class L0Attn:
    def __init__(self, nc, ctx, io, dbg=()):
        self.nc, self.ctx, self.io = nc, ctx, io
        self.dbg = set(dbg)
        A = nc.alloc_sbuf_tensor
        self.xstage = A("xstage", [128, 2 * C], BF16)
        self.xT = A("xT", [128, NC * N], BF16)
        self.qcat = A("qcat", [128, H * N], BF16)
        self.kcat = A("kcat", [128, H * N], BF16)
        self.vsl = A("vsl", [128, NT * H * 130], BF16)
        self.ybuf = A("ybuf", [128, H * N], BF16)
        self.wsl = A("wsl", [128, NC * 128], BF16)
        self.wos_all = self.vsl
        self.wj = A("wj", [128, 1024], BF16)
        self.wt_stage = A("wt_stage", [128, 1024], BF16)
        self.ident_b = A("ident_b", [128, 128], BF16)
        self.e2_f = A("e2_f", [2, 128], F32)
        self.e2_b = A("e2_b", [2, 128], BF16)
        self.opT_f = A("opT_f", [2, 128], F32)
        self.onespair_b = A("onespair_b", [128, 2], BF16)
        self.w128_b = A("w128_b", [128, 1], BF16)
        self.sq_sb = A("sq_sb", [128, 512], BF16)
        self.rb_sb = A("rb_sb", [128, 512], F32)
        self.g_sb = A("g_sb", [128, 256], BF16)
        self.s1_sb = A("s1_sb", [128, 256], F32)
        self.s2_sb = A("s2_sb", [128, 256], F32)
        self.e1_sb = A("e1_sb", [128, 256], BF16)
        self.e2sb = A("e2sb", [128, 256], BF16)
        self.qd_sb = A("qd_sb", [128, 256], BF16)
        self.ssf = A("ssf", [2, 512], F32)
        self.rQ = A("rQ", [2, 512], F32)
        self.tmpa = A("tmpa", [128, 256], F32)
        self.tmpb = A("tmpb", [128, 256], F32)
        self.r1st = A("r1st", [2, 256], F32)
        self.r2st = A("r2st", [2, 256], F32)
        self.rb1 = A("rb1", [128, 256], F32)
        self.rb2 = A("rb2", [128, 256], F32)
        self.rstd2 = A("rstd2", [2, 512], F32)
        self.lam_neg = A("lam_neg", [1, 1], F32)
        self.lrow1 = A("lrow1", [1, 256], F32)
        self.lprod1 = A("lprod1", [1, 128], F32)
        self.lsum2 = A("lsum2", [1, 2], F32)
        self.lexp2 = A("lexp2", [1, 2], F32)
        self.csrow = A("csrow", [1, 8], F32)
        self.dgrow = A("dgrow", [1, 8], F32)
        self.chdrow = A("chdrow", [1, 8], F32)
        self.craw = A("craw", [1, 1], F32)
        self.stage28 = A("stage28", [2, 8], F32)
        self.cs_bc = A("cs_bc", [128, 8], F32)
        self.dgs_bc = A("dgs_bc", [128, 8], F32)
        self.sign_col = A("sign_col", [128, 1], F32)
        self.scal1 = A("scal1", [1, 8], F32)
        self.scal2 = A("scal2", [1, 8], F32)
        self.bo2 = A("bo2", [2, C], BF16)
        self.subln_c = A("subln_c", [128, 1], F32)
        self.tsb = A("tsb", [128, C], F32)
        self.bn6 = A("bn6", [128, 12], F32)
        self.mv2 = A("mv2", [128, 2], F32)
        self.lnr = A("lnr", [128, 1], F32)
        self.lnm = A("lnm", [128, 1], F32)
        self.xout_sb = A("xout_sb", [128, C], F32)
        self.sqy = A("sqy", [128, 512], BF16)
        self.rstd_r = A("rstd_r", [1, 512], F32)
        self.blkstage = A("blkstage", [128, 128], BF16)
        self.eps_col = A("eps_col", [128, 1], F32)
        self.ones_col_b = A("ones_col_b", [128, 1], BF16)

        self.fencea = A("fencea", [128, 8], F32)
        self.fenceb = A("fenceb", [128, 8], F32)
        self.craw2 = A("craw2", [1, 1], F32)
        self.P = [nc.alloc_psum_tensor(f"pool{i}", [128, 512], F32) for i in range(7)]
        self.PT = nc.alloc_psum_tensor("poolT", [128, 256], BF16)

    def vf(self, v):
        self.ctx.inc(v.tensor_copy(self.fenceb[:], self.fencea[:]), "vS")

    def af(self, a):
        self.ctx.inc(a.activation(self.fenceb[:], self.fencea[:], AF.Identity), "aS")

    def p0_consts(self):
        nc, ctx, io = self.nc, self.ctx, self.io
        with nc.Block("p0c") as blk:
            def g_(g):
                for dst, src in [(self.ident_b[:], io["c_ident"]), (self.e2_b[:], io["c_e2"]),
                                 (self.onespair_b[:], io["c_onespair"]), (self.w128_b[:], io["c_w128"])]:
                    ctx.inc(g.dma_start(dst, src), "ld", 16)
            blk.gpsimd(g_)

            def s_(s):
                for dst, src in [(self.e2_f[:], io["c_e2"]), (self.opT_f[:], io["c_opT"]),
                                 (self.lrow1[0:1, 0:64], io["lambda_q1"]),
                                 (self.lrow1[0:1, 64:128], io["lambda_k1"]), (self.lrow1[0:1, 128:192], io["lambda_q2"]),
                                 (self.lrow1[0:1, 192:256], io["lambda_k2"]), (self.craw[:], io["cos_raw"]),
                                 (self.chdrow[:], io["cos_head_delta"]), (self.dgrow[:], io["delta_gain"]),
                                 (self.sign_col[:], io["c_sign"]), (self.subln_c[:], io["subln"])]:
                    ctx.inc(s.dma_start(dst, src), "ld", 16)
            blk.sync(s_)

            def v_(v):
                ctx.wait_all(v, "ld")
                for t in [self.rQ, self.r1st, self.r2st, self.rstd2, self.stage28]:
                    v.memset(t[:], 0.0)
                v.memset(self.eps_col[:], 1e-5)
                v.memset(self.ones_col_b[:], 1.0)
                ctx.inc(v.memset(self.bo2[:], 0.0), "vS")
                v.tensor_tensor(self.lprod1[0:1, 0:64], self.lrow1[0:1, 0:64], self.lrow1[0:1, 64:128], OP.mult)
                v.tensor_tensor(self.lprod1[0:1, 64:128], self.lrow1[0:1, 128:192], self.lrow1[0:1, 192:256], OP.mult)
                v.reduce_sum(self.lsum2[0:1, 0:1], self.lprod1[0:1, 0:64], axis=AX)
                v.reduce_sum(self.lsum2[0:1, 1:2], self.lprod1[0:1, 64:128], axis=AX)
                v.reduce_sum(self.scal1[:, 0:1], self.chdrow[:], axis=AX)
                self.vf(v)
                self.vf(v)
                v.tensor_scalar(self.scal1[:, 0:1], self.scal1[:, 0:1], 1.0 / H, None, OP.mult)
                self.vf(v)
                ctx.inc(v.tensor_copy(self.fencea[:], self.fenceb[:]), "vS")
            blk.vector(v_)

            def a_(a):
                ctx.wait_all(a, "vS")
                a.activation(self.lexp2[:], self.lsum2[:], AF.Exp)
                a.activation(self.craw[:], self.craw[:], AF.Exp, scale=-1.0)
                self.af(a)
                self.af(a)
            blk.scalar(a_)

            def gbo_(g):
                ctx.wait_all(g, "vS")
                ctx.inc(g.dma_start(self.bo2[0:1, :], io["bo"]), "ld", 16)
            blk.gpsimd(gbo_)

            def v2_(v):
                ctx.wait_all(v, "aS")
                v.tensor_tensor(self.lam_neg[:], self.lexp2[0:1, 1:2], self.lexp2[0:1, 0:1], OP.subtract)
                self.vf(v)
                v.tensor_scalar(self.lam_neg[:], self.lam_neg[:], LAMBDA_INIT, None, OP.subtract)
                v.tensor_scalar(self.craw[:], self.craw[:], 1.0, None, OP.add)
                self.vf(v)
                v.reciprocal(self.craw2[:], self.craw[:])
                self.vf(v)
                self.vf(v)
                v.tensor_scalar(self.craw[:], self.craw2[:], 15.0, None, OP.mult)
                v.tensor_scalar(self.scal2[:], self.chdrow[:], self.scal1[:, 0:1], None, OP.subtract)
                self.vf(v)
                ctx.inc(v.tensor_copy(self.fencea[:], self.fenceb[:]), "vS")
            blk.vector(v2_)

            def a2_(a):
                ctx.wait_all(a, "vS")
                a.activation(self.scal2[:], self.scal2[:], AF.Tanh)
                self.af(a)
                self.af(a)
            blk.scalar(a2_)

            def v3_(v):
                ctx.wait_all(v, "aS")
                v.tensor_scalar(self.scal2[:], self.scal2[:], 0.5, 1.0, OP.mult, OP.add)
                self.vf(v)
                v.tensor_scalar(self.csrow[:], self.scal2[:], self.craw[0:1, 0:1], None, OP.mult)
                self.vf(v)
                v.tensor_copy(self.stage28[0:1, :], self.csrow[:])
                self.vf(v)
                ctx.inc(v.tensor_copy(self.fencea[:], self.fenceb[:]), "vS")
            blk.vector(v3_)

            def t_(t):
                ctx.wait_all(t, "vS")
                ctx.inc(t.matmul(self.P[0][:, 0:8], self.e2_f[:], self.stage28[:], start=True, stop=True), "tS")
            blk.tensor(t_)

            def v4_(v):
                ctx.wait_all(v, "tS")
                v.tensor_copy(self.cs_bc[:], self.P[0][:, 0:8])
                v.tensor_copy(self.stage28[0:1, :], self.dgrow[:])
                self.vf(v)
                ctx.inc(v.tensor_copy(self.fencea[:], self.fenceb[:]), "vS")
            blk.vector(v4_)

            def t2_(t):
                ctx.wait_all(t, "vS")
                ctx.inc(t.matmul(self.P[1][:, 0:8], self.e2_f[:], self.stage28[:], start=True, stop=True), "tS")
            blk.tensor(t2_)

            def v5_(v):
                ctx.wait_all(v, "tS")
                ctx.inc(v.tensor_scalar(self.dgs_bc[:], self.P[1][:, 0:8], self.sign_col[:], None, OP.mult), "vS")
            blk.vector(v5_)

    def p0_weights(self):
        nc, ctx, io = self.nc, self.ctx, self.io
        scr = io["wT_scr"]
        scro = io["woT_scr"]
        with nc.Block("p0w") as blk:
            for wi, wname in enumerate(["Wq", "Wk", "Wv"]):
                for jt in range(8):
                    def g_(g, wname=wname, jt=jt):
                        ctx.wait_all(g, "st")
                        ctx.inc(g.dma_start(self.wj[:], io[wname][jt * 128:(jt + 1) * 128, :]), "ld", 16)
                    blk.gpsimd(g_)
                    ld_t = ctx.cnt["ld"]
                    for cc in range(8):
                        slot = cc % 2
                        def t_(t, cc=cc, slot=slot, ld_t=ld_t):
                            t.wait_ge(ctx.sem("ld"), ld_t)
                            ctx.wait_all(t, "vS")
                            ctx.inc(t.transpose(self.PT[:, slot * 128:slot * 128 + 128],
                                                self.wj[:, cc * 128:(cc + 1) * 128], self.ident_b[:]), "tS")
                        blk.tensor(t_)

                        def v_(v, cc=cc, slot=slot):
                            ctx.wait_all(v, "tS")
                            if cc == 0:
                                ctx.wait_all(v, "st")
                            ctx.inc(v.tensor_copy(self.wt_stage[:, cc * 128:(cc + 1) * 128],
                                                  self.PT[:, slot * 128:slot * 128 + 128]), "vS")
                        blk.vector(v_)

                    def vfx_(v):
                        self.vf(v)
                    blk.vector(vfx_)

                    def s_(s, wi=wi, jt=jt):
                        ctx.wait_all(s, "vS")
                        ctx.inc(s.dma_start(scr[wi, jt], self.wt_stage[:]), "st", 16)
                    blk.sync(s_)

            for ct in range(8):
                def g2_(g, ct=ct):
                    ctx.wait_all(g, "st")
                    ctx.inc(g.dma_start(self.wj[:], io["Wo"][ct * 128:(ct + 1) * 128, :]), "ld", 16)
                blk.gpsimd(g2_)
                ld_t = ctx.cnt["ld"]
                for jj in range(8):
                    slot = jj % 2
                    def t3_(t, jj=jj, slot=slot, ld_t=ld_t):
                        t.wait_ge(ctx.sem("ld"), ld_t)
                        ctx.wait_all(t, "vS")
                        ctx.inc(t.transpose(self.PT[:, slot * 128:slot * 128 + 128],
                                            self.wj[:, jj * 128:(jj + 1) * 128], self.ident_b[:]), "tS")
                    blk.tensor(t3_)

                    def v6_(v, jj=jj, slot=slot):
                        ctx.wait_all(v, "tS")
                        ctx.wait_all(v, "st")
                        ctx.inc(v.tensor_scalar(self.blkstage[:], self.PT[:, slot * 128:slot * 128 + 128],
                                                self.subln_c[:], 0.5, OP.mult, OP.mult), "vS")
                    blk.vector(v6_)

                    def vfy_(v):
                        self.vf(v)
                    blk.vector(vfy_)

                    def s2_(s, jj=jj, ct=ct):
                        ctx.wait_all(s, "vS")
                        ctx.inc(s.dma_start(scro[jj, ct], self.blkstage[:]), "st", 16)
                    blk.sync(s2_)


    def phase_a(self, b):
        nc, ctx, io = self.nc, self.ctx, self.io
        with nc.Block(f"pa{b}") as blk:
            tpose_done = []
            for t in range(NT):
                xslot = t % 2
                wait_t = tpose_done[t - 2] if t >= 2 else 0
                def g_(g, b=b, t=t, xslot=xslot, wait_t=wait_t):
                    if wait_t:
                        g.wait_ge(ctx.sem("tS"), wait_t)
                    else:
                        ctx.wait_all(g, "st")
                    ctx.inc(g.dma_start(self.xstage[:, xslot * C:(xslot + 1) * C],
                                        io["x_in"][b, t * 128:(t + 1) * 128, :]), "ld", 16)
                blk.gpsimd(g_)
                ld_t = ctx.cnt["ld"]
                for cc in range(NC):
                    slot = (t * NC + cc) % 2
                    def t_(tt, t=t, cc=cc, slot=slot, xslot=xslot, ld_t=ld_t):
                        tt.wait_ge(ctx.sem("ld"), ld_t)
                        ctx.wait_all(tt, "vS")
                        ctx.inc(tt.transpose(self.PT[:, slot * 128:slot * 128 + 128],
                                             self.xstage[:, xslot * C + cc * 128: xslot * C + (cc + 1) * 128],
                                             self.ident_b[:]), "tS")
                    blk.tensor(t_)

                    def v_(v, t=t, cc=cc, slot=slot):
                        ctx.wait_all(v, "tS")
                        ctx.inc(v.tensor_copy(self.xT[:, cc * N + t * 128: cc * N + (t + 1) * 128],
                                              self.PT[:, slot * 128:slot * 128 + 128]), "vS")
                    blk.vector(v_)
                tpose_done.append(ctx.cnt["tS"])

    def phase_b(self, b):
        nc, ctx, io = self.nc, self.ctx, self.io
        with nc.Block(f"pb{b}") as blk:
            for h in range(H):
                for side in range(2):
                    def gw_(g, side=side, h=h):
                        ctx.wait_all(g, "vS")
                        ctx.wait_all(g, "tS")
                        ctx.inc(g.dma_start(self.wsl[:], io["wT_scr"][side, h]), "ld", 16)
                    blk.gpsimd(gw_)
                    ld_t = ctx.cnt["ld"]
                    for ch in range(4):
                        pq = self.P[ch % 2]
                        def t_(t, h=h, ch=ch, pq=pq, ld_t=ld_t):
                            t.wait_ge(ctx.sem("ld"), ld_t)
                            ctx.wait_all(t, "vS")
                            for cc in range(NC):
                                ctx.inc(t.matmul(pq[:], self.wsl[:, cc * 128:(cc + 1) * 128],
                                                 self.xT[:, cc * N + ch * 512: cc * N + (ch + 1) * 512],
                                                 start=(cc == 0), stop=(cc == NC - 1)), "tS")
                        blk.tensor(t_)
                        tq = ctx.cnt["tS"]

                        def a_(a, pq=pq, tq=tq):
                            a.wait_ge(ctx.sem("tS"), tq)
                            ctx.inc(a.activation(self.sq_sb[:], pq[:], AF.Square), "aS")
                        blk.scalar(a_)
                        asq = ctx.cnt["aS"]

                        def t2_(t, asq=asq):
                            t.wait_ge(ctx.sem("aS"), asq)
                            ctx.inc(t.matmul(self.P[2][0:2, 0:512], self.onespair_b[:], self.sq_sb[:],
                                             start=True, stop=True), "tS")
                        blk.tensor(t2_)
                        tss = ctx.cnt["tS"]

                        def v_(v, tss=tss):
                            v.wait_ge(ctx.sem("tS"), tss)
                            ctx.inc(v.tensor_scalar(self.ssf[:], self.P[2][0:2, 0:512], 1e-12, None, OP.max), "vS")
                        blk.vector(v_)
                        vss = ctx.cnt["vS"]

                        def a2_(a, vss=vss):
                            a.wait_ge(ctx.sem("vS"), vss)
                            a.activation(self.ssf[:], self.ssf[:], AF.Ln)
                            a.activation(self.rQ[0:2, :], self.ssf[:], AF.Exp, scale=-0.5)
                            self.af(a)
                        blk.scalar(a2_)
                        ar = ctx.cnt["aS"]

                        def t3_(t, ar=ar):
                            t.wait_ge(ctx.sem("aS"), ar)
                            ctx.inc(t.matmul(self.P[3][:, 0:512], self.opT_f[:], self.rQ[:],
                                             start=True, stop=True), "tS")
                        blk.tensor(t3_)
                        trb = ctx.cnt["tS"]

                        def v2_(v, pq=pq, side=side, h=h, ch=ch, trb=trb):
                            v.wait_ge(ctx.sem("tS"), trb)
                            v.tensor_copy(self.rb_sb[:], self.P[3][:, 0:512])
                            dst = (self.qcat if side == 0 else self.kcat)
                            sl = dst[:, h * N + ch * 512: h * N + (ch + 1) * 512]
                            if side == 0:
                                ctx.inc(v.scalar_tensor_tensor(sl, pq[:], self.cs_bc[:, h:h + 1], self.rb_sb[:],
                                                               OP.mult, OP.mult), "vS")
                            else:
                                ctx.inc(v.tensor_tensor(sl, pq[:], self.rb_sb[:], OP.mult), "vS")
                        blk.vector(v2_)

                def gv_(g, h=h):
                    ctx.wait_all(g, "vS")
                    ctx.wait_all(g, "tS")
                    ctx.inc(g.dma_start(self.wsl[:], io["wT_scr"][2, h]), "ld", 16)
                blk.gpsimd(gv_)
                ld_t = ctx.cnt["ld"]
                for tt_i in range(NT):
                    pv = self.P[4 + (tt_i % 2)]
                    def t4_(t, h=h, tt_i=tt_i, pv=pv, ld_t=ld_t):
                        t.wait_ge(ctx.sem("ld"), ld_t)
                        ctx.wait_all(t, "aS")
                        for cc in range(NC):
                            ctx.inc(t.matmul(pv[:, 0:128],
                                             self.xT[:, cc * N + tt_i * 128: cc * N + (tt_i + 1) * 128],
                                             self.wsl[:, cc * 128:(cc + 1) * 128],
                                             start=(cc == 0), stop=(cc == NC - 1)), "tS")
                    blk.tensor(t4_)
                    tv = ctx.cnt["tS"]

                    def a3_(a, h=h, tt_i=tt_i, pv=pv, tv=tv):
                        a.wait_ge(ctx.sem("tS"), tv)
                        base = (tt_i * H + h) * 130
                        src = pv[:, 0:128].rearrange("p (g d) -> p g d", g=2)
                        dst = self.vsl[:, base:base + 130].rearrange("p (g d) -> p g d", d=65)[:, :, 0:64]
                        ctx.inc(a.activation(dst, src, AF.Identity), "aS")
                    blk.scalar(a3_)

    def phase_c(self, b):
        nc, ctx = self.nc, self.ctx
        with nc.Block(f"pc{b}") as blk:
            def v_ones(v):
                ctx.wait_all(v, "aS")
                view = self.vsl[:].rearrange("p (s d) -> p s d", d=130)
                v.memset(view[:, :, 64:65], 1.0)
                ctx.inc(v.memset(view[:, :, 129:130], 1.0), "vS")
            blk.vector(v_ones)

            for h in range(H):
                for qg in range(8):
                    qoff = qg * 256
                    def vqd(v, h=h, qoff=qoff):
                        ctx.wait_all(v, "aS")
                        v.tensor_scalar(self.qd_sb[:], self.qcat[:, h * N + qoff: h * N + qoff + 256],
                                        self.dgs_bc[:, h:h + 1], None, OP.mult)
                        self.vf(v)
                    blk.vector(vqd)
                    vqd_cnt = ctx.cnt["vS"]

                    ya1 = self.P[3][0:65, 0:256]; yb1 = self.P[4][0:65, 0:256]
                    ya2 = self.P[5][0:65, 0:256]; yb2 = self.P[6][0:65, 0:256]
                    v_done = []
                    av_done = []
                    for kt in range(NT):
                        par = kt % 2
                        ps1 = self.P[0][:, par * 256:(par + 1) * 256]
                        ps2 = self.P[1][:, par * 256:(par + 1) * 256]
                        psd = self.P[2][:, par * 256:(par + 1) * 256]
                        koff = h * N + kt * 128
                        wait_v = vqd_cnt if kt < 2 else v_done[kt - 2]

                        def t_(t, h=h, qoff=qoff, koff=koff, ps1=ps1, ps2=ps2, psd=psd, wait_v=wait_v):
                            t.wait_ge(ctx.sem("vS"), wait_v)
                            ctx.inc(t.matmul(ps1, self.kcat[0:64, koff:koff + 128],
                                             self.qcat[0:64, h * N + qoff: h * N + qoff + 256],
                                             start=True, stop=True), "tS")
                            ctx.inc(t.matmul(ps2, self.kcat[64:128, koff:koff + 128],
                                             self.qcat[64:128, h * N + qoff: h * N + qoff + 256],
                                             start=True, stop=True), "tS")
                            ctx.inc(t.matmul(psd, self.kcat[:, koff:koff + 128], self.qd_sb[:],
                                             start=True, stop=True), "tS")
                        blk.tensor(t_)
                        tS_sd = ctx.cnt["tS"]

                        def a_(a, psd=psd, tS_sd=tS_sd):
                            a.wait_ge(ctx.sem("tS"), tS_sd)
                            ctx.inc(a.activation(self.g_sb[:], psd, AF.Tanh), "aS")
                        blk.scalar(a_)
                        aS_g = ctx.cnt["aS"]

                        def v_(v, ps1=ps1, ps2=ps2, aS_g=aS_g, h=h, qg=qg, kt=kt, b=b):
                            v.wait_ge(ctx.sem("aS"), aS_g)
                            ctx.inc(v.tensor_tensor(self.s1_sb[:], ps1, self.g_sb[:], OP.add), "vS")
                            ctx.inc(v.tensor_tensor(self.s2_sb[:], ps2, self.g_sb[:], OP.subtract), "vS")
                            if b == 0 and h == 0 and qg == 0 and kt == 0 and "dbg_ps1" in self.io:
                                v.tensor_copy(self.tmpa[:], ps1)
                                v.tensor_copy(self.tmpb[:], ps2)
                                self.vf(v); ctx.inc(v.tensor_copy(self.fencea[:], self.fenceb[:]), "vS")
                        blk.vector(v_)
                        v_done.append(ctx.cnt["vS"])
                        vS_s = ctx.cnt["vS"]
                        wait_t_av = av_done[kt - 1] if kt >= 1 else 0

                        def a2_(a, vS_s=vS_s, wait_t_av=wait_t_av):
                            a.wait_ge(ctx.sem("vS"), vS_s)
                            if wait_t_av:
                                a.wait_ge(ctx.sem("tS"), wait_t_av)
                            ctx.inc(a.activation(self.e1_sb[:], self.s1_sb[:], AF.Exp), "aS")
                            ctx.inc(a.activation(self.e2sb[:], self.s2_sb[:], AF.Exp), "aS")
                        blk.scalar(a2_)
                        aS_e = ctx.cnt["aS"]

                        def gdmp_(g, h=h, qg=qg, kt=kt, b=b, aS_e=ctx.cnt["aS"], vS_d=ctx.cnt["vS"]):
                            if b == 0 and h == 0 and qg == 0 and kt == 0 and "dbg_ps1" in self.io:
                                g.wait_ge(ctx.sem("aS"), aS_e)
                                g.wait_ge(ctx.sem("vS"), vS_d)
                                for nm, t_ in [("dbg_ps1", self.tmpa), ("dbg_ps2", self.tmpb),
                                               ("dbg_gd", self.g_sb), ("dbg_s1d", self.s1_sb),
                                               ("dbg_e1d", self.e1_sb), ("dbg_e2d", self.e2sb)]:
                                    ctx.inc(g.dma_start(self.io[nm], t_[:]), "st", 16)
                        blk.gpsimd(gdmp_)

                        def t2_(t, h=h, kt=kt, ya1=ya1, yb1=yb1, ya2=ya2, yb2=yb2, aS_e=aS_e):
                            t.wait_ge(ctx.sem("aS"), aS_e)
                            base = (kt * H + h) * 130
                            st_ = (kt == 0); sp = (kt == NT - 1)
                            ctx.inc(t.matmul(ya1, self.vsl[:, base:base + 65], self.e1_sb[:], start=st_, stop=sp), "tS")
                            ctx.inc(t.matmul(yb1, self.vsl[:, base + 65:base + 130], self.e1_sb[:], start=st_, stop=sp), "tS")
                            ctx.inc(t.matmul(ya2, self.vsl[:, base:base + 65], self.e2sb[:], start=st_, stop=sp), "tS")
                            ctx.inc(t.matmul(yb2, self.vsl[:, base + 65:base + 130], self.e2sb[:], start=st_, stop=sp), "tS")
                        blk.tensor(t2_)
                        av_done.append(ctx.cnt["tS"])

                    tS_av = ctx.cnt["tS"]

                    def ve_(v, ya1=ya1, ya2=ya2, tS_av=tS_av):
                        v.wait_ge(ctx.sem("tS"), tS_av)
                        v.reciprocal(self.r1st[0:1, :], ya1[64:65, :])
                        v.reciprocal(self.r2st[0:1, :], ya2[64:65, :])
                        self.vf(v)
                        self.vf(v)
                        v.tensor_scalar(self.r2st[0:1, :], self.r2st[0:1, :],
                                        self.lam_neg[0:1, 0:1], None, OP.mult)
                        self.vf(v)
                        ctx.inc(v.tensor_copy(self.fencea[:], self.fenceb[:]), "vS")
                    blk.vector(ve_)
                    vS_r = ctx.cnt["vS"]

                    def te_(t, vS_r=vS_r):
                        t.wait_ge(ctx.sem("vS"), vS_r)
                        ctx.inc(t.matmul(self.P[2][:, 0:256], self.e2_f[:], self.r1st[:], start=True, stop=True), "tS")
                        ctx.inc(t.matmul(self.P[2][:, 256:512], self.e2_f[:], self.r2st[:], start=True, stop=True), "tS")
                    blk.tensor(te_)
                    tS_b = ctx.cnt["tS"]

                    def vstash_(v, b=b, h=h, qg=qg, ya1=ya1, yb1=yb1, ya2=ya2, yb2=yb2, tS_b=ctx.cnt["tS"]):
                        if b == 0 and h == 0 and qg == 0 and "dbg_ya1" in self.io:
                            v.wait_ge(ctx.sem("tS"), tS_b)
                            v.tensor_copy(self.tsb[0:65, 0:256], ya1)
                            v.tensor_copy(self.tsb[0:65, 256:512], yb1)
                            v.tensor_copy(self.tsb[0:65, 512:768], ya2)
                            v.tensor_copy(self.tsb[0:65, 768:1024], yb2)
                            v.tensor_copy(self.rQ[0:1, 0:512], self.P[3][0:1, 0:512])
                            self.vf(v)
                            ctx.inc(v.tensor_copy(self.fencea[:], self.fenceb[:]), "vS")
                    blk.vector(vstash_)

                    def gdmp2_(g, b=b, h=h, qg=qg, vS_b=ctx.cnt["vS"]):
                        if b == 0 and h == 0 and qg == 0 and "dbg_ya1" in self.io:
                            g.wait_ge(ctx.sem("vS"), vS_b)
                            ctx.inc(g.dma_start(self.io["dbg_ya1"], self.tsb[0:65, 0:256]), "st", 16)
                            ctx.inc(g.dma_start(self.io["dbg_yb1"], self.tsb[0:65, 256:512]), "st", 16)
                            ctx.inc(g.dma_start(self.io["dbg_ya2"], self.tsb[0:65, 512:768]), "st", 16)
                            ctx.inc(g.dma_start(self.io["dbg_yb2"], self.tsb[0:65, 768:1024]), "st", 16)
                            ctx.inc(g.dma_start(self.io["dbg_dens"], self.rQ[0:1, 0:512]), "st", 16)
                    blk.gpsimd(gdmp2_)

                    def ve2_(v, h=h, qoff=qoff, ya1=ya1, yb1=yb1, ya2=ya2, yb2=yb2, tS_b=tS_b):
                        v.wait_ge(ctx.sem("tS"), tS_b)
                        v.tensor_copy(self.rb1[:], self.P[2][:, 0:256])
                        v.tensor_copy(self.rb2[:], self.P[2][:, 256:512])
                        if False:
                            pass
                        v.tensor_tensor(self.tmpa[0:64, :], ya1[0:64, :], self.rb1[0:64, :], OP.mult)
                        v.tensor_tensor(self.tmpb[0:64, :], ya2[0:64, :], self.rb2[0:64, :], OP.mult)
                        ctx.inc(v.tensor_tensor(self.ybuf[0:64, h * N + qoff:h * N + qoff + 256],
                                                self.tmpa[0:64, :], self.tmpb[0:64, :], OP.add), "vS")
                        v.tensor_tensor(self.tmpa[64:128, :], yb1[0:64, :], self.rb1[64:128, :], OP.mult)
                        v.tensor_tensor(self.tmpb[64:128, :], yb2[0:64, :], self.rb2[64:128, :], OP.mult)
                        ctx.inc(v.tensor_tensor(self.ybuf[64:128, h * N + qoff:h * N + qoff + 256],
                                                self.tmpa[64:128, :], self.tmpb[64:128, :], OP.add), "vS")
                    blk.vector(ve2_)

                    def gdmp3_(g, b=b, h=h, qg=qg, vS_c=ctx.cnt["vS"]):
                        if b == 0 and h == 0 and qg == 0 and "dbg_rb1d" in self.io:
                            g.wait_ge(ctx.sem("vS"), vS_c)
                            ctx.inc(g.dma_start(self.io["dbg_rb1d"], self.rb1[:]), "st", 16)
                            ctx.inc(g.dma_start(self.io["dbg_rb2d"], self.rb2[:]), "st", 16)
                    blk.gpsimd(gdmp3_)

    def phase_d(self, b):
        nc, ctx, io = self.nc, self.ctx, self.io
        scro = io["woT_scr"]
        with nc.Block(f"pd{b}") as blk:
            def g3_(g):
                ctx.wait_all(g, "tS")
                ctx.wait_all(g, "vS")
                for h in range(H):
                    ctx.inc(g.dma_start(self.wos_all[:, h * C:(h + 1) * C].rearrange("p (ct c) -> p ct c", c=128),
                                        scro[h].rearrange("ct p c -> p ct c")), "ld", 16)
            blk.gpsimd(g3_)
            wos_ld = ctx.cnt["ld"]
            for h in range(H):
                for ch in range(4):
                    off = h * N + ch * 512
                    def a_(a, off=off):
                        ctx.wait_all(a, "vS")
                        ctx.wait_all(a, "tS")
                        ctx.inc(a.activation(self.sqy[:], self.ybuf[:, off:off + 512], AF.Square), "aS")
                    blk.scalar(a_)
                    aS_sq = ctx.cnt["aS"]

                    def t_(t, aS_sq=aS_sq):
                        t.wait_ge(ctx.sem("aS"), aS_sq)
                        ctx.inc(t.matmul(self.P[0][0:1, 0:512], self.w128_b[:], self.sqy[:], start=True, stop=True), "tS")
                    blk.tensor(t_)
                    tS_ms = ctx.cnt["tS"]

                    def a2_(a, tS_ms=tS_ms):
                        a.wait_ge(ctx.sem("tS"), tS_ms)
                        a.activation(self.rstd_r[:], self.P[0][0:1, 0:512], AF.Ln, bias=self.eps_col[0:1, :])
                        a.activation(self.rstd2[0:1, :], self.rstd_r[:], AF.Exp, scale=-0.5)
                        self.af(a)
                    blk.scalar(a2_)
                    aS_r = ctx.cnt["aS"]

                    def t2_(t, aS_r=aS_r):
                        t.wait_ge(ctx.sem("aS"), aS_r)
                        ctx.inc(t.matmul(self.P[1][:, 0:512], self.e2_f[:], self.rstd2[:], start=True, stop=True), "tS")
                    blk.tensor(t2_)
                    tS_rb = ctx.cnt["tS"]

                    def v_(v, off=off, tS_rb=tS_rb):
                        v.wait_ge(ctx.sem("tS"), tS_rb)
                        v.tensor_copy(self.rb_sb[:], self.P[1][:, 0:512])
                        ctx.inc(v.tensor_tensor(self.qcat[:, off:off + 512], self.ybuf[:, off:off + 512],
                                                self.rb_sb[:], OP.mult), "vS")
                    blk.vector(v_)

            res_used = []
            for tt_i in range(NT):
                po = None
                xslot = tt_i % 2
                wait_res = res_used[tt_i - 2] if tt_i >= 2 else 0
                def gx_(g, b=b, tt_i=tt_i, xslot=xslot, wait_res=wait_res):
                    if wait_res:
                        g.wait_ge(ctx.sem("vS"), wait_res)
                    ctx.inc(g.dma_start(self.xstage[:, xslot * C:(xslot + 1) * C],
                                        io["x_in"][b, tt_i * 128:(tt_i + 1) * 128, :]), "ld", 16)
                blk.gpsimd(gx_)
                ld_res = ctx.cnt["ld"]
                vS_pre = ctx.cnt["vS"]
                aS_pre = ctx.cnt["aS"]

                def t3_(t, tt_i=tt_i, vS_pre=vS_pre, aS_pre=aS_pre, wos_ld=wos_ld):
                    t.wait_ge(ctx.sem("ld"), wos_ld)
                    t.wait_ge(ctx.sem("vS"), vS_pre)
                    t.wait_ge(ctx.sem("aS"), aS_pre)
                    for cc2 in range(2):
                        po = self.P[2 + cc2]
                        for h in range(H):
                            ctx.inc(t.matmul(po[:],
                                             self.qcat[:, h * N + tt_i * 128: h * N + (tt_i + 1) * 128],
                                             self.wos_all[:, h * C + cc2 * 512: h * C + (cc2 + 1) * 512],
                                             start=(h == 0), stop=False), "tS")
                        ctx.inc(t.matmul(po[:], self.e2_b[:],
                                         self.bo2[:, cc2 * 512:(cc2 + 1) * 512],
                                         start=False, stop=True), "tS")
                blk.tensor(t3_)
                tS_o = ctx.cnt["tS"]

                def v2_(v, tt_i=tt_i, tS_o=tS_o, xslot=xslot, ld_res=ld_res):
                    v.wait_ge(ctx.sem("tS"), tS_o)
                    v.wait_ge(ctx.sem("ld"), ld_res)
                    v.tensor_tensor(self.tsb[:, 0:512], self.P[2][:],
                                    self.xstage[:, xslot * C: xslot * C + 512], OP.add)
                    v.tensor_tensor(self.tsb[:, 512:1024], self.P[3][:],
                                    self.xstage[:, xslot * C + 512: (xslot + 1) * C], OP.add)
                    v.bn_stats(self.bn6[:, 0:6], self.tsb[:, 0:512])
                    v.bn_stats(self.bn6[:, 6:12], self.tsb[:, 512:1024])
                    self.vf(v)
                    self.vf(v)
                    v.bn_aggr(self.mv2[:], self.bn6[:])
                    self.vf(v)
                    self.vf(v)
                    v.tensor_scalar(self.lnm[:], self.mv2[:, 0:1], -1.0, None, OP.mult)
                    self.vf(v)
                    ctx.inc(v.tensor_copy(self.fencea[:], self.fenceb[:]), "vS")
                blk.vector(v2_)
                vS_ln = ctx.cnt["vS"]

                def sdump_(s, b=b, tt_i=tt_i, vS_ln=vS_ln):
                    if "dbg_preln" in self.io and b == 0:
                        s.wait_ge(ctx.sem("vS"), vS_ln)
                        ctx.inc(s.dma_start(self.io["dbg_preln"][tt_i * 128:(tt_i + 1) * 128, :], self.tsb[:]), "st", 16)
                blk.sync(sdump_)

                def a3_(a, vS_ln=vS_ln):
                    a.wait_ge(ctx.sem("vS"), vS_ln)
                    a.activation(self.lnr[:], self.mv2[:, 1:2], AF.Ln, bias=self.eps_col[:])
                    self.af(a)
                    a.activation(self.lnr[:], self.lnr[:], AF.Exp, scale=-0.5)
                    self.af(a)
                    self.af(a)
                blk.scalar(a3_)
                aS_ln = ctx.cnt["aS"]

                def v3_(v, aS_ln=aS_ln):
                    v.wait_ge(ctx.sem("aS"), aS_ln)
                    v.tensor_tensor(self.lnm[:], self.lnm[:], self.lnr[:], OP.mult)
                    self.vf(v)
                    ctx.inc(v.tensor_copy(self.fencea[:], self.fenceb[:]), "vS")
                blk.vector(v3_)
                vS_nm = ctx.cnt["vS"]

                def a4_(a, vS_nm=vS_nm):
                    a.wait_ge(ctx.sem("vS"), vS_nm)
                    ctx.inc(a.activation(self.xout_sb[:], self.tsb[:], AF.Identity,
                                         bias=self.lnm[:], scale=self.lnr[:]), "aS")
                blk.scalar(a4_)
                aS_x = ctx.cnt["aS"]

                def s_(s, b=b, tt_i=tt_i, aS_x=aS_x):
                    s.wait_ge(ctx.sem("aS"), aS_x)
                    ctx.inc(s.dma_start(io["x_out"][b, tt_i * 128:(tt_i + 1) * 128, :], self.xout_sb[:]), "st", 16)
                blk.sync(s_)
                res_used.append(ctx.cnt["vS"])

    def dump(self, name, sb):
        nc, ctx, io = self.nc, self.ctx, self.io
        with nc.Block(f"dump_{name}") as blk:
            def s_(g):
                ctx.wait_all(g, "vS")
                ctx.wait_all(g, "aS")
                ctx.wait_all(g, "tS")
                ctx.inc(g.dma_start(io[name], sb[:]), "st", 16)
            blk.gpsimd(s_)

    def finish(self):
        nc, ctx = self.nc, self.ctx
        with nc.Block("fin") as blk:
            def g_(g):
                ctx.wait_all(g, "st")
            blk.gpsimd(g_)


def build_l0_launch(dbg=()):
    nc = bass.Bass("TRN2", target_bir_lowering=False)
    io = {}
    io["x_in"] = nc.dram_tensor("x_in", [2, N, C], F32, kind="ExternalInput").ap()
    io["x_out"] = nc.dram_tensor("x_out", [2, N, C], F32, kind="ExternalOutput").ap()
    for w in ["Wq", "Wk", "Wv", "Wo"]:
        io[w] = nc.dram_tensor(w, [C, C], F32, kind="ExternalInput").ap()
    io["bo"] = nc.dram_tensor("bo", [1, C], F32, kind="ExternalInput").ap()
    for nm in ["lambda_q1", "lambda_k1", "lambda_q2", "lambda_k2"]:
        io[nm] = nc.dram_tensor(nm, [1, 64], F32, kind="ExternalInput").ap()
    io["delta_gain"] = nc.dram_tensor("delta_gain", [1, 8], F32, kind="ExternalInput").ap()
    io["cos_head_delta"] = nc.dram_tensor("cos_head_delta", [1, 8], F32, kind="ExternalInput").ap()
    io["cos_raw"] = nc.dram_tensor("cos_raw", [1, 1], F32, kind="ExternalInput").ap()
    io["subln"] = nc.dram_tensor("subln", [128, 1], F32, kind="ExternalInput").ap()
    for k, v in host_consts().items():
        io[k] = nc.dram_tensor(k, list(v.shape), F32, kind="ExternalInput").ap()
    io["wT_scr"] = nc.dram_tensor("wT_scr", [3, 8, 128, 1024], BF16, kind="Internal").ap()
    io["woT_scr"] = nc.dram_tensor("woT_scr", [8, 8, 128, 128], BF16, kind="Internal").ap()
    for d in dbg:
        shape = {"qcat": [128, H * N], "kcat": [128, H * N], "ybuf": [128, H * N],
                 "xTd": [128, NC * N], "vsld": [128, NT * H * 130],
                 "csd": [128, 8], "dgsd": [128, 8], "misc": [1, 32], "ynT": [128, H * N],
                 "preln": [N, C], "wosd": [128, H*C],
                 "ps1": [128, 256], "ps2": [128, 256], "gd": [128, 256],
                 "s1d": [128, 256], "e1d": [128, 256], "e2d": [128, 256],
                 "ya1": [65, 256], "yb1": [65, 256], "ya2": [65, 256], "yb2": [65, 256],
                 "dens": [1, 512], "rb1d": [128, 256], "rb2d": [128, 256]}[d]
        io["dbg_" + d] = nc.dram_tensor("dbg_" + d, shape, F32, kind="ExternalOutput").ap()

    ctx = Ctx(nc)
    b_ = L0Attn(nc, ctx, io, dbg)
    b_.p0_consts()
    if "csd" in dbg:
        b_.dump("dbg_csd", b_.cs_bc)
        b_.dump("dbg_dgsd", b_.dgs_bc)
    if "misc" in dbg:
        with nc.Block("dmisc") as blk:
            def vm_(v):
                ctx.wait_all(v, "aS"); ctx.wait_all(v, "vS"); ctx.wait_all(v, "tS")
                v.tensor_copy(b_.tsb[0:1, 0:8], b_.scal2[:])
                v.tensor_copy(b_.tsb[0:1, 8:16], b_.csrow[:])
                v.tensor_copy(b_.tsb[0:1, 16:24], b_.stage28[0:1, 0:8])
                v.tensor_copy(b_.tsb[0:1, 24:25], b_.craw[:])
                v.tensor_copy(b_.tsb[0:1, 25:26], b_.scal1[0:1, 0:1])
                ctx.inc(v.tensor_copy(b_.tsb[0:1, 26:27], b_.lam_neg[:]), "vS")
            blk.vector(vm_)
        with nc.Block("dmisc2") as blk:
            def gm_(g):
                ctx.wait_all(g, "vS")
                ctx.inc(g.dma_start(io["dbg_misc"], b_.tsb[0:1, 0:32]), "st", 16)
            blk.gpsimd(gm_)
    b_.p0_weights()
    for b in range(2):
        b_.phase_a(b)
        if b == 0:
            if "xTd" in dbg: b_.dump("dbg_xTd", b_.xT)
        b_.phase_b(b)
        if b == 0:
            if "qcat" in dbg: b_.dump("dbg_qcat", b_.qcat)
            if "kcat" in dbg: b_.dump("dbg_kcat", b_.kcat)
            if "vsld" in dbg: b_.dump("dbg_vsld", b_.vsl)
        b_.phase_c(b)
        if b == 0 and "ybuf" in dbg: b_.dump("dbg_ybuf", b_.ybuf)
        b_.phase_d(b)
        if b == 0 and "ynT" in dbg: b_.dump("dbg_ynT", b_.qcat)
        if b == 0 and "wosd" in dbg: b_.dump("dbg_wosd", b_.wos_all[:, 0:H*C])
    b_.finish()
    nc.finalize()
    return nc


def l0_in_map(xb, w, consts):
    m = {"x_in": xb,
         "Wq": w["L0_attn_Wq"], "Wk": w["L0_attn_Wk"], "Wv": w["L0_attn_Wv"], "Wo": w["L0_attn_Wo"],
         "bo": w["L0_attn_bo"].reshape(1, -1),
         "lambda_q1": w["L0_attn_lambda_q1"].reshape(1, -1), "lambda_k1": w["L0_attn_lambda_k1"].reshape(1, -1),
         "lambda_q2": w["L0_attn_lambda_q2"].reshape(1, -1), "lambda_k2": w["L0_attn_lambda_k2"].reshape(1, -1),
         "delta_gain": w["L0_attn_delta_gain"].reshape(1, -1),
         "cos_head_delta": w["L0_attn_cos_head_delta"].reshape(1, -1),
         "cos_raw": np.asarray(w["L0_attn_cos_logit_scale_raw"]).reshape(1, 1),
         "subln": w["L0_attn_subln_weight"].reshape(-1, 1)}
    m.update(consts)
    return m


class L0FFN:

    def __init__(self, nc, ctx, io):
        self.nc, self.ctx, self.io = nc, ctx, io
        A = nc.alloc_sbuf_tensor
        TC = 1024
        self.TC = TC
        self.xbf = A("f_xbf", [128, 8 * 1024], BF16)
        self.xT = A("f_xT", [128, 8 * TC], BF16)
        self.hT = A("f_hT", [128, 22 * TC], BF16)
        self.acc = A("f_acc", [128, 8 * 1024], F32)
        self.wgu_sb = A("f_wgu", [128, 2 * 8 * 128], BF16)
        self.wd_full = A("f_wd", [128, 22 * 1024], BF16)
        self.gw_sb = A("f_gw", [128, 8 * 8], BF16)
        self.sg_sb = A("f_sg", [128, 1024], BF16)
        self.lgall = A("f_lg", [128, 64], F32)
        self.cw = A("f_cw", [128, 64], F32)
        self.m1 = A("f_m1", [128, 8], F32)
        self.exin = A("f_exin", [128, 64], F32)
        self.ex = A("f_ex", [128, 64], F32)
        self.ismax = A("f_ismax", [128, 64], F32)
        self.notmax = A("f_nmx", [128, 64], F32)
        self.em2 = A("f_em2", [128, 64], F32)
        self.m2 = A("f_m2", [128, 8], F32)
        self.den8 = A("f_den", [128, 8], F32)
        self.rr8 = A("f_rr", [128, 8], F32)
        self.w28 = A("f_w2c", [128, 8], F32)
        self.ismax2 = A("f_ismax2", [128, 64], F32)
        self.c1 = A("f_c1", [128, 64], F32)
        self.tsb = A("f_tsb", [128, 1024], F32)
        self.bn6 = A("f_bn6", [128, 12], F32)
        self.mv2 = A("f_mv2", [128, 2], F32)
        self.lnr = A("f_lnr", [128, 1], F32)
        self.lnm = A("f_lnm", [128, 1], F32)
        self.xout_sb = A("f_xout", [128, 1024], F32)
        self.eps_col = A("f_eps", [128, 1], F32)
        self.ident_b = A("f_ident", [128, 128], BF16)
        self.fencea = A("f_fa", [128, 8], F32)
        self.fenceb = A("f_fb", [128, 8], F32)
        self.bigfa = A("f_bfa", [128, 256], BF16)
        self.bigfb = A("f_bfb", [128, 256], BF16)
        self.P = [nc.alloc_psum_tensor(f"f_pool{i}", [128, 512], F32) for i in range(6)]
        self.PT = nc.alloc_psum_tensor("f_poolT", [128, 256], BF16)

    def vf(self, v):
        self.ctx.inc(v.tensor_copy(self.fenceb[:], self.fencea[:]), "vS")

    def vF(self, v):
        self.ctx.inc(v.tensor_copy(self.bigfb[:], self.bigfa[:]), "vS")

    def af(self, a):
        self.ctx.inc(a.activation(self.fenceb[:], self.fencea[:], AF.Identity), "aS")

    def prologue(self):
        nc, ctx, io = self.nc, self.ctx, self.io
        with nc.Block("f_pro") as blk:
            def g_(g):
                ctx.inc(g.dma_start(self.ident_b[:], io["c_ident"]), "ld", 16)
                ctx.inc(g.dma_start(self.gw_sb[:].rearrange("p (cc e) -> p cc e", e=8),
                                    io["gate_w"].rearrange("(cc p) e -> p cc e", p=128)), "ld", 16)
            blk.gpsimd(g_)

            def v_(v):
                ctx.wait_all(v, "ld")
                v.memset(self.fencea[:], 0.0)
                v.memset(self.bigfa[:], 0.0)
                ctx.inc(v.memset(self.eps_col[:], 1e-5), "vS")
            blk.vector(v_)

    def chunk(self, ci, dbg=False):
        nc, ctx, io = self.nc, self.ctx, self.io
        b = ci // 2
        r0 = (ci % 2) * 1024
        TC = self.TC

        with nc.Block(f"f_a{ci}") as blk:
            def g_(g):
                ctx.wait_all(g, "st")
                for t in range(8):
                    ctx.inc(g.dma_start(self.xbf[:, t * 1024:(t + 1) * 1024],
                                        io["x_in"][b, r0 + t * 128: r0 + (t + 1) * 128, :]), "ld", 16)
            blk.gpsimd(g_)
            ld_t = ctx.cnt["ld"]
            for t in range(8):
                for cc in range(8):
                    slot = (t * 8 + cc) % 2
                    def t_(tt, t=t, cc=cc, slot=slot, ld_t=ld_t):
                        tt.wait_ge(ctx.sem("ld"), ld_t)
                        ctx.wait_all(tt, "vS")
                        ctx.inc(tt.transpose(self.PT[:, slot * 128:slot * 128 + 128],
                                             self.xbf[:, t * 1024 + cc * 128: t * 1024 + (cc + 1) * 128],
                                             self.ident_b[:]), "tS")
                    blk.tensor(t_)

                    def v_(v, t=t, cc=cc, slot=slot):
                        ctx.wait_all(v, "tS")
                        ctx.inc(v.tensor_copy(self.xT[:, cc * TC + t * 128: cc * TC + (t + 1) * 128],
                                              self.PT[:, slot * 128:slot * 128 + 128]), "vS")
                    blk.vector(v_)

        with nc.Block(f"f_g{ci}") as blk:
            for t in range(8):
                def tg_(tt, t=t):
                    ctx.wait_all(tt, "vS")
                    for cc in range(8):
                        ctx.inc(tt.matmul(self.P[4][:, t * 8:(t + 1) * 8],
                                          self.xT[:, cc * TC + t * 128: cc * TC + (t + 1) * 128],
                                          self.gw_sb[:, cc * 8:(cc + 1) * 8],
                                          start=(cc == 0), stop=(cc == 7)), "tS")
                blk.tensor(tg_)
            tS_g = ctx.cnt["tS"]

            def e8(x):
                return x.rearrange("p (t e) -> p t e", e=8)

            def b8(x):
                return x.rearrange("p (t o) -> p t o", o=1).broadcast_to((128, 8, 8))

            def vg_(v, tS_g=tS_g):
                v.wait_ge(ctx.sem("tS"), tS_g)
                self.vF(v)
                v.tensor_copy(self.lgall[:], self.P[4][:, 0:64])
                v.reduce_max(self.m1[:].rearrange("p (t o) -> p t o", o=1), e8(self.lgall[:]), axis=AX)
                self.vF(v)
                self.vF(v)
                v.tensor_tensor(e8(self.exin[:]), e8(self.lgall[:]), b8(self.m1[:]), OP.subtract)
                v.tensor_tensor(e8(self.ismax[:]), e8(self.lgall[:]), b8(self.m1[:]), OP.is_equal)
                self.vF(v)
                ctx.inc(v.tensor_copy(self.bigfa[:], self.bigfb[:]), "vS")
            blk.vector(vg_)
            vS_g = ctx.cnt["vS"]

            def ag_(a, vS_g=vS_g):
                a.wait_ge(ctx.sem("vS"), vS_g)
                a.activation(self.ex[:], self.exin[:], AF.Exp)
                self.af(a)
                self.af(a)
            blk.scalar(ag_)
            aS_g = ctx.cnt["aS"]

            def vg2_(v, aS_g=aS_g):
                v.wait_ge(ctx.sem("aS"), aS_g)
                v.tensor_scalar(self.notmax[:], self.ismax[:], -1.0, 1.0, OP.mult, OP.add)
                self.vF(v)
                v.tensor_tensor(self.em2[:], self.ex[:], self.notmax[:], OP.mult)
                self.vF(v)
                v.reduce_max(self.m2[:].rearrange("p (t o) -> p t o", o=1), e8(self.em2[:]), axis=AX)
                self.vF(v)
                self.vF(v)
                v.tensor_scalar(self.den8[:], self.m2[:], 1.0, None, OP.add)
                self.vF(v)
                v.reciprocal(self.rr8[:], self.den8[:])
                self.vF(v)
                self.vF(v)
                v.tensor_tensor(e8(self.ismax2[:]), e8(self.em2[:]), b8(self.m2[:]), OP.is_equal)
                v.tensor_tensor(self.w28[:], self.m2[:], self.rr8[:], OP.mult)
                self.vF(v)
                v.tensor_tensor(e8(self.c1[:]), e8(self.ismax[:]), b8(self.rr8[:]), OP.mult)
                v.tensor_tensor(e8(self.ismax2[:]), e8(self.ismax2[:]), b8(self.w28[:]), OP.mult)
                self.vF(v)
                v.tensor_tensor(self.cw[:], self.c1[:], self.ismax2[:], OP.add)
                self.vF(v)
                ctx.inc(v.tensor_copy(self.bigfa[:], self.bigfb[:]), "vS")
            blk.vector(vg2_)

        if dbg and ci == 0:
            with nc.Block("f_dcw") as blk:
                def gd_(g):
                    ctx.wait_all(g, "vS")
                    ctx.inc(g.dma_start(io["dbg_cw"], self.cw[:]), "st", 16)
                blk.gpsimd(gd_)

        for m in range(9):
            wgu_d = io["base_wgu"] if m == 0 else io["exp_wgu"][m - 1]
            wd_d = io["base_wd"] if m == 0 else io["exp_wd"][m - 1]
            with nc.Block(f"f_e{ci}_{m}") as blk:
                for fp in range(22):
                    for half, fi in [(0, fp), (1, fp + 22)]:
                        def gw_(g, fi=fi, half=half, wgu_d=wgu_d):
                            ctx.wait_all(g, "tS")
                            ctx.inc(g.dma_start(
                                self.wgu_sb[:, half * 1024:(half + 1) * 1024].rearrange("p (cc f) -> p cc f", f=128),
                                wgu_d[:, fi * 128:(fi + 1) * 128].rearrange("(cc p) f -> p cc f", p=128)), "ld", 16)
                        blk.gpsimd(gw_)
                        ld_w = ctx.cnt["ld"]
                        for th in range(2):
                            pg = self.P[half]
                            def tm_(t, half=half, th=th, pg=pg, ld_w=ld_w):
                                t.wait_ge(ctx.sem("ld"), ld_w)
                                if half == 0:
                                    ctx.wait_all(t, "aS")
                                else:
                                    ctx.wait_all(t, "vS")
                                for cc in range(8):
                                    ctx.inc(t.matmul(pg[:], self.wgu_sb[:, half * 1024 + cc * 128: half * 1024 + (cc + 1) * 128],
                                                     self.xT[:, cc * TC + th * 512: cc * TC + (th + 1) * 512],
                                                     start=(cc == 0), stop=(cc == 7)), "tS")
                            blk.tensor(tm_)
                            tS_m = ctx.cnt["tS"]
                            if half == 0:
                                def as_(a, th=th, pg=pg, tS_m=tS_m):
                                    a.wait_ge(ctx.sem("tS"), tS_m)
                                    ctx.inc(a.activation(self.sg_sb[:, th * 512:(th + 1) * 512], pg[:], AF.Silu), "aS")
                                blk.scalar(as_)
                            else:
                                aS_m = ctx.cnt["aS"]
                                def vm_(v, fp=fp, th=th, pg=pg, tS_m=tS_m, aS_m=aS_m):
                                    v.wait_ge(ctx.sem("tS"), tS_m)
                                    v.wait_ge(ctx.sem("aS"), aS_m)
                                    ctx.inc(v.tensor_tensor(self.hT[:, fp * TC + th * 512: fp * TC + (th + 1) * 512],
                                                            pg[:], self.sg_sb[:, th * 512:(th + 1) * 512], OP.mult), "vS")
                                blk.vector(vm_)
                def gw2_(g, wd_d=wd_d):
                    ctx.wait_all(g, "tS")
                    for i0 in range(0, 22, 4):
                        i1 = min(i0 + 4, 22)
                        ctx.inc(g.dma_start(self.wd_full[:, i0 * 1024:i1 * 1024].rearrange("p (i c) -> p i c", c=1024),
                                            wd_d[i0 * 128:i1 * 128, :].rearrange("(i p) c -> p i c", p=128)), "ld", 16)
                blk.gpsimd(gw2_)
                ld_d = ctx.cnt["ld"]
                ev_done = []
                for tt_i in range(8):
                    for cc2 in range(2):
                        po = self.P[2 + (tt_i % 2) * 2 + cc2]
                        wait_ev = ev_done[(tt_i - 2) * 2 + cc2] if tt_i >= 2 else 0
                        def tw_(t, tt_i=tt_i, cc2=cc2, po=po, ld_d=ld_d, wait_ev=wait_ev):
                            t.wait_ge(ctx.sem("ld"), ld_d)
                            if wait_ev:
                                t.wait_ge(ctx.sem("vS"), wait_ev)
                            else:
                                ctx.wait_all(t, "vS")
                            for ic in range(22):
                                ctx.inc(t.matmul(po[:], self.hT[:, ic * TC + tt_i * 128: ic * TC + (tt_i + 1) * 128],
                                                 self.wd_full[:, ic * 1024 + cc2 * 512: ic * 1024 + (cc2 + 1) * 512],
                                                 start=(ic == 0), stop=(ic == 21)), "tS")
                        blk.tensor(tw_)
                        tS_wd = ctx.cnt["tS"]

                        def vev_(v, m=m, tt_i=tt_i, cc2=cc2, po=po, tS_wd=tS_wd):
                            v.wait_ge(ctx.sem("tS"), tS_wd)
                            dst = self.acc[:, tt_i * 1024 + cc2 * 512: tt_i * 1024 + (cc2 + 1) * 512]
                            if m == 0:
                                ctx.inc(v.tensor_copy(dst, po[:]), "vS")
                            else:
                                ctx.inc(v.scalar_tensor_tensor(dst, po[:], self.cw[:, tt_i * 8 + (m - 1):tt_i * 8 + m],
                                                               dst, OP.mult, OP.add), "vS")
                        blk.vector(vev_)
                        ev_done.append(ctx.cnt["vS"])

        if dbg and ci == 0:
            with nc.Block("f_dacc") as blk:
                def gd2_(g):
                    ctx.wait_all(g, "vS")
                    ctx.wait_all(g, "tS")
                    ctx.inc(g.dma_start(io["dbg_acc"], self.acc[:]), "st", 16)
                blk.gpsimd(gd2_)

        with nc.Block(f"f_l{ci}") as blk:
            for tt_i in range(8):
                vS_pre = ctx.cnt["vS"]
                def v2_(v, tt_i=tt_i, vS_pre=vS_pre):
                    v.wait_ge(ctx.sem("vS"), vS_pre)
                    ctx.wait_all(v, "aS")
                    v.tensor_tensor(self.tsb[:], self.acc[:, tt_i * 1024:(tt_i + 1) * 1024],
                                    self.xbf[:, tt_i * 1024:(tt_i + 1) * 1024], OP.add)
                    v.bn_stats(self.bn6[:, 0:6], self.tsb[:, 0:512])
                    v.bn_stats(self.bn6[:, 6:12], self.tsb[:, 512:1024])
                    self.vf(v)
                    self.vf(v)
                    v.bn_aggr(self.mv2[:], self.bn6[:])
                    self.vf(v)
                    self.vf(v)
                    v.tensor_scalar(self.lnm[:], self.mv2[:, 0:1], -1.0, None, OP.mult)
                    self.vf(v)
                    ctx.inc(v.tensor_copy(self.fencea[:], self.fenceb[:]), "vS")
                blk.vector(v2_)
                vS_ln = ctx.cnt["vS"]

                def a3_(a, vS_ln=vS_ln):
                    a.wait_ge(ctx.sem("vS"), vS_ln)
                    a.activation(self.lnr[:], self.mv2[:, 1:2], AF.Ln, bias=self.eps_col[:])
                    self.af(a)
                    a.activation(self.lnr[:], self.lnr[:], AF.Exp, scale=-0.5)
                    self.af(a)
                    self.af(a)
                blk.scalar(a3_)
                aS_ln = ctx.cnt["aS"]

                def v3_(v, aS_ln=aS_ln):
                    v.wait_ge(ctx.sem("aS"), aS_ln)
                    v.tensor_tensor(self.lnm[:], self.lnm[:], self.lnr[:], OP.mult)
                    self.vf(v)
                    ctx.inc(v.tensor_copy(self.fencea[:], self.fenceb[:]), "vS")
                blk.vector(v3_)
                vS_nm = ctx.cnt["vS"]

                def a4_(a, vS_nm=vS_nm):
                    a.wait_ge(ctx.sem("vS"), vS_nm)
                    ctx.inc(a.activation(self.xout_sb[:], self.tsb[:], AF.Identity,
                                         bias=self.lnm[:], scale=self.lnr[:]), "aS")
                blk.scalar(a4_)
                aS_x = ctx.cnt["aS"]

                def s_(s, b=b, r0=r0, tt_i=tt_i, aS_x=aS_x):
                    s.wait_ge(ctx.sem("aS"), aS_x)
                    ctx.inc(s.dma_start(io["x_out"][b, r0 + tt_i * 128: r0 + (tt_i + 1) * 128, :], self.xout_sb[:]), "st", 16)
                blk.sync(s_)

    def finish(self):
        nc, ctx = self.nc, self.ctx
        with nc.Block("f_fin") as blk:
            def g_(g):
                ctx.wait_all(g, "st")
            blk.gpsimd(g_)


def build_l0ffn_launch(dbg=False):
    nc = bass.Bass("TRN2", target_bir_lowering=False)
    io = {}
    io["x_in"] = nc.dram_tensor("x_in", [2, N, C], F32, kind="ExternalInput").ap()
    io["x_out"] = nc.dram_tensor("x_out", [2, N, C], F32, kind="ExternalOutput").ap()
    io["gate_w"] = nc.dram_tensor("gate_w", [C, 8], F32, kind="ExternalInput").ap()
    io["base_wgu"] = nc.dram_tensor("base_wgu", [C, 5632], F32, kind="ExternalInput").ap()
    io["base_wd"] = nc.dram_tensor("base_wd", [2816, C], F32, kind="ExternalInput").ap()
    io["exp_wgu"] = nc.dram_tensor("exp_wgu", [8, C, 5632], F32, kind="ExternalInput").ap()
    io["exp_wd"] = nc.dram_tensor("exp_wd", [8, 2816, C], F32, kind="ExternalInput").ap()
    io["c_ident"] = nc.dram_tensor("c_ident", [128, 128], F32, kind="ExternalInput").ap()
    if dbg:
        io["dbg_cw"] = nc.dram_tensor("dbg_cw", [128, 64], F32, kind="ExternalOutput").ap()
        io["dbg_acc"] = nc.dram_tensor("dbg_acc", [128, 8192], F32, kind="ExternalOutput").ap()
    ctx = Ctx(nc)
    f = L0FFN(nc, ctx, io)
    f.prologue()
    for ci in range(4):
        f.chunk(ci, dbg=dbg)
    f.finish()
    nc.finalize()
    return nc


class L1Attn:

    def __init__(self, nc, ctx, io):
        self.nc, self.ctx, self.io = nc, ctx, io
        A = nc.alloc_sbuf_tensor
        self.xbf = A("a_xbf", [128, 16 * 1024], BF16)
        self.xT = A("a_xT", [128, 8 * N], BF16)
        self.qT = A("a_qT", [128, 8 * N], BF16)
        self.kT = A("a_kT", [128, 2 * N], BF16)
        self.klat = A("a_klat", [32, 2 * N], BF16)
        self.vlat = A("a_vlat", [32, 2 * N], BF16)
        self.vsl = A("a_vsl", [128, 16 * 2 * 130], BF16)
        self.ynT = A("a_ynT", [128, 8 * N], BF16)
        self.wq_sb = A("a_wq", [128, 2 * 1024], BF16)
        self.wkv_sb = A("a_wkv", [128, 512], BF16)
        self.wup_sb = A("a_wup", [32, 256], BF16)
        self.wo_sb = A("a_wo", [128, 8 * 1024], BF16)
        self.masks = A("a_masks", [128, 4 * 512], BF16)
        self.e_sb = A("a_e", [128, 512], BF16)
        self.r1st = A("a_r1st", [2, 512], F32)
        self.rb_sb = A("a_rb", [128, 512], F32)
        self.e2_f = A("a_e2f", [2, 128], F32)
        self.ident_b = A("a_ident", [128, 128], BF16)
        self.tsb = A("a_tsb", [128, 1024], F32)
        self.bn6 = A("a_bn6", [128, 12], F32)
        self.mv2 = A("a_mv2", [128, 2], F32)
        self.lnr = A("a_lnr", [128, 1], F32)
        self.lnm = A("a_lnm", [128, 1], F32)
        self.xout_sb = A("a_xout", [128, 1024], F32)
        self.eps_col = A("a_eps", [128, 1], F32)
        self.fencea = A("a_fa", [128, 8], F32)
        self.fenceb = A("a_fb", [128, 8], F32)
        self.bigfa = A("a_bfa", [128, 256], BF16)
        self.bigfb = A("a_bfb", [128, 256], BF16)
        self.P = [nc.alloc_psum_tensor(f"a_pool{i}", [128, 512], F32) for i in range(6)]
        self.PT = nc.alloc_psum_tensor("a_poolT", [128, 256], BF16)

    def vf(self, v):
        self.ctx.inc(v.tensor_copy(self.fenceb[:], self.fencea[:]), "vS")

    def vF(self, v):
        self.ctx.inc(v.tensor_copy(self.bigfb[:], self.bigfa[:]), "vS")

    def af(self, a):
        self.ctx.inc(a.activation(self.fenceb[:], self.fencea[:], AF.Identity), "aS")

    def prologue(self):
        nc, ctx, io = self.nc, self.ctx, self.io
        with nc.Block("a_pro") as blk:
            def g_(g):
                ctx.inc(g.dma_start(self.ident_b[:], io["c_ident"]), "ld", 16)
                ctx.inc(g.dma_start(self.masks[:].rearrange("p (m q) -> p m q", q=512),
                                    io["c_masks"].rearrange("m p q -> p m q")), "ld", 16)
                ctx.inc(g.dma_start(self.wup_sb[:, 0:128], io["wk_up"]), "ld", 16)
                ctx.inc(g.dma_start(self.wup_sb[:, 128:256], io["wv_up"]), "ld", 16)
                for jt2 in range(0, 8, 2):
                    ctx.inc(g.dma_start(self.wo_sb[:, jt2 * 1024:(jt2 + 2) * 1024].rearrange("p (jt c) -> p jt c", c=1024),
                                        io["wo"][jt2 * 128:(jt2 + 2) * 128, :].rearrange("(jt p) c -> p jt c", p=128)), "ld", 16)
            blk.gpsimd(g_)

            def s_(s):
                ctx.inc(s.dma_start(self.e2_f[:], io["c_e2"]), "ld", 16)
            blk.sync(s_)

            def v_(v):
                ctx.wait_all(v, "ld")
                v.memset(self.fencea[:], 0.0)
                v.memset(self.bigfa[:], 0.0)
                v.memset(self.r1st[:], 0.0)
                view = self.vsl[:].rearrange("p (s d) -> p s d", d=130)
                v.memset(view[:, :, 64:65], 1.0)
                v.memset(view[:, :, 129:130], 1.0)
                ctx.inc(v.memset(self.eps_col[:], 1e-5), "vS")
            blk.vector(v_)

    def batch(self, b, upto="d"):
        nc, ctx, io = self.nc, self.ctx, self.io
        SC = 0.08838834764831845

        with nc.Block(f"a_a{b}") as blk:
            def g_(g):
                ctx.wait_all(g, "st")
                for t in range(NT):
                    ctx.inc(g.dma_start(self.xbf[:, t * 1024:(t + 1) * 1024],
                                        io["x_in"][b, t * 128:(t + 1) * 128, :]), "ld", 16)
            blk.gpsimd(g_)
            ld_t = ctx.cnt["ld"]
            for t in range(NT):
                for cc in range(8):
                    slot = (t * 8 + cc) % 2
                    def t_(tt, t=t, cc=cc, slot=slot, ld_t=ld_t):
                        tt.wait_ge(ctx.sem("ld"), ld_t)
                        ctx.wait_all(tt, "vS")
                        ctx.inc(tt.transpose(self.PT[:, slot * 128:slot * 128 + 128],
                                             self.xbf[:, t * 1024 + cc * 128: t * 1024 + (cc + 1) * 128],
                                             self.ident_b[:]), "tS")
                    blk.tensor(t_)

                    def v_(v, t=t, cc=cc, slot=slot):
                        ctx.wait_all(v, "tS")
                        ctx.inc(v.tensor_copy(self.xT[:, cc * N + t * 128: cc * N + (t + 1) * 128],
                                              self.PT[:, slot * 128:slot * 128 + 128]), "vS")
                    blk.vector(v_)

        if upto == "a":
            return
        with nc.Block(f"a_b{b}") as blk:
            for jt in range(8):
                slot = jt % 2
                def gw_(g, jt=jt, slot=slot):
                    ctx.wait_all(g, "tS")
                    ctx.inc(g.dma_start(self.wq_sb[:, slot * 1024:(slot + 1) * 1024]
                                        .rearrange("p (cc j) -> p cc j", j=128),
                                        io["wq"][:, jt * 128:(jt + 1) * 128]
                                        .rearrange("(cc p) j -> p cc j", p=128)), "ld", 16)
                blk.gpsimd(gw_)
                ld_w = ctx.cnt["ld"]
                for ch in range(4):
                    pq = self.P[ch % 2]
                    def t_(t, jt=jt, ch=ch, pq=pq, slot=slot, ld_w=ld_w):
                        t.wait_ge(ctx.sem("ld"), ld_w)
                        ctx.wait_all(t, "aS")
                        for cc in range(8):
                            ctx.inc(t.matmul(pq[:], self.wq_sb[:, slot * 1024 + cc * 128: slot * 1024 + (cc + 1) * 128],
                                             self.xT[:, cc * N + ch * 512: cc * N + (ch + 1) * 512],
                                             start=(cc == 0), stop=(cc == 7)), "tS")
                    blk.tensor(t_)
                    tq = ctx.cnt["tS"]

                    def a_(a, jt=jt, ch=ch, pq=pq, tq=tq):
                        a.wait_ge(ctx.sem("tS"), tq)
                        ctx.inc(a.activation(self.qT[:, jt * N + ch * 512: jt * N + (ch + 1) * 512],
                                             pq[:], AF.Identity, scale=SC), "aS")
                    blk.scalar(a_)

            for which, dst in [("wk_down", self.klat), ("wv_down", self.vlat)]:
                def gw2_(g, which=which):
                    ctx.wait_all(g, "tS")
                    ctx.inc(g.dma_start(self.wkv_sb[:].rearrange("p (cc l) -> p cc l", l=64),
                                        io[which].rearrange("(cc p) l -> p cc l", p=128)), "ld", 16)
                blk.gpsimd(gw2_)
                ld_w = ctx.cnt["ld"]
                for ch in range(4):
                    pl = self.P[2]
                    def t2_(t, ch=ch, pl=pl, ld_w=ld_w):
                        t.wait_ge(ctx.sem("ld"), ld_w)
                        ctx.wait_all(t, "aS")
                        for cc in range(8):
                            ctx.inc(t.matmul(pl[0:64, 0:512], self.wkv_sb[:, cc * 64:(cc + 1) * 64],
                                             self.xT[:, cc * N + ch * 512: cc * N + (ch + 1) * 512],
                                             start=(cc == 0), stop=(cc == 7)), "tS")
                    blk.tensor(t2_)
                    tl = ctx.cnt["tS"]

                    def a2_(a, ch=ch, dst=dst, pl=pl, tl=tl):
                        a.wait_ge(ctx.sem("tS"), tl)
                        a.activation(dst[0:32, ch * 512:(ch + 1) * 512], pl[0:32, 0:512], AF.Identity)
                        ctx.inc(a.activation(dst[0:32, N + ch * 512: N + (ch + 1) * 512], pl[32:64, 0:512], AF.Identity), "aS")
                    blk.scalar(a2_)

            for g_i in range(2):
                for ch in range(4):
                    pk = self.P[3]
                    def t3_(t, g_i=g_i, ch=ch, pk=pk):
                        ctx.wait_all(t, "aS")
                        ctx.inc(t.matmul(pk[:, 0:512], self.wup_sb[0:32, 0:128],
                                         self.klat[0:32, g_i * N + ch * 512: g_i * N + (ch + 1) * 512],
                                         start=True, stop=True), "tS")
                    blk.tensor(t3_)
                    tk = ctx.cnt["tS"]

                    def a3_(a, g_i=g_i, ch=ch, pk=pk, tk=tk):
                        a.wait_ge(ctx.sem("tS"), tk)
                        ctx.inc(a.activation(self.kT[:, g_i * N + ch * 512: g_i * N + (ch + 1) * 512],
                                             pk[:, 0:512], AF.Identity), "aS")
                    blk.scalar(a3_)

                for tt_i in range(NT):
                    pv = self.P[4]
                    def t4_(t, g_i=g_i, tt_i=tt_i, pv=pv):
                        ctx.wait_all(t, "aS")
                        ctx.inc(t.matmul(pv[:, 0:128], self.vlat[0:32, g_i * N + tt_i * 128: g_i * N + (tt_i + 1) * 128],
                                         self.wup_sb[0:32, 128:256],
                                         start=True, stop=True), "tS")
                    blk.tensor(t4_)
                    tv = ctx.cnt["tS"]

                    def a4_(a, g_i=g_i, tt_i=tt_i, pv=pv, tv=tv):
                        a.wait_ge(ctx.sem("tS"), tv)
                        base = (tt_i * 2 + g_i) * 130
                        src = pv[:, 0:128].rearrange("p (u d) -> p u d", u=2)
                        dst = self.vsl[:, base:base + 130].rearrange("p (u d) -> p u d", d=65)[:, :, 0:64]
                        ctx.inc(a.activation(dst, src, AF.Identity), "aS")
                    blk.scalar(a4_)

        if upto == "b":
            return
        with nc.Block(f"a_c{b}") as blk:
            for h in range(H):
                g_i = h // 4
                for qg in range(4):
                    ya = self.P[3][0:65, 0:512]
                    yb = self.P[4][0:65, 0:512]
                    nkt = qg * 4 + 4
                    exp_done = []
                    av_tS = []
                    for kt in range(nkt):
                        ps = self.P[kt % 2][:, 0:512]
                        wait_e = exp_done[kt - 2] if kt >= 2 else 0

                        def t_(t, h=h, g_i=g_i, qg=qg, kt=kt, ps=ps, wait_e=wait_e):
                            if wait_e:
                                t.wait_ge(ctx.sem("aS"), wait_e)
                            else:
                                ctx.wait_all(t, "vS")
                                ctx.wait_all(t, "aS")
                            ctx.inc(t.matmul(ps, self.kT[:, g_i * N + kt * 128: g_i * N + (kt + 1) * 128],
                                             self.qT[:, h * N + qg * 512: h * N + (qg + 1) * 512],
                                             start=True, stop=True), "tS")
                        blk.tensor(t_)
                        tS_s = ctx.cnt["tS"]
                        wait_av = av_tS[kt - 1] if kt >= 1 else 0

                        def a_(a, kt=kt, qg=qg, ps=ps, tS_s=tS_s, wait_av=wait_av):
                            a.wait_ge(ctx.sem("tS"), max(tS_s, wait_av))
                            ctx.inc(a.activation(self.e_sb[:], ps, AF.Exp), "aS")
                        blk.scalar(a_)
                        exp_done.append(ctx.cnt["aS"])
                        aS_e = ctx.cnt["aS"]

                        m = kt - qg * 4
                        if m >= 0:
                            def vm_(v, m=m, aS_e=aS_e):
                                v.wait_ge(ctx.sem("aS"), aS_e)
                                ctx.inc(v.tensor_tensor(self.e_sb[:], self.e_sb[:],
                                                        self.masks[:, m * 512:(m + 1) * 512], OP.mult), "vS")
                            blk.vector(vm_)
                            gate_sem, gate_cnt = "vS", ctx.cnt["vS"]
                        else:
                            gate_sem, gate_cnt = "aS", aS_e

                        def t2_(t, g_i=g_i, kt=kt, qg=qg, ya=ya, yb=yb, gate_sem=gate_sem, gate_cnt=gate_cnt, nkt=nkt):
                            t.wait_ge(ctx.sem(gate_sem), gate_cnt)
                            base = (kt * 2 + g_i) * 130
                            st_ = (kt == 0); sp = (kt == nkt - 1)
                            ctx.inc(t.matmul(ya, self.vsl[:, base:base + 65], self.e_sb[:], start=st_, stop=sp), "tS")
                            ctx.inc(t.matmul(yb, self.vsl[:, base + 65:base + 130], self.e_sb[:], start=st_, stop=sp), "tS")
                        blk.tensor(t2_)
                        av_tS.append(ctx.cnt["tS"])

                    tS_av = ctx.cnt["tS"]

                    def ve_(v, ya=ya, tS_av=tS_av):
                        v.wait_ge(ctx.sem("tS"), tS_av)
                        self.vF(v)
                        v.reciprocal(self.r1st[0:1, :], ya[64:65, :])
                        self.vF(v)
                        ctx.inc(v.tensor_copy(self.bigfa[:], self.bigfb[:]), "vS")
                    blk.vector(ve_)
                    vS_r = ctx.cnt["vS"]

                    def te_(t, vS_r=vS_r):
                        t.wait_ge(ctx.sem("vS"), vS_r)
                        ctx.inc(t.matmul(self.P[5][:, 0:512], self.e2_f[:], self.r1st[:], start=True, stop=True), "tS")
                    blk.tensor(te_)
                    tS_b = ctx.cnt["tS"]

                    def ve2_(v, h=h, qg=qg, ya=ya, yb=yb, tS_b=tS_b):
                        v.wait_ge(ctx.sem("tS"), tS_b)
                        v.tensor_copy(self.rb_sb[:], self.P[5][:, 0:512])
                        qoff = h * N + qg * 512
                        ctx.inc(v.tensor_tensor(self.ynT[0:64, qoff:qoff + 512], ya[0:64, :],
                                                self.rb_sb[0:64, :], OP.mult), "vS")
                        ctx.inc(v.tensor_tensor(self.ynT[64:128, qoff:qoff + 512], yb[0:64, :],
                                                self.rb_sb[64:128, :], OP.mult), "vS")
                    blk.vector(ve2_)

        if upto == "c":
            return
        with nc.Block(f"a_d{b}") as blk:
            for tt_i in range(NT):
                vS_pre = ctx.cnt["vS"]
                aS_pre = ctx.cnt["aS"]

                def t3_(t, tt_i=tt_i, vS_pre=vS_pre, aS_pre=aS_pre):
                    t.wait_ge(ctx.sem("vS"), vS_pre)
                    t.wait_ge(ctx.sem("aS"), aS_pre)
                    for cc2 in range(2):
                        po = self.P[cc2]
                        for h in range(H):
                            ctx.inc(t.matmul(po[:], self.ynT[:, h * N + tt_i * 128: h * N + (tt_i + 1) * 128],
                                             self.wo_sb[:, h * 1024 + cc2 * 512: h * 1024 + (cc2 + 1) * 512],
                                             start=(h == 0), stop=(h == 7)), "tS")
                blk.tensor(t3_)
                tS_o = ctx.cnt["tS"]

                def v2_(v, tt_i=tt_i, tS_o=tS_o):
                    v.wait_ge(ctx.sem("tS"), tS_o)
                    v.tensor_tensor(self.tsb[:, 0:512], self.P[0][:],
                                    self.xbf[:, tt_i * 1024: tt_i * 1024 + 512], OP.add)
                    v.tensor_tensor(self.tsb[:, 512:1024], self.P[1][:],
                                    self.xbf[:, tt_i * 1024 + 512: (tt_i + 1) * 1024], OP.add)
                    v.bn_stats(self.bn6[:, 0:6], self.tsb[:, 0:512])
                    v.bn_stats(self.bn6[:, 6:12], self.tsb[:, 512:1024])
                    self.vf(v)
                    self.vf(v)
                    v.bn_aggr(self.mv2[:], self.bn6[:])
                    self.vf(v)
                    self.vf(v)
                    v.tensor_scalar(self.lnm[:], self.mv2[:, 0:1], -1.0, None, OP.mult)
                    self.vf(v)
                    ctx.inc(v.tensor_copy(self.fencea[:], self.fenceb[:]), "vS")
                blk.vector(v2_)
                vS_ln = ctx.cnt["vS"]

                def a3_(a, vS_ln=vS_ln):
                    a.wait_ge(ctx.sem("vS"), vS_ln)
                    a.activation(self.lnr[:], self.mv2[:, 1:2], AF.Ln, bias=self.eps_col[:])
                    self.af(a)
                    a.activation(self.lnr[:], self.lnr[:], AF.Exp, scale=-0.5)
                    self.af(a)
                    self.af(a)
                blk.scalar(a3_)
                aS_ln = ctx.cnt["aS"]

                def v3_(v, aS_ln=aS_ln):
                    v.wait_ge(ctx.sem("aS"), aS_ln)
                    v.tensor_tensor(self.lnm[:], self.lnm[:], self.lnr[:], OP.mult)
                    self.vf(v)
                    ctx.inc(v.tensor_copy(self.fencea[:], self.fenceb[:]), "vS")
                blk.vector(v3_)
                vS_nm = ctx.cnt["vS"]

                def a4_(a, vS_nm=vS_nm):
                    a.wait_ge(ctx.sem("vS"), vS_nm)
                    ctx.inc(a.activation(self.xout_sb[:], self.tsb[:], AF.Identity,
                                         bias=self.lnm[:], scale=self.lnr[:]), "aS")
                blk.scalar(a4_)
                aS_x = ctx.cnt["aS"]

                def s_(s, b=b, tt_i=tt_i, aS_x=aS_x):
                    s.wait_ge(ctx.sem("aS"), aS_x)
                    ctx.inc(s.dma_start(io["x_out"][b, tt_i * 128:(tt_i + 1) * 128, :], self.xout_sb[:]), "st", 16)
                blk.sync(s_)

    def finish(self):
        nc, ctx = self.nc, self.ctx
        with nc.Block("a_fin") as blk:
            def g_(g):
                ctx.wait_all(g, "st")
            blk.gpsimd(g_)


def l1_masks():
    masks = np.zeros((4, 128, 512), np.float32)
    for m in range(4):
        for k in range(128):
            q0 = m * 128 + k
            if q0 < 512:
                masks[m, k, q0:] = 1.0
    return masks


def build_l1attn_launch(upto="d"):
    nc = bass.Bass("TRN2", target_bir_lowering=False)
    io = {}
    io["x_in"] = nc.dram_tensor("x_in", [2, N, C], F32, kind="ExternalInput").ap()
    io["x_out"] = nc.dram_tensor("x_out", [2, N, C], F32, kind="ExternalOutput").ap()
    io["wq"] = nc.dram_tensor("wq", [C, C], F32, kind="ExternalInput").ap()
    io["wk_down"] = nc.dram_tensor("wk_down", [C, 64], F32, kind="ExternalInput").ap()
    io["wv_down"] = nc.dram_tensor("wv_down", [C, 64], F32, kind="ExternalInput").ap()
    io["wk_up"] = nc.dram_tensor("wk_up", [32, 128], F32, kind="ExternalInput").ap()
    io["wv_up"] = nc.dram_tensor("wv_up", [32, 128], F32, kind="ExternalInput").ap()
    io["wo"] = nc.dram_tensor("wo", [C, C], F32, kind="ExternalInput").ap()
    io["c_ident"] = nc.dram_tensor("c_ident", [128, 128], F32, kind="ExternalInput").ap()
    io["c_e2"] = nc.dram_tensor("c_e2", [2, 128], F32, kind="ExternalInput").ap()
    io["c_masks"] = nc.dram_tensor("c_masks", [4, 128, 512], F32, kind="ExternalInput").ap()
    ctx = Ctx(nc)
    a_ = L1Attn(nc, ctx, io)
    a_.prologue()
    for b in range(2):
        a_.batch(b, upto=upto)
    a_.finish()
    nc.finalize()
    return nc


class L1FFN:
    CAP = 5120.0

    def __init__(self, nc, ctx, io, totals_only=False):
        self.nc, self.ctx, self.io = nc, ctx, io
        self.totals_only = totals_only
        A = nc.alloc_sbuf_tensor
        TC = 1024
        self.TC = TC
        self.xbf = A("m_xbf", [128, 8 * 1024], BF16)
        self.xT = A("m_xT", [128, 8 * TC], BF16)
        self.wr_sb = A("m_wr", [128, 64], BF16)
        self.lgall = A("m_lg", [128, 64], F32)
        self.m1 = A("m_m1", [128, 8], F32)
        self.exin = A("m_exin", [128, 64], F32)
        self.ex = A("m_ex", [128, 64], F32)
        self.oh = A("m_oh", [128, 64], F32)
        self.ohb = A("m_ohb", [128, 64], BF16)
        self.sume = A("m_sume", [128, 8], F32)
        self.pmax = A("m_pmax", [128, 8], F32)
        self.ident_b = A("m_ident", [128, 128], BF16)
        self.ones_col = A("m_ones", [128, 1], BF16)
        self.fencea = A("m_fa", [128, 8], F32)
        self.fenceb = A("m_fb", [128, 8], F32)
        self.bigfa = A("m_bfa", [128, 256], BF16)
        self.bigfb = A("m_bfb", [128, 256], BF16)
        self.cnt_sb = A("m_cnt", [1, 8], F32)
        if not totals_only:
            self.hT = A("m_hT", [128, 32 * TC], BF16)
            self.acc = A("m_acc", [128, 8 * 1024], F32)
            self.w1_sb = A("m_w1", [128, 2 * 1024], BF16)
            self.w2_sb = A("m_w2", [128, 16 * 1024], BF16)
            self.ohT = A("m_ohT", [8, TC], F32)
            self.posT = A("m_posT", [8, TC], F32)
            self.runc = A("m_runc", [8, 1], F32)
            self.prodT = A("m_prod", [8, TC], F32)
            self.poT = A("m_poT", [1, TC], F32)
            self.kept2 = A("m_kept2", [2, TC], F32)
            self.kb2 = A("m_kb2", [2, TC], BF16)
            self.keptcol = A("m_keptcol", [128, 1], F32)
            self.tmpcol = A("m_tmpcol", [128, 1], F32)
            self.cwall = A("m_cwall", [128, 64], F32)
            self.passall = A("m_pass", [128, 8], F32)
            self.sel_sb = A("m_sel", [8, 1], F32)
            self.atot_sb = A("m_atot", [8, 8], F32)
            self.off_row = A("m_offr", [1, 8], F32)
            self.off_col = A("m_offc", [8, 1], F32)
            self.tsb = A("m_tsb", [128, 1024], F32)
            self.bn6 = A("m_bn6", [128, 12], F32)
            self.mv2 = A("m_mv2", [128, 2], F32)
            self.lnr = A("m_lnr", [128, 1], F32)
            self.lnm = A("m_lnm", [128, 1], F32)
            self.xout_sb = A("m_xout", [128, 1024], F32)
            self.eps_col = A("m_eps", [128, 1], F32)
        self.P = [nc.alloc_psum_tensor(f"m_pool{i}", [128, 512], F32) for i in range(6)]
        self.PT = nc.alloc_psum_tensor("m_poolT", [128, 256], BF16)

    def vf(self, v):
        self.ctx.inc(v.tensor_copy(self.fenceb[:], self.fencea[:]), "vS")

    def vF(self, v):
        self.ctx.inc(v.tensor_copy(self.bigfb[:], self.bigfa[:]), "vS")

    def af(self, a):
        self.ctx.inc(a.activation(self.fenceb[:], self.fencea[:], AF.Identity), "aS")

    def prologue(self):
        nc, ctx, io = self.nc, self.ctx, self.io
        with nc.Block("m_pro") as blk:
            def g_(g):
                ctx.inc(g.dma_start(self.ident_b[:], io["c_ident"]), "ld", 16)
                ctx.inc(g.dma_start(self.wr_sb[:].rearrange("p (cc e) -> p cc e", e=8),
                                    io["Wr"].rearrange("(cc p) e -> p cc e", p=128)), "ld", 16)
                if not self.totals_only:
                    ctx.inc(g.dma_start(self.sel_sb[:], io["c_sel"]), "ld", 16)
                    ctx.inc(g.dma_start(self.atot_sb[:], io["all_totals"]), "ld", 16)
            blk.gpsimd(g_)

            def v_(v):
                ctx.wait_all(v, "ld")
                v.memset(self.fencea[:], 0.0)
                v.memset(self.bigfa[:], 0.0)
                v.memset(self.ones_col[:], 1.0)
                if not self.totals_only:
                    v.memset(self.eps_col[:], 1e-5)
                ctx.inc(v.memset(self.cnt_sb[:], 0.0), "vS")
            blk.vector(v_)

        if not self.totals_only:
            with nc.Block("m_off") as blk:
                def t_(t):
                    ctx.wait_all(t, "vS")
                    ctx.inc(t.matmul(self.P[5][0:1, 0:8], self.sel_sb[:], self.atot_sb[:],
                                     start=True, stop=True), "tS")
                blk.tensor(t_)

                def v2_(v):
                    ctx.wait_all(v, "tS")
                    self.vF(v)
                    v.tensor_scalar(self.off_row[:], self.P[5][0:1, 0:8], 1.0, None, OP.subtract)
                    self.vF(v)
                    ctx.inc(v.tensor_copy(self.bigfa[:], self.bigfb[:]), "vS")
                blk.vector(v2_)

                def s_(s):
                    ctx.wait_all(s, "vS")
                    ctx.inc(s.dma_start(io["scr8"][None, :], self.off_row[0:1, :]), "st", 16)
                blk.sync(s_)

                def g2_(g):
                    ctx.wait_all(g, "st")
                    ctx.inc(g.dma_start(self.runc[:], io["scr8"][:, None]), "ld", 16)
                blk.gpsimd(g2_)

    def chunk(self, ci):
        nc, ctx, io = self.nc, self.ctx, self.io
        b = ci // 2
        r0 = (ci % 2) * 1024
        TC = self.TC

        with nc.Block(f"m_a{ci}") as blk:
            def g_(g):
                ctx.wait_all(g, "st")
                for t in range(8):
                    ctx.inc(g.dma_start(self.xbf[:, t * 1024:(t + 1) * 1024],
                                        io["x_in"][b, r0 + t * 128: r0 + (t + 1) * 128, :]), "ld", 16)
            blk.gpsimd(g_)
            ld_t = ctx.cnt["ld"]
            for t in range(8):
                for cc in range(8):
                    slot = (t * 8 + cc) % 2
                    def t_(tt, t=t, cc=cc, slot=slot, ld_t=ld_t):
                        tt.wait_ge(ctx.sem("ld"), ld_t)
                        ctx.wait_all(tt, "vS")
                        ctx.inc(tt.transpose(self.PT[:, slot * 128:slot * 128 + 128],
                                             self.xbf[:, t * 1024 + cc * 128: t * 1024 + (cc + 1) * 128],
                                             self.ident_b[:]), "tS")
                    blk.tensor(t_)

                    def v_(v, t=t, cc=cc, slot=slot):
                        ctx.wait_all(v, "tS")
                        ctx.inc(v.tensor_copy(self.xT[:, cc * TC + t * 128: cc * TC + (t + 1) * 128],
                                              self.PT[:, slot * 128:slot * 128 + 128]), "vS")
                    blk.vector(v_)

        with nc.Block(f"m_r{ci}") as blk:
            for t in range(8):
                def tg_(tt, t=t):
                    ctx.wait_all(tt, "vS")
                    for cc in range(8):
                        ctx.inc(tt.matmul(self.P[4][:, t * 8:(t + 1) * 8],
                                          self.xT[:, cc * TC + t * 128: cc * TC + (t + 1) * 128],
                                          self.wr_sb[:, cc * 8:(cc + 1) * 8],
                                          start=(cc == 0), stop=(cc == 7)), "tS")
                blk.tensor(tg_)
            tS_g = ctx.cnt["tS"]

            def e8(x):
                return x.rearrange("p (t e) -> p t e", e=8)

            def b8(x):
                return x.rearrange("p (t o) -> p t o", o=1).broadcast_to((128, 8, 8))

            def vg_(v, tS_g=tS_g):
                v.wait_ge(ctx.sem("tS"), tS_g)
                self.vF(v)
                v.tensor_copy(self.lgall[:], self.P[4][:, 0:64])
                v.reduce_max(self.m1[:].rearrange("p (t o) -> p t o", o=1), e8(self.lgall[:]), axis=AX)
                self.vF(v)
                self.vF(v)
                v.tensor_tensor(e8(self.exin[:]), e8(self.lgall[:]), b8(self.m1[:]), OP.subtract)
                v.tensor_tensor(e8(self.oh[:]), e8(self.lgall[:]), b8(self.m1[:]), OP.is_equal)
                self.vF(v)
                v.tensor_copy(self.ohb[:], self.oh[:])
                ctx.inc(v.tensor_copy(self.bigfa[:], self.bigfb[:]), "vS")
            blk.vector(vg_)
            vS_g = ctx.cnt["vS"]

            def ag_(a, vS_g=vS_g):
                a.wait_ge(ctx.sem("vS"), vS_g)
                a.activation(self.ex[:], self.exin[:], AF.Exp)
                self.af(a)
                self.af(a)
            blk.scalar(ag_)
            aS_g = ctx.cnt["aS"]

            def vg2_(v, aS_g=aS_g):
                v.wait_ge(ctx.sem("aS"), aS_g)
                v.reduce_sum(self.sume[:].rearrange("p (t o) -> p t o", o=1), e8(self.ex[:]), axis=AX)
                self.vF(v)
                self.vF(v)
                v.reciprocal(self.pmax[:], self.sume[:])
                self.vF(v)
                self.vF(v)
                ctx.inc(v.tensor_copy(self.bigfa[:], self.bigfb[:]), "vS")
            blk.vector(vg2_)
            vS_p = ctx.cnt["vS"]

            def tc_(t, ci=ci, vS_p=vS_p):
                t.wait_ge(ctx.sem("vS"), vS_p)
                for t_i in range(8):
                    ctx.inc(t.matmul(self.P[5][0:1, 8:16], self.ones_col[:], self.ohb[:, t_i * 8:(t_i + 1) * 8],
                                     start=(ci == 0 and t_i == 0), stop=(ci == 3 and t_i == 7)), "tS")
            blk.tensor(tc_)

        if self.totals_only:
            if ci == 3:
                with nc.Block("m_tot") as blk:
                    def vt_(v):
                        ctx.wait_all(v, "tS")
                        self.vF(v)
                        v.tensor_copy(self.cnt_sb[:], self.P[5][0:1, 8:16])
                        self.vF(v)
                        ctx.inc(v.tensor_copy(self.bigfa[:], self.bigfb[:]), "vS")
                    blk.vector(vt_)

                    def st_(s):
                        ctx.wait_all(s, "vS")
                        ctx.inc(s.dma_start(io["totals"][None, :], self.cnt_sb[0:1, :]), "st", 16)
                    blk.sync(st_)
            return

        with nc.Block(f"m_p{ci}") as blk:
            for t in range(8):
                slot = t % 2
                def tt_(tt, t=t, slot=slot):
                    ctx.wait_all(tt, "vS")
                    ctx.inc(tt.transpose(self.PT[0:8, slot * 128:slot * 128 + 128],
                                         self.ohb[:, t * 8:(t + 1) * 8], self.ident_b[:]), "tS")
                blk.tensor(tt_)
                tS_t = ctx.cnt["tS"]

                def vt_(v, t=t, slot=slot, tS_t=tS_t):
                    v.wait_ge(ctx.sem("tS"), tS_t)
                    ctx.inc(v.tensor_copy(self.ohT[:, t * 128:(t + 1) * 128],
                                          self.PT[0:8, slot * 128:slot * 128 + 128]), "vS")
                blk.vector(vt_)

            def vs_(v):
                ctx.wait_all(v, "vS")
                v.tensor_tensor_scan(self.posT[:], self.ohT[:], self.ohT[:], self.runc[:], OP.add, OP.bypass)
                self.vF(v)
                self.vF(v)
                v.tensor_copy(self.runc[:], self.posT[:, TC - 1:TC])
                v.tensor_tensor(self.prodT[:], self.posT[:], self.ohT[:], OP.mult)
                v.memset(self.kept2[:], 0.0)
                self.vF(v)
                ctx.inc(v.tensor_copy(self.bigfa[:], self.bigfb[:]), "vS")
            blk.vector(vs_)
            vS_s = ctx.cnt["vS"]

            def gpo_(g, vS_s=vS_s):
                g.wait_ge(ctx.sem("vS"), vS_s)
                ctx.inc(g.reduce_sum(self.poT[:], self.prodT[:], axis=mybir.AxisListType.C), "gS")
            blk.gpsimd(gpo_)
            gS_po = ctx.cnt["gS"]

            def vko_(v, gS_po=gS_po):
                v.wait_ge(ctx.sem("gS"), gS_po)
                self.vF(v)
                v.tensor_scalar(self.kept2[0:1, :], self.poT[:], L1FFN.CAP, None, OP.is_lt)
                self.vF(v)
                ctx.inc(v.tensor_copy(self.bigfa[:], self.bigfb[:]), "vS")
            blk.vector(vko_)

            def vkb_(v):
                ctx.wait_all(v, "vS")
                v.tensor_copy(self.kb2[:], self.kept2[:])
                self.vF(v)
                ctx.inc(v.tensor_copy(self.bigfa[:], self.bigfb[:]), "vS")
            blk.vector(vkb_)
            vS_kb = ctx.cnt["vS"]

            for t_i in range(8):
                def tkt_(t, t_i=t_i, vS_kb=vS_kb):
                    t.wait_ge(ctx.sem("vS"), vS_kb)
                    ctx.wait_all(t, "vS")
                    ctx.inc(t.transpose(self.PT[:, 0:2], self.kb2[:, t_i * 128:(t_i + 1) * 128],
                                        self.ident_b[0:2, 0:2]), "tS")
                blk.tensor(tkt_)
                tS_kt = ctx.cnt["tS"]

                def vcw_(v, t_i=t_i, tS_kt=tS_kt):
                    v.wait_ge(ctx.sem("tS"), tS_kt)
                    self.vF(v)
                    v.tensor_copy(self.keptcol[:], self.PT[:, 0:1])
                    self.vF(v)
                    v.tensor_tensor(self.tmpcol[:], self.keptcol[:], self.pmax[:, t_i:t_i + 1], OP.mult)
                    v.tensor_scalar(self.passall[:, t_i:t_i + 1], self.keptcol[:], -1.0, 1.0, OP.mult, OP.add)
                    self.vF(v)
                    v.tensor_scalar(self.cwall[:, t_i * 8:(t_i + 1) * 8], self.oh[:, t_i * 8:(t_i + 1) * 8],
                                    self.tmpcol[:], None, OP.mult)
                    self.vF(v)
                    ctx.inc(v.tensor_copy(self.bigfa[:], self.bigfb[:]), "vS")
                blk.vector(vcw_)

            def vi_(v):
                for t in range(8):
                    ctx.inc(v.tensor_scalar(self.acc[:, t * 1024:(t + 1) * 1024],
                                            self.xbf[:, t * 1024:(t + 1) * 1024],
                                            self.passall[:, t:t + 1], None, OP.mult), "vS")
            blk.vector(vi_)

        for e in range(8):
            with nc.Block(f"m_e{ci}_{e}") as blk:
                for fi in range(32):
                    def gw_(g, fi=fi, e=e):
                        ctx.wait_all(g, "tS")
                        slot = fi % 2
                        ctx.inc(g.dma_start(
                            self.w1_sb[:, slot * 1024:(slot + 1) * 1024].rearrange("p (cc f) -> p cc f", f=128),
                            io["W1"][e][:, fi * 128:(fi + 1) * 128].rearrange("(cc p) f -> p cc f", p=128)), "ld", 16)
                    blk.gpsimd(gw_)
                    ld_w = ctx.cnt["ld"]
                    for th in range(2):
                        pg = self.P[th]
                        def tm_(t, fi=fi, th=th, pg=pg, ld_w=ld_w):
                            t.wait_ge(ctx.sem("ld"), ld_w)
                            ctx.wait_all(t, "aS")
                            slot = fi % 2
                            for cc in range(8):
                                ctx.inc(t.matmul(pg[:], self.w1_sb[:, slot * 1024 + cc * 128: slot * 1024 + (cc + 1) * 128],
                                                 self.xT[:, cc * TC + th * 512: cc * TC + (th + 1) * 512],
                                                 start=(cc == 0), stop=(cc == 7)), "tS")
                        blk.tensor(tm_)
                        tS_m = ctx.cnt["tS"]

                        def am_(a, fi=fi, th=th, pg=pg, tS_m=tS_m):
                            a.wait_ge(ctx.sem("tS"), tS_m)
                            ctx.inc(a.activation(self.hT[:, fi * TC + th * 512: fi * TC + (th + 1) * 512],
                                                 pg[:], AF.Relu), "aS")
                        blk.scalar(am_)

                for ih in range(2):
                    def gw2_(g, e=e, ih=ih):
                        ctx.wait_all(g, "tS")
                        for q in range(4):
                            ctx.inc(g.dma_start(self.w2_sb[:, q * 4096:(q + 1) * 4096].rearrange("p (i c) -> p i c", c=1024),
                                                io["W2"][e][ih * 2048 + q * 512: ih * 2048 + (q + 1) * 512, :]
                                                .rearrange("(i p) c -> p i c", p=128)), "ld", 16)
                    blk.gpsimd(gw2_)
                    ld_2 = ctx.cnt["ld"]
                    ev_done = []
                    for tt_i in range(8):
                        for cc2 in range(2):
                            po = self.P[2 + (tt_i % 2) * 2 + cc2]
                            wait_ev = ev_done[(tt_i - 2) * 2 + cc2] if tt_i >= 2 else 0
                            def tw_(t, ih=ih, tt_i=tt_i, cc2=cc2, po=po, ld_2=ld_2, wait_ev=wait_ev):
                                t.wait_ge(ctx.sem("ld"), ld_2)
                                if wait_ev:
                                    t.wait_ge(ctx.sem("vS"), wait_ev)
                                else:
                                    ctx.wait_all(t, "vS")
                                for ic in range(16):
                                    fi_g = ih * 16 + ic
                                    ctx.inc(t.matmul(po[:], self.hT[:, fi_g * TC + tt_i * 128: fi_g * TC + (tt_i + 1) * 128],
                                                     self.w2_sb[:, ic * 1024 + cc2 * 512: ic * 1024 + (cc2 + 1) * 512],
                                                     start=(ic == 0), stop=(ic == 15)), "tS")
                            blk.tensor(tw_)
                            tS_wd = ctx.cnt["tS"]

                            def vev_(v, e=e, tt_i=tt_i, cc2=cc2, po=po, tS_wd=tS_wd):
                                v.wait_ge(ctx.sem("tS"), tS_wd)
                                dst = self.acc[:, tt_i * 1024 + cc2 * 512: tt_i * 1024 + (cc2 + 1) * 512]
                                ctx.inc(v.scalar_tensor_tensor(dst, po[:], self.cwall[:, tt_i * 8 + e:tt_i * 8 + e + 1],
                                                               dst, OP.mult, OP.add), "vS")
                            blk.vector(vev_)
                            ev_done.append(ctx.cnt["vS"])

        with nc.Block(f"m_l{ci}") as blk:
            for tt_i in range(8):
                vS_pre = ctx.cnt["vS"]
                def v2_(v, tt_i=tt_i, vS_pre=vS_pre):
                    v.wait_ge(ctx.sem("vS"), vS_pre)
                    ctx.wait_all(v, "aS")
                    v.tensor_tensor(self.tsb[:], self.acc[:, tt_i * 1024:(tt_i + 1) * 1024],
                                    self.xbf[:, tt_i * 1024:(tt_i + 1) * 1024], OP.add)
                    v.bn_stats(self.bn6[:, 0:6], self.tsb[:, 0:512])
                    v.bn_stats(self.bn6[:, 6:12], self.tsb[:, 512:1024])
                    self.vf(v)
                    self.vf(v)
                    v.bn_aggr(self.mv2[:], self.bn6[:])
                    self.vf(v)
                    self.vf(v)
                    v.tensor_scalar(self.lnm[:], self.mv2[:, 0:1], -1.0, None, OP.mult)
                    self.vf(v)
                    ctx.inc(v.tensor_copy(self.fencea[:], self.fenceb[:]), "vS")
                blk.vector(v2_)
                vS_ln = ctx.cnt["vS"]

                def a3_(a, vS_ln=vS_ln):
                    a.wait_ge(ctx.sem("vS"), vS_ln)
                    a.activation(self.lnr[:], self.mv2[:, 1:2], AF.Ln, bias=self.eps_col[:])
                    self.af(a)
                    a.activation(self.lnr[:], self.lnr[:], AF.Exp, scale=-0.5)
                    self.af(a)
                    self.af(a)
                blk.scalar(a3_)
                aS_ln = ctx.cnt["aS"]

                def v3_(v, aS_ln=aS_ln):
                    v.wait_ge(ctx.sem("aS"), aS_ln)
                    v.tensor_tensor(self.lnm[:], self.lnm[:], self.lnr[:], OP.mult)
                    self.vf(v)
                    ctx.inc(v.tensor_copy(self.fencea[:], self.fenceb[:]), "vS")
                blk.vector(v3_)
                vS_nm = ctx.cnt["vS"]

                def a4_(a, vS_nm=vS_nm):
                    a.wait_ge(ctx.sem("vS"), vS_nm)
                    ctx.inc(a.activation(self.xout_sb[:], self.tsb[:], AF.Identity,
                                         bias=self.lnm[:], scale=self.lnr[:]), "aS")
                blk.scalar(a4_)
                aS_x = ctx.cnt["aS"]

                def s_(s, b=b, r0=r0, tt_i=tt_i, aS_x=aS_x):
                    s.wait_ge(ctx.sem("aS"), aS_x)
                    ctx.inc(s.dma_start(io["x_out"][b, r0 + tt_i * 128: r0 + (tt_i + 1) * 128, :], self.xout_sb[:]), "st", 16)
                blk.sync(s_)

    def finish(self):
        nc, ctx = self.nc, self.ctx
        with nc.Block("m_fin") as blk:
            def g_(g):
                ctx.wait_all(g, "st")
            blk.gpsimd(g_)


def build_l1ffn_launch(totals_only=False):
    nc = bass.Bass("TRN2", target_bir_lowering=False)
    io = {}
    io["x_in"] = nc.dram_tensor("x_in", [2, N, C], F32, kind="ExternalInput").ap()
    io["Wr"] = nc.dram_tensor("Wr", [C, 8], F32, kind="ExternalInput").ap()
    io["c_ident"] = nc.dram_tensor("c_ident", [128, 128], F32, kind="ExternalInput").ap()
    if totals_only:
        io["totals"] = nc.dram_tensor("totals", [8], F32, kind="ExternalOutput").ap()
    else:
        io["x_out"] = nc.dram_tensor("x_out", [2, N, C], F32, kind="ExternalOutput").ap()
        io["W1"] = nc.dram_tensor("W1", [8, C, 4096], F32, kind="ExternalInput").ap()
        io["W2"] = nc.dram_tensor("W2", [8, 4096, C], F32, kind="ExternalInput").ap()
        io["c_sel"] = nc.dram_tensor("c_sel", [8, 1], F32, kind="ExternalInput").ap()
        io["all_totals"] = nc.dram_tensor("all_totals", [8, 8], F32, kind="ExternalInput").ap()
        io["scr8"] = nc.dram_tensor("scr8", [8], F32, kind="Internal").ap()
    ctx = Ctx(nc)
    f = L1FFN(nc, ctx, io, totals_only=totals_only)
    f.prologue()
    for ci in range(4):
        f.chunk(ci)
    f.finish()
    nc.finalize()
    return nc


_CACHE = {}


def _programs():
    if "p" not in _CACHE:
        _CACHE["p"] = (
            build_l0_launch(),
            build_l0ffn_launch(),
            build_l1attn_launch(),
            build_l1ffn_launch(totals_only=True),
            build_l1ffn_launch(totals_only=False),
        )
    return _CACHE["p"]


def kernel(**inp):
    n_cores = 8
    consts = host_consts()
    ident = consts["c_ident"]
    e2 = consts["c_e2"]
    masks = l1_masks()
    x = np.ascontiguousarray(inp["L0_attn_x"])
    p_l0a, p_l0f, p_l1a, p_l1t, p_l1m = _programs()

    maps = []
    for c in range(n_cores):
        m = l0_in_map(x[2 * c:2 * c + 2], inp, consts)
        maps.append(m)
    res = run_bass_kernel_spmd(p_l0a, maps, core_ids=list(range(n_cores)))
    x1 = [np.asarray(r["x_out"]) for r in res.results]

    maps = [{"x_in": x1[c], "gate_w": inp["L0_ffn_gate_w"], "base_wgu": inp["L0_ffn_base_wgu"],
             "base_wd": inp["L0_ffn_base_wd"], "exp_wgu": inp["L0_ffn_exp_wgu"],
             "exp_wd": inp["L0_ffn_exp_wd"], "c_ident": ident} for c in range(n_cores)]
    res = run_bass_kernel_spmd(p_l0f, maps, core_ids=list(range(n_cores)))
    x2 = [np.asarray(r["x_out"]) for r in res.results]

    maps = [{"x_in": x2[c], "wq": inp["L1_attn_wq"], "wk_down": inp["L1_attn_wk_down"],
             "wv_down": inp["L1_attn_wv_down"], "wk_up": inp["L1_attn_wk_up"],
             "wv_up": inp["L1_attn_wv_up"], "wo": inp["L1_attn_wo"],
             "c_ident": ident, "c_e2": e2, "c_masks": masks} for c in range(n_cores)]
    res = run_bass_kernel_spmd(p_l1a, maps, core_ids=list(range(n_cores)))
    x3 = [np.asarray(r["x_out"]) for r in res.results]

    maps = [{"x_in": x3[c], "Wr": inp["L1_ffn_Wr"], "c_ident": ident} for c in range(n_cores)]
    res = run_bass_kernel_spmd(p_l1t, maps, core_ids=list(range(n_cores)))
    all_totals = np.stack([np.asarray(r["totals"]) for r in res.results]).astype(np.float32)

    maps = []
    for c in range(n_cores):
        sel = np.zeros((8, 1), np.float32)
        sel[:c] = 1.0
        maps.append({"x_in": x3[c], "Wr": inp["L1_ffn_Wr"], "W1": inp["L1_ffn_W1"],
                     "W2": inp["L1_ffn_W2"], "c_ident": ident, "c_sel": sel,
                     "all_totals": all_totals})
    res = run_bass_kernel_spmd(p_l1m, maps, core_ids=list(range(n_cores)))
    x4 = np.concatenate([np.asarray(r["x_out"]) for r in res.results], axis=0)
    return x4.astype(np.float32)


# version 07 (final)
# speedup vs baseline: 1.5564x; 1.5564x over previous
import os
DBG_SKIP = set(os.environ.get("KDEV_SKIP", "").split(","))
import numpy as np
import ml_dtypes
import concourse.bass as bass
import concourse.mybir as mybir
from concourse.alu_op_type import AluOpType as OP
from concourse.bass_utils import run_bass_kernel_spmd
from contextlib import ExitStack
from concourse.replica_groups import maybe_share_collective_output_space

F32 = mybir.dt.float32
BF16 = mybir.dt.bfloat16
AF = mybir.ActivationFunctionType
AX = mybir.AxisListType.X

N = 2048
C = 1024
H = 8
NT = N // 128
NC = C // 128
LAMBDA_INIT = 0.5


def host_consts():
    d = {}
    d["c_ident"] = np.eye(128, dtype=np.float32)
    e2 = np.zeros((2, 128), np.float32); e2[0] = 1.0
    d["c_e2"] = e2
    op_ = np.zeros((128, 2), np.float32)
    op_[:64, 0] = 1.0; op_[64:, 1] = 1.0
    d["c_onespair"] = op_
    d["c_opT"] = np.ascontiguousarray(op_.T)
    d["c_w128"] = np.full((128, 1), 1.0 / 128, np.float32)
    d["c_sign"] = np.concatenate([np.ones(64), -np.ones(64)]).astype(np.float32)[:, None]
    return d


def pin_table(a, idx):
    a.add_instruction(mybir.InstLoadActFuncSet(
        name=a.bass.get_next_instruction_name(), act_func_set_id=idx, ins=[], outs=[]))


def mk_pools(nc):
    P = [nc.alloc_psum_tensor(f"pool{i}", [128, 512], F32) for i in range(7)]
    PT = nc.alloc_psum_tensor("poolT", [128, 256], BF16)
    return P, PT


class Ctx:
    def __init__(self, nc):
        self.nc = nc
        self._sems = {}
        self.cnt = {}

    def sem(self, name):
        if name not in self._sems:
            self._sems[name] = self.nc.alloc_semaphore(name)
            self.cnt[name] = 0
        return self._sems[name]

    def inc(self, instr, name, n=1):
        instr.then_inc(self.sem(name), n)
        self.cnt[name] += n

    def wait_all(self, eng, name):
        eng.wait_ge(self.sem(name), self.cnt[name])


class L0Attn:
    def __init__(self, nc, ctx, io, dbg=(), pools=None):
        self.nc, self.ctx, self.io = nc, ctx, io
        self.dbg = set(dbg)
        self._es = ExitStack()
        A = lambda n, s, d: self._es.enter_context(nc.sbuf_tensor(n, s, d))
        self.xstage = A("xstage", [128, 2 * C], BF16)
        self.xT = A("xT", [128, NC * N], BF16)
        self.qcat = A("qcat", [128, H * N], BF16)
        self.kcat = A("kcat", [128, H * N], BF16)
        self.vsl = A("vsl", [128, NT * H * 129 + 1], BF16)
        self.ybuf = A("ybuf", [128, H * N], BF16)
        self.wsl = A("wsl", [128, NC * 128], BF16)
        self.wos_all = self.vsl
        self.wj = A("wj", [128, 1024], BF16)
        self.wt_stage = A("wt_stage", [128, 1024], BF16)
        self.ident_b = A("ident_b", [128, 128], BF16)
        self.e2_f = A("e2_f", [2, 128], F32)
        self.e2_b = A("e2_b", [2, 128], BF16)
        self.opT_f = A("opT_f", [2, 128], F32)
        self.onespair_b = A("onespair_b", [128, 2], BF16)
        self.w128_b = A("w128_b", [128, 1], BF16)
        self.g_sb = A("g_sb", [128, 256], BF16)
        self.s1_sb = A("s1_sb", [128, 256], F32)
        self.s2_sb = A("s2_sb", [128, 256], F32)
        self.e1_sb = A("e1_sb", [128, 256], BF16)
        self.e2sb = A("e2sb", [128, 256], BF16)
        self.sq_sb = A("sq_sb", [128, 512], BF16)
        self.ssf = A("ssf", [2, 512], F32)
        self.rQ = A("rQ", [2, 512], F32)
        self.bo2 = A("bo2", [2, C], BF16)
        self.xout_sb2 = A("xout_sb2", [128, C], F32)
        self.qd_sb = A("qd_sb", [128, 256], BF16)
        self.tmpa = A("tmpa", [128, 256], F32)
        self.tmpb = A("tmpb", [128, 256], F32)
        self.r1st = A("r1st", [2, 256], F32)
        self.r2st = A("r2st", [2, 256], F32)
        self.rb1 = A("rb1", [128, 256], F32)
        self.rb2 = A("rb2", [128, 256], F32)
        self.rstd2 = A("rstd2", [2, 512], F32)
        self.lam_neg = A("lam_neg", [1, 1], F32)
        self.lrow1 = A("lrow1", [1, 256], F32)
        self.lprod1 = A("lprod1", [1, 128], F32)
        self.lsum2 = A("lsum2", [1, 2], F32)
        self.lexp2 = A("lexp2", [1, 2], F32)
        self.csrow = A("csrow", [1, 8], F32)
        self.dgrow = A("dgrow", [1, 8], F32)
        self.chdrow = A("chdrow", [1, 8], F32)
        self.craw = A("craw", [1, 1], F32)
        self.stage28 = A("stage28", [2, 8], F32)
        self.cs_bc = A("cs_bc", [128, 8], F32)
        self.dgs_bc = A("dgs_bc", [128, 8], F32)
        self.sign_col = A("sign_col", [128, 1], F32)
        self.scal1 = A("scal1", [1, 8], F32)
        self.scal2 = A("scal2", [1, 8], F32)
        self.subln_c = A("subln_c", [128, 1], F32)
        self.tsb = A("tsb", [128, C], F32)
        self.bn6 = A("bn6", [128, 12], F32)
        self.mv2 = A("mv2", [128, 2], F32)
        self.lnr = A("lnr", [128, 1], F32)
        self.lnm = A("lnm", [128, 1], F32)
        self.rb_sb = self.tsb
        self.xout_sb = self.xout_sb2
        self.sqy = self.sq_sb[:, 0:512]
        self.blkstage = A("blkstage", [128, 128], BF16)
        self.eps_col = A("eps_col", [128, 1], F32)
        self.ones_col_b = A("ones_col_b", [128, 1], BF16)

        self.fencea = A("fencea", [128, 8], F32)
        self.fenceb = A("fenceb", [128, 8], F32)
        self.craw2 = A("craw2", [1, 1], F32)
        self.P, self.PT = pools if pools is not None else mk_pools(nc)

    def vf(self, v):
        self.ctx.inc(v.tensor_copy(self.fenceb[:], self.fencea[:]), "vS")

    def af(self, a):
        self.ctx.inc(a.activation(self.fenceb[:], self.fencea[:], AF.Identity), "aS")

    def p0_consts(self):
        nc, ctx, io = self.nc, self.ctx, self.io
        with nc.Block("p0c") as blk:
            def g_(g):
                for dst, src in [(self.ident_b[:], io["c_ident"]), (self.e2_b[:], io["c_e2"]),
                                 (self.onespair_b[:], io["c_onespair"]), (self.w128_b[:], io["c_w128"])]:
                    ctx.inc(g.dma_start(dst, src), "ld", 16)
            blk.gpsimd(g_)

            def s_(s):
                for dst, src in [(self.e2_f[:], io["c_e2"]), (self.opT_f[:], io["c_opT"]),
                                 (self.lrow1[0:1, 0:64], io["lambda_q1"]),
                                 (self.lrow1[0:1, 64:128], io["lambda_k1"]), (self.lrow1[0:1, 128:192], io["lambda_q2"]),
                                 (self.lrow1[0:1, 192:256], io["lambda_k2"]), (self.craw[:], io["cos_raw"]),
                                 (self.chdrow[:], io["cos_head_delta"]), (self.dgrow[:], io["delta_gain"]),
                                 (self.sign_col[:], io["c_sign"]), (self.subln_c[:], io["subln"])]:
                    ctx.inc(s.dma_start(dst, src), "ld", 16)
            blk.sync(s_)

            def v_(v):
                ctx.wait_all(v, "ld")
                for t in [self.rQ, self.r1st, self.r2st, self.rstd2, self.stage28]:
                    v.memset(t[:], 0.0)
                v.memset(self.eps_col[:], 1e-5)
                v.memset(self.bo2[:], 0.0)
                ctx.inc(v.memset(self.ones_col_b[:], 1.0), "vS")
                v.tensor_tensor(self.lprod1[0:1, 0:64], self.lrow1[0:1, 0:64], self.lrow1[0:1, 64:128], OP.mult)
                v.tensor_tensor(self.lprod1[0:1, 64:128], self.lrow1[0:1, 128:192], self.lrow1[0:1, 192:256], OP.mult)
                v.reduce_sum(self.lsum2[0:1, 0:1], self.lprod1[0:1, 0:64], axis=AX)
                v.reduce_sum(self.lsum2[0:1, 1:2], self.lprod1[0:1, 64:128], axis=AX)
                v.reduce_sum(self.scal1[:, 0:1], self.chdrow[:], axis=AX)
                self.vf(v)
                self.vf(v)
                v.tensor_scalar(self.scal1[:, 0:1], self.scal1[:, 0:1], 1.0 / H, None, OP.mult)
                self.vf(v)
                ctx.inc(v.tensor_copy(self.fencea[:], self.fenceb[:]), "vS")
            blk.vector(v_)

            def gbo_(g):
                ctx.wait_all(g, "vS")
                ctx.inc(g.dma_start(self.bo2[0:1, :], io["bo"]), "ld", 16)
            blk.gpsimd(gbo_)

            def a_(a):
                pin_table(a, 0)
                ctx.wait_all(a, "vS")
                a.activation(self.lexp2[:], self.lsum2[:], AF.Exp)
                a.activation(self.craw[:], self.craw[:], AF.Exp, scale=-1.0)
                self.af(a)
                self.af(a)
            blk.scalar(a_)



            def v2_(v):
                ctx.wait_all(v, "aS")
                v.tensor_tensor(self.lam_neg[:], self.lexp2[0:1, 1:2], self.lexp2[0:1, 0:1], OP.subtract)
                self.vf(v)
                v.tensor_scalar(self.lam_neg[:], self.lam_neg[:], LAMBDA_INIT, None, OP.subtract)
                v.tensor_scalar(self.craw[:], self.craw[:], 1.0, None, OP.add)
                self.vf(v)
                v.reciprocal(self.craw2[:], self.craw[:])
                self.vf(v)
                self.vf(v)
                v.tensor_scalar(self.craw[:], self.craw2[:], 15.0, None, OP.mult)
                v.tensor_scalar(self.scal2[:], self.chdrow[:], self.scal1[:, 0:1], None, OP.subtract)
                self.vf(v)
                ctx.inc(v.tensor_copy(self.fencea[:], self.fenceb[:]), "vS")
            blk.vector(v2_)

            def a2_(a):
                ctx.wait_all(a, "vS")
                a.activation(self.scal2[:], self.scal2[:], AF.Tanh)
                self.af(a)
                self.af(a)
            blk.scalar(a2_)

            def v3_(v):
                ctx.wait_all(v, "aS")
                v.tensor_scalar(self.scal2[:], self.scal2[:], 0.5, 1.0, OP.mult, OP.add)
                self.vf(v)
                v.tensor_scalar(self.csrow[:], self.scal2[:], self.craw[0:1, 0:1], None, OP.mult)
                self.vf(v)
                v.tensor_copy(self.stage28[0:1, :], self.csrow[:])
                self.vf(v)
                ctx.inc(v.tensor_copy(self.fencea[:], self.fenceb[:]), "vS")
            blk.vector(v3_)

            def t_(t):
                ctx.wait_all(t, "vS")
                ctx.inc(t.matmul(self.P[0][:, 0:8], self.e2_f[:], self.stage28[:], start=True, stop=True), "tS")
            blk.tensor(t_)

            def v4_(v):
                ctx.wait_all(v, "tS")
                v.tensor_copy(self.cs_bc[:], self.P[0][:, 0:8])
                v.tensor_copy(self.stage28[0:1, :], self.dgrow[:])
                self.vf(v)
                ctx.inc(v.tensor_copy(self.fencea[:], self.fenceb[:]), "vS")
            blk.vector(v4_)

            def t2_(t):
                ctx.wait_all(t, "vS")
                ctx.inc(t.matmul(self.P[1][:, 0:8], self.e2_f[:], self.stage28[:], start=True, stop=True), "tS")
            blk.tensor(t2_)

            def v5_(v):
                ctx.wait_all(v, "tS")
                ctx.inc(v.tensor_scalar(self.dgs_bc[:], self.P[1][:, 0:8], self.sign_col[:], None, OP.mult), "vS")
            blk.vector(v5_)

    def p0_weights(self):
        nc, ctx, io = self.nc, self.ctx, self.io
        scr = io["wT_scr"]
        scro = io["woT_scr"]
        with nc.Block("p0w") as blk:
            for wi, wname in enumerate(["Wq", "Wk", "Wv"]):
                for jt in range(8):
                    def g_(g, wname=wname, jt=jt):
                        ctx.wait_all(g, "st")
                        ctx.inc(g.dma_start(self.wj[:], io[wname][jt * 128:(jt + 1) * 128, :]), "ld", 16)
                    blk.gpsimd(g_)
                    ld_t = ctx.cnt["ld"]
                    for cc in range(8):
                        slot = cc % 2
                        def t_(t, cc=cc, slot=slot, ld_t=ld_t):
                            t.wait_ge(ctx.sem("ld"), ld_t)
                            ctx.wait_all(t, "vS")
                            ctx.inc(t.transpose(self.PT[:, slot * 128:slot * 128 + 128],
                                                self.wj[:, cc * 128:(cc + 1) * 128], self.ident_b[:]), "tS")
                        blk.tensor(t_)

                        def v_(v, cc=cc, slot=slot):
                            ctx.wait_all(v, "tS")
                            if cc == 0:
                                ctx.wait_all(v, "st")
                            ctx.inc(v.tensor_copy(self.wt_stage[:, cc * 128:(cc + 1) * 128],
                                                  self.PT[:, slot * 128:slot * 128 + 128]), "vS")
                        blk.vector(v_)

                    def vfx_(v):
                        self.vf(v)
                    blk.vector(vfx_)

                    def s_(s, wi=wi, jt=jt):
                        ctx.wait_all(s, "vS")
                        ctx.inc(s.dma_start(scr[wi, jt], self.wt_stage[:]), "st", 16)
                    blk.sync(s_)

            for ct in range(8):
                def g2_(g, ct=ct):
                    ctx.wait_all(g, "st")
                    ctx.inc(g.dma_start(self.wj[:], io["Wo"][ct * 128:(ct + 1) * 128, :]), "ld", 16)
                blk.gpsimd(g2_)
                ld_t = ctx.cnt["ld"]
                for jj in range(8):
                    slot = jj % 2
                    def t3_(t, jj=jj, slot=slot, ld_t=ld_t):
                        t.wait_ge(ctx.sem("ld"), ld_t)
                        ctx.wait_all(t, "vS")
                        ctx.inc(t.transpose(self.PT[:, slot * 128:slot * 128 + 128],
                                            self.wj[:, jj * 128:(jj + 1) * 128], self.ident_b[:]), "tS")
                    blk.tensor(t3_)

                    def v6_(v, jj=jj, slot=slot):
                        ctx.wait_all(v, "tS")
                        ctx.wait_all(v, "st")
                        ctx.inc(v.tensor_scalar(self.blkstage[:], self.PT[:, slot * 128:slot * 128 + 128],
                                                self.subln_c[:], 0.5, OP.mult, OP.mult), "vS")
                    blk.vector(v6_)

                    def vfy_(v):
                        self.vf(v)
                    blk.vector(vfy_)

                    def s2_(s, jj=jj, ct=ct):
                        ctx.wait_all(s, "vS")
                        ctx.inc(s.dma_start(scro[jj, ct], self.blkstage[:]), "st", 16)
                    blk.sync(s2_)


    def phase_a(self, b):
        nc, ctx, io = self.nc, self.ctx, self.io
        with nc.Block(f"pa{b}") as blk:
            tpose_done = []
            for t in range(NT):
                xslot = t % 2
                wait_t = tpose_done[t - 2] if t >= 2 else 0
                def g_(g, b=b, t=t, xslot=xslot, wait_t=wait_t):
                    if wait_t:
                        g.wait_ge(ctx.sem("tS"), wait_t)
                    else:
                        ctx.wait_all(g, "st")
                    ctx.inc(g.dma_start(self.xstage[:, xslot * C:(xslot + 1) * C],
                                        io["x_in"][b, t * 128:(t + 1) * 128, :]), "ld", 16)
                blk.gpsimd(g_)
                ld_t = ctx.cnt["ld"]
                for cc in range(NC):
                    slot = (t * NC + cc) % 2
                    def t_(tt, t=t, cc=cc, slot=slot, xslot=xslot, ld_t=ld_t):
                        tt.wait_ge(ctx.sem("ld"), ld_t)
                        ctx.wait_all(tt, "vS")
                        ctx.inc(tt.transpose(self.PT[:, slot * 128:slot * 128 + 128],
                                             self.xstage[:, xslot * C + cc * 128: xslot * C + (cc + 1) * 128],
                                             self.ident_b[:]), "tS")
                    blk.tensor(t_)

                    def v_(v, t=t, cc=cc, slot=slot):
                        ctx.wait_all(v, "tS")
                        ctx.inc(v.tensor_copy(self.xT[:, cc * N + t * 128: cc * N + (t + 1) * 128],
                                              self.PT[:, slot * 128:slot * 128 + 128]), "vS")
                    blk.vector(v_)
                tpose_done.append(ctx.cnt["tS"])

    def phase_b(self, b):
        if "b" in DBG_SKIP:
            return
        nc, ctx, io = self.nc, self.ctx, self.io
        with nc.Block(f"pb{b}") as blk:
            for h in range(H):
                for side in range(2):
                    def gw_(g, side=side, h=h):
                        ctx.wait_all(g, "vS")
                        ctx.wait_all(g, "tS")
                        ctx.inc(g.dma_start(self.wsl[:], io["wT_scr"][side, h]), "ld", 16)
                    blk.gpsimd(gw_)
                    ld_t = ctx.cnt["ld"]
                    for ch in range(4):
                        pq = self.P[ch % 2]
                        def t_(t, h=h, ch=ch, pq=pq, ld_t=ld_t):
                            t.wait_ge(ctx.sem("ld"), ld_t)
                            ctx.wait_all(t, "vS")
                            for cc in range(NC):
                                ctx.inc(t.matmul(pq[:], self.wsl[:, cc * 128:(cc + 1) * 128],
                                                 self.xT[:, cc * N + ch * 512: cc * N + (ch + 1) * 512],
                                                 start=(cc == 0), stop=(cc == NC - 1)), "tS")
                        blk.tensor(t_)
                        tq = ctx.cnt["tS"]

                        def a_(a, pq=pq, tq=tq, pin=(h == 0 and side == 0 and ch == 0)):
                            if pin:
                                pin_table(a, 6)
                            a.wait_ge(ctx.sem("tS"), tq)
                            ctx.inc(a.activation(self.sq_sb[:], pq[:], AF.Square), "aS")
                        blk.scalar(a_)
                        asq = ctx.cnt["aS"]

                        def t2_(t, asq=asq):
                            t.wait_ge(ctx.sem("aS"), asq)
                            ctx.inc(t.matmul(self.P[2][0:2, 0:512], self.onespair_b[:], self.sq_sb[:],
                                             start=True, stop=True), "tS")
                        blk.tensor(t2_)
                        tss = ctx.cnt["tS"]

                        def v_(v, tss=tss):
                            v.wait_ge(ctx.sem("tS"), tss)
                            ctx.inc(v.tensor_scalar(self.ssf[:], self.P[2][0:2, 0:512], 1e-12, None, OP.max), "vS")
                        blk.vector(v_)
                        vss = ctx.cnt["vS"]

                        def a2_(a, vss=vss):
                            a.wait_ge(ctx.sem("vS"), vss)
                            a.activation(self.ssf[:], self.ssf[:], AF.Ln)
                            a.activation(self.rQ[0:2, :], self.ssf[:], AF.Exp, scale=-0.5)
                            self.af(a)
                        blk.scalar(a2_)
                        ar = ctx.cnt["aS"]

                        def t3_(t, ar=ar):
                            t.wait_ge(ctx.sem("aS"), ar)
                            ctx.inc(t.matmul(self.P[3][:, 0:512], self.opT_f[:], self.rQ[:],
                                             start=True, stop=True), "tS")
                        blk.tensor(t3_)
                        trb = ctx.cnt["tS"]

                        def v2_(v, pq=pq, side=side, h=h, ch=ch, trb=trb):
                            v.wait_ge(ctx.sem("tS"), trb)
                            v.tensor_copy(self.rb_sb[:, 0:512], self.P[3][:, 0:512])
                            dst = (self.qcat if side == 0 else self.kcat)
                            sl = dst[:, h * N + ch * 512: h * N + (ch + 1) * 512]
                            if side == 0:
                                ctx.inc(v.scalar_tensor_tensor(sl, pq[:], self.cs_bc[:, h:h + 1], self.rb_sb[:, 0:512],
                                                               OP.mult, OP.mult), "vS")
                            else:
                                ctx.inc(v.tensor_tensor(sl, pq[:], self.rb_sb[:, 0:512], OP.mult), "vS")
                        blk.vector(v2_)

                def gv_(g, h=h):
                    ctx.wait_all(g, "vS")
                    ctx.wait_all(g, "tS")
                    ctx.inc(g.dma_start(self.wsl[:], io["wT_scr"][2, h]), "ld", 16)
                blk.gpsimd(gv_)
                ld_t = ctx.cnt["ld"]
                for tt_i in range(NT):
                    pv = self.P[4 + (tt_i % 2)]
                    def t4_(t, h=h, tt_i=tt_i, pv=pv, ld_t=ld_t):
                        t.wait_ge(ctx.sem("ld"), ld_t)
                        ctx.wait_all(t, "aS")
                        for cc in range(NC):
                            ctx.inc(t.matmul(pv[:, 0:128],
                                             self.xT[:, cc * N + tt_i * 128: cc * N + (tt_i + 1) * 128],
                                             self.wsl[:, cc * 128:(cc + 1) * 128],
                                             start=(cc == 0), stop=(cc == NC - 1)), "tS")
                    blk.tensor(t4_)
                    tv = ctx.cnt["tS"]

                    def a3_(a, h=h, tt_i=tt_i, pv=pv, tv=tv):
                        a.wait_ge(ctx.sem("tS"), tv)
                        base = (tt_i * H + h) * 129
                        src = pv[:, 0:128].rearrange("p (g d) -> p g d", g=2)
                        dst = self.vsl[:, base:base + 130].rearrange("p (g d) -> p g d", d=65)[:, :, 0:64]
                        ctx.inc(a.activation(dst, src, AF.Identity), "aS")
                    blk.scalar(a3_)

    def phase_c(self, b):
        if "c" in DBG_SKIP:
            return
        nc, ctx = self.nc, self.ctx
        with nc.Block(f"pc{b}") as blk:
            def v_ones(v):
                ctx.wait_all(v, "aS")
                view = self.vsl[:, 0:NT * H * 129].rearrange("p (s d) -> p s d", d=129)
                ctx.inc(v.memset(view[:, :, 64:65], 1.0), "vS")
            blk.vector(v_ones)

            for h in range(H):
                for qg in range(8):
                    qoff = qg * 256
                    def vqd(v, h=h, qoff=qoff):
                        ctx.wait_all(v, "aS")
                        v.tensor_scalar(self.qd_sb[:], self.qcat[:, h * N + qoff: h * N + qoff + 256],
                                        self.dgs_bc[:, h:h + 1], None, OP.mult)
                        self.vf(v)
                        ctx.inc(v.tensor_copy(self.fencea[:], self.fenceb[:]), "vS")
                    blk.vector(vqd)
                    vqd_cnt = ctx.cnt["vS"]

                    ya1 = self.P[3][0:65, 0:256]; yb1 = self.P[4][0:64, 0:256]
                    ya2 = self.P[5][0:65, 0:256]; yb2 = self.P[6][0:64, 0:256]
                    v_done = []
                    av_done = []
                    for kt in range(NT):
                        par = kt % 2
                        ps1 = self.P[0][:, par * 256:(par + 1) * 256]
                        ps2 = self.P[1][:, par * 256:(par + 1) * 256]
                        psd = self.P[2][:, par * 256:(par + 1) * 256]
                        koff = h * N + kt * 128
                        wait_v = vqd_cnt if kt < 2 else v_done[kt - 2]

                        def t_(t, h=h, qoff=qoff, koff=koff, ps1=ps1, ps2=ps2, psd=psd, wait_v=wait_v):
                            t.wait_ge(ctx.sem("vS"), wait_v)
                            ctx.inc(t.matmul(ps1, self.kcat[0:64, koff:koff + 128],
                                             self.qcat[0:64, h * N + qoff: h * N + qoff + 256],
                                             start=True, stop=True), "tS")
                            ctx.inc(t.matmul(ps2, self.kcat[64:128, koff:koff + 128],
                                             self.qcat[64:128, h * N + qoff: h * N + qoff + 256],
                                             start=True, stop=True), "tS")
                            ctx.inc(t.matmul(psd, self.kcat[:, koff:koff + 128], self.qd_sb[:],
                                             start=True, stop=True), "tS")
                        blk.tensor(t_)
                        tS_sd = ctx.cnt["tS"]

                        def a_(a, psd=psd, tS_sd=tS_sd, pin=(h == 0 and qg == 0 and kt == 0)):
                            if pin:
                                pin_table(a, 0)
                            a.wait_ge(ctx.sem("tS"), tS_sd)
                            ctx.inc(a.activation(self.g_sb[:], psd, AF.Tanh), "aS")
                        blk.scalar(a_)
                        aS_g = ctx.cnt["aS"]

                        def v_(v, ps1=ps1, ps2=ps2, aS_g=aS_g):
                            v.wait_ge(ctx.sem("aS"), aS_g)
                            ctx.inc(v.tensor_tensor(self.s1_sb[:], ps1, self.g_sb[:], OP.add), "vS")
                            ctx.inc(v.tensor_tensor(self.s2_sb[:], ps2, self.g_sb[:], OP.subtract), "vS")
                        blk.vector(v_)
                        v_done.append(ctx.cnt["vS"])
                        vS_s = ctx.cnt["vS"]
                        wait_t_av = av_done[kt - 1] if kt >= 1 else 0

                        def a2_(a, vS_s=vS_s, wait_t_av=wait_t_av):
                            a.wait_ge(ctx.sem("vS"), vS_s)
                            if wait_t_av:
                                a.wait_ge(ctx.sem("tS"), wait_t_av)
                            ctx.inc(a.activation(self.e1_sb[:], self.s1_sb[:], AF.Exp), "aS")
                            ctx.inc(a.activation(self.e2sb[:], self.s2_sb[:], AF.Exp), "aS")
                        blk.scalar(a2_)
                        aS_e = ctx.cnt["aS"]

                        def t2_(t, h=h, kt=kt, ya1=ya1, yb1=yb1, ya2=ya2, yb2=yb2, aS_e=aS_e):
                            t.wait_ge(ctx.sem("aS"), aS_e)
                            base = (kt * H + h) * 129
                            st_ = (kt == 0); sp = (kt == NT - 1)
                            ctx.inc(t.matmul(ya1, self.vsl[:, base:base + 65], self.e1_sb[:], start=st_, stop=sp), "tS")
                            ctx.inc(t.matmul(yb1, self.vsl[:, base + 65:base + 129], self.e1_sb[:], start=st_, stop=sp), "tS")
                            ctx.inc(t.matmul(ya2, self.vsl[:, base:base + 65], self.e2sb[:], start=st_, stop=sp), "tS")
                            ctx.inc(t.matmul(yb2, self.vsl[:, base + 65:base + 129], self.e2sb[:], start=st_, stop=sp), "tS")
                        blk.tensor(t2_)
                        av_done.append(ctx.cnt["tS"])

                    tS_av = ctx.cnt["tS"]

                    def ve_(v, ya1=ya1, ya2=ya2, tS_av=tS_av):
                        v.wait_ge(ctx.sem("tS"), tS_av)
                        v.reciprocal(self.r1st[0:1, :], ya1[64:65, :])
                        v.reciprocal(self.r2st[0:1, :], ya2[64:65, :])
                        self.vf(v)
                        self.vf(v)
                        v.tensor_scalar(self.r2st[0:1, :], self.r2st[0:1, :],
                                        self.lam_neg[0:1, 0:1], None, OP.mult)
                        self.vf(v)
                        ctx.inc(v.tensor_copy(self.fencea[:], self.fenceb[:]), "vS")
                    blk.vector(ve_)
                    vS_r = ctx.cnt["vS"]

                    def te_(t, vS_r=vS_r):
                        t.wait_ge(ctx.sem("vS"), vS_r)
                        ctx.inc(t.matmul(self.P[2][:, 0:256], self.e2_f[:], self.r1st[:], start=True, stop=True), "tS")
                        ctx.inc(t.matmul(self.P[2][:, 256:512], self.e2_f[:], self.r2st[:], start=True, stop=True), "tS")
                    blk.tensor(te_)
                    tS_b = ctx.cnt["tS"]

                    def ve2_(v, h=h, qoff=qoff, ya1=ya1, yb1=yb1, ya2=ya2, yb2=yb2, tS_b=tS_b):
                        v.wait_ge(ctx.sem("tS"), tS_b)
                        v.tensor_copy(self.rb1[:], self.P[2][:, 0:256])
                        v.tensor_copy(self.rb2[:], self.P[2][:, 256:512])
                        v.tensor_tensor(self.tmpa[0:64, :], ya1[0:64, :], self.rb1[0:64, :], OP.mult)
                        v.tensor_tensor(self.tmpb[0:64, :], ya2[0:64, :], self.rb2[0:64, :], OP.mult)
                        ctx.inc(v.tensor_tensor(self.ybuf[0:64, h * N + qoff:h * N + qoff + 256],
                                                self.tmpa[0:64, :], self.tmpb[0:64, :], OP.add), "vS")
                        v.tensor_tensor(self.tmpa[64:128, :], yb1[0:64, :], self.rb1[64:128, :], OP.mult)
                        v.tensor_tensor(self.tmpb[64:128, :], yb2[0:64, :], self.rb2[64:128, :], OP.mult)
                        ctx.inc(v.tensor_tensor(self.ybuf[64:128, h * N + qoff:h * N + qoff + 256],
                                                self.tmpa[64:128, :], self.tmpb[64:128, :], OP.add), "vS")
                    blk.vector(ve2_)

    def phase_d(self, b):
        if "d" in DBG_SKIP:
            return
        nc, ctx, io = self.nc, self.ctx, self.io
        scro = io["woT_scr"]
        with nc.Block(f"pd{b}") as blk:
            bo_ld = 0
            def g3_(g):
                ctx.wait_all(g, "tS")
                ctx.wait_all(g, "vS")
                for h in range(H):
                    ctx.inc(g.dma_start(self.wos_all[:, h * C:(h + 1) * C].rearrange("p (ct c) -> p ct c", c=128),
                                        scro[h].rearrange("ct p c -> p ct c")), "ld", 16)
            blk.gpsimd(g3_)
            wos_ld = ctx.cnt["ld"]
            for h in range(H):
                for ch in range(4):
                    off = h * N + ch * 512
                    def a_(a, off=off, pin=(h == 0 and ch == 0)):
                        if pin:
                            pin_table(a, 6)
                        ctx.wait_all(a, "vS")
                        ctx.wait_all(a, "tS")
                        ctx.inc(a.activation(self.sqy, self.ybuf[:, off:off + 512], AF.Square), "aS")
                    blk.scalar(a_)
                    aS_sq = ctx.cnt["aS"]

                    def t_(t, aS_sq=aS_sq):
                        t.wait_ge(ctx.sem("aS"), aS_sq)
                        ctx.inc(t.matmul(self.P[0][0:1, 0:512], self.w128_b[:], self.sqy, start=True, stop=True), "tS")
                    blk.tensor(t_)
                    tS_ms = ctx.cnt["tS"]

                    def a2_(a, tS_ms=tS_ms):
                        a.wait_ge(ctx.sem("tS"), tS_ms)
                        a.activation(self.ssf[0:1, :], self.P[0][0:1, 0:512], AF.Ln, bias=self.eps_col[0:1, :])
                        a.activation(self.rstd2[0:1, :], self.ssf[0:1, :], AF.Exp, scale=-0.5)
                        self.af(a)
                    blk.scalar(a2_)
                    aS_r = ctx.cnt["aS"]

                    def t2_(t, aS_r=aS_r):
                        t.wait_ge(ctx.sem("aS"), aS_r)
                        ctx.inc(t.matmul(self.P[1][:, 0:512], self.e2_f[:], self.rstd2[:], start=True, stop=True), "tS")
                    blk.tensor(t2_)
                    tS_rb = ctx.cnt["tS"]

                    def v_(v, off=off, tS_rb=tS_rb):
                        v.wait_ge(ctx.sem("tS"), tS_rb)
                        v.tensor_copy(self.rb_sb[:, 0:512], self.P[1][:, 0:512])
                        ctx.inc(v.tensor_tensor(self.qcat[:, off:off + 512], self.ybuf[:, off:off + 512],
                                                self.rb_sb[:, 0:512], OP.mult), "vS")
                    blk.vector(v_)

            res_used = []
            for tt_i in range(NT):
                po = None
                xslot = tt_i % 2
                wait_res = res_used[tt_i - 2] if tt_i >= 2 else 0
                def gx_(g, b=b, tt_i=tt_i, xslot=xslot, wait_res=wait_res):
                    if wait_res:
                        g.wait_ge(ctx.sem("vS"), wait_res)
                    ctx.inc(g.dma_start(self.xstage[:, xslot * C:(xslot + 1) * C],
                                        io["x_in"][b, tt_i * 128:(tt_i + 1) * 128, :]), "ld", 16)
                blk.gpsimd(gx_)
                ld_res = ctx.cnt["ld"]
                vS_pre = ctx.cnt["vS"]
                aS_pre = ctx.cnt["aS"]

                def t3_(t, tt_i=tt_i, vS_pre=vS_pre, aS_pre=aS_pre, wos_ld=wos_ld, bo_ld=bo_ld):
                    t.wait_ge(ctx.sem("ld"), max(wos_ld, bo_ld))
                    t.wait_ge(ctx.sem("vS"), vS_pre)
                    t.wait_ge(ctx.sem("aS"), aS_pre)
                    for cc2 in range(2):
                        po = self.P[2 + cc2]
                        for h in range(H):
                            ctx.inc(t.matmul(po[:],
                                             self.qcat[:, h * N + tt_i * 128: h * N + (tt_i + 1) * 128],
                                             self.wos_all[:, h * C + cc2 * 512: h * C + (cc2 + 1) * 512],
                                             start=(h == 0), stop=False), "tS")
                        ctx.inc(t.matmul(po[:], self.e2_b[:],
                                         self.bo2[:, cc2 * 512:(cc2 + 1) * 512],
                                         start=False, stop=True), "tS")
                blk.tensor(t3_)
                tS_o = ctx.cnt["tS"]

                def v2_(v, tt_i=tt_i, tS_o=tS_o, xslot=xslot, ld_res=ld_res):
                    v.wait_ge(ctx.sem("tS"), tS_o)
                    v.wait_ge(ctx.sem("ld"), ld_res)
                    ctx.wait_all(v, "st")
                    v.tensor_tensor(self.tsb[:, 0:512], self.P[2][:],
                                    self.xstage[:, xslot * C: xslot * C + 512], OP.add)
                    v.tensor_tensor(self.tsb[:, 512:1024], self.P[3][:],
                                    self.xstage[:, xslot * C + 512: (xslot + 1) * C], OP.add)
                    v.bn_stats(self.bn6[:, 0:6], self.tsb[:, 0:512])
                    v.bn_stats(self.bn6[:, 6:12], self.tsb[:, 512:1024])
                    self.vf(v)
                    self.vf(v)
                    v.bn_aggr(self.mv2[:], self.bn6[:])
                    self.vf(v)
                    self.vf(v)
                    v.tensor_scalar(self.lnm[:], self.mv2[:, 0:1], -1.0, None, OP.mult)
                    self.vf(v)
                    ctx.inc(v.tensor_copy(self.fencea[:], self.fenceb[:]), "vS")
                blk.vector(v2_)
                vS_ln = ctx.cnt["vS"]

                def sdump_(s, b=b, tt_i=tt_i, vS_ln=vS_ln):
                    if "dbg_preln" in self.io and b == 0:
                        s.wait_ge(ctx.sem("vS"), vS_ln)
                        ctx.inc(s.dma_start(self.io["dbg_preln"][tt_i * 128:(tt_i + 1) * 128, :], self.tsb[:]), "st", 16)
                blk.sync(sdump_)

                def a3_(a, vS_ln=vS_ln, pin=(tt_i == 0)):
                    if pin:
                        pin_table(a, 6)
                    a.wait_ge(ctx.sem("vS"), vS_ln)
                    a.activation(self.lnr[:], self.mv2[:, 1:2], AF.Ln, bias=self.eps_col[:])
                    self.af(a)
                    a.activation(self.lnr[:], self.lnr[:], AF.Exp, scale=-0.5)
                    self.af(a)
                    self.af(a)
                blk.scalar(a3_)
                aS_ln = ctx.cnt["aS"]

                def v3_(v, aS_ln=aS_ln):
                    v.wait_ge(ctx.sem("aS"), aS_ln)
                    v.tensor_tensor(self.lnm[:], self.lnm[:], self.lnr[:], OP.mult)
                    self.vf(v)
                    ctx.inc(v.tensor_copy(self.fencea[:], self.fenceb[:]), "vS")
                blk.vector(v3_)
                vS_nm = ctx.cnt["vS"]

                def a4_(a, vS_nm=vS_nm):
                    a.wait_ge(ctx.sem("vS"), vS_nm)
                    ctx.inc(a.activation(self.xout_sb[:], self.tsb[:], AF.Identity,
                                         bias=self.lnm[:], scale=self.lnr[:]), "aS")
                blk.scalar(a4_)
                aS_x = ctx.cnt["aS"]

                def s_(s, b=b, tt_i=tt_i, aS_x=aS_x):
                    s.wait_ge(ctx.sem("aS"), aS_x)
                    ctx.inc(s.dma_start(io["x_out"][b, tt_i * 128:(tt_i + 1) * 128, :], self.xout_sb[:]), "st", 16)
                blk.sync(s_)
                res_used.append(ctx.cnt["vS"])

    def close(self):
        self._es.close()

    def dump(self, name, sb):
        nc, ctx, io = self.nc, self.ctx, self.io
        with nc.Block(f"dump_{name}") as blk:
            def s_(g):
                ctx.wait_all(g, "vS")
                ctx.wait_all(g, "aS")
                ctx.wait_all(g, "tS")
                ctx.inc(g.dma_start(io[name], sb[:]), "st", 16)
            blk.gpsimd(s_)

    def finish(self):
        nc, ctx = self.nc, self.ctx
        with nc.Block("fin") as blk:
            def g_(g):
                ctx.wait_all(g, "st")
            blk.gpsimd(g_)


def build_l0_launch(dbg=()):
    nc = bass.Bass("TRN2", target_bir_lowering=False)
    io = {}
    io["x_in"] = nc.dram_tensor("x_in", [2, N, C], F32, kind="ExternalInput").ap()
    io["x_out"] = nc.dram_tensor("x_out", [2, N, C], F32, kind="ExternalOutput").ap()
    for w in ["Wq", "Wk", "Wv", "Wo"]:
        io[w] = nc.dram_tensor(w, [C, C], F32, kind="ExternalInput").ap()
    io["bo"] = nc.dram_tensor("bo", [1, C], F32, kind="ExternalInput").ap()
    for nm in ["lambda_q1", "lambda_k1", "lambda_q2", "lambda_k2"]:
        io[nm] = nc.dram_tensor(nm, [1, 64], F32, kind="ExternalInput").ap()
    io["delta_gain"] = nc.dram_tensor("delta_gain", [1, 8], F32, kind="ExternalInput").ap()
    io["cos_head_delta"] = nc.dram_tensor("cos_head_delta", [1, 8], F32, kind="ExternalInput").ap()
    io["cos_raw"] = nc.dram_tensor("cos_raw", [1, 1], F32, kind="ExternalInput").ap()
    io["subln"] = nc.dram_tensor("subln", [128, 1], F32, kind="ExternalInput").ap()
    for k, v in host_consts().items():
        io[k] = nc.dram_tensor(k, list(v.shape), F32, kind="ExternalInput").ap()
    io["wT_scr"] = nc.dram_tensor("wT_scr", [3, 8, 128, 1024], BF16, kind="Internal").ap()
    io["woT_scr"] = nc.dram_tensor("woT_scr", [8, 8, 128, 128], BF16, kind="Internal").ap()
    for d in dbg:
        shape = {"qcat": [128, H * N], "kcat": [128, H * N], "ybuf": [128, H * N],
                 "xTd": [128, NC * N], "vsld": [128, NT * H * 130],
                 "csd": [128, 8], "dgsd": [128, 8], "misc": [1, 32], "ynT": [128, H * N],
                 "preln": [N, C], "wosd": [128, H*C],
                 "ps1": [128, 256], "ps2": [128, 256], "gd": [128, 256],
                 "s1d": [128, 256], "e1d": [128, 256], "e2d": [128, 256],
                 "ya1": [65, 256], "yb1": [65, 256], "ya2": [65, 256], "yb2": [65, 256],
                 "dens": [1, 512], "rb1d": [128, 256], "rb2d": [128, 256]}[d]
        io["dbg_" + d] = nc.dram_tensor("dbg_" + d, shape, F32, kind="ExternalOutput").ap()

    ctx = Ctx(nc)
    b_ = L0Attn(nc, ctx, io, dbg)
    b_.p0_consts()
    if "csd" in dbg:
        b_.dump("dbg_csd", b_.cs_bc)
        b_.dump("dbg_dgsd", b_.dgs_bc)
    if "misc" in dbg:
        with nc.Block("dmisc") as blk:
            def vm_(v):
                ctx.wait_all(v, "aS"); ctx.wait_all(v, "vS"); ctx.wait_all(v, "tS")
                v.tensor_copy(b_.tsb[0:1, 0:8], b_.scal2[:])
                v.tensor_copy(b_.tsb[0:1, 8:16], b_.csrow[:])
                v.tensor_copy(b_.tsb[0:1, 16:24], b_.stage28[0:1, 0:8])
                v.tensor_copy(b_.tsb[0:1, 24:25], b_.craw[:])
                v.tensor_copy(b_.tsb[0:1, 25:26], b_.scal1[0:1, 0:1])
                ctx.inc(v.tensor_copy(b_.tsb[0:1, 26:27], b_.lam_neg[:]), "vS")
            blk.vector(vm_)
        with nc.Block("dmisc2") as blk:
            def gm_(g):
                ctx.wait_all(g, "vS")
                ctx.inc(g.dma_start(io["dbg_misc"], b_.tsb[0:1, 0:32]), "st", 16)
            blk.gpsimd(gm_)
    b_.p0_weights()
    for b in range(2):
        b_.phase_a(b)
        if b == 0:
            if "xTd" in dbg: b_.dump("dbg_xTd", b_.xT)
        b_.phase_b(b)
        if b == 0:
            if "qcat" in dbg: b_.dump("dbg_qcat", b_.qcat)
            if "kcat" in dbg: b_.dump("dbg_kcat", b_.kcat)
            if "vsld" in dbg: b_.dump("dbg_vsld", b_.vsl)
        b_.phase_c(b)
        if b == 0 and "ybuf" in dbg: b_.dump("dbg_ybuf", b_.ybuf)
        b_.phase_d(b)
        if b == 0 and "ynT" in dbg: b_.dump("dbg_ynT", b_.qcat)
        if b == 0 and "wosd" in dbg: b_.dump("dbg_wosd", b_.wos_all[:, 0:H*C])
    b_.finish()
    nc.finalize()
    return nc


def l0_in_map(xb, w, consts):
    m = {"x_in": xb,
         "Wq": w["L0_attn_Wq"], "Wk": w["L0_attn_Wk"], "Wv": w["L0_attn_Wv"], "Wo": w["L0_attn_Wo"],
         "bo": w["L0_attn_bo"].reshape(1, -1),
         "lambda_q1": w["L0_attn_lambda_q1"].reshape(1, -1), "lambda_k1": w["L0_attn_lambda_k1"].reshape(1, -1),
         "lambda_q2": w["L0_attn_lambda_q2"].reshape(1, -1), "lambda_k2": w["L0_attn_lambda_k2"].reshape(1, -1),
         "delta_gain": w["L0_attn_delta_gain"].reshape(1, -1),
         "cos_head_delta": w["L0_attn_cos_head_delta"].reshape(1, -1),
         "cos_raw": np.asarray(w["L0_attn_cos_logit_scale_raw"]).reshape(1, 1),
         "subln": w["L0_attn_subln_weight"].reshape(-1, 1)}
    m.update(consts)
    return m


class L0FFN:

    def __init__(self, nc, ctx, io, pools=None):
        self.nc, self.ctx, self.io = nc, ctx, io
        self._es = ExitStack()
        A = lambda n, s, d: self._es.enter_context(nc.sbuf_tensor(n, s, d))
        TC = 1024
        self.TC = TC
        self.xbf = A("f_xbf", [128, 8 * 1024], BF16)
        self.xT = A("f_xT", [128, 8 * TC], BF16)
        self.hT = A("f_hT", [128, 22 * TC], BF16)
        self.acc = A("f_acc", [128, 8 * 1024], F32)
        self.wgu_sb = A("f_wgu", [128, 2 * 8 * 128], BF16)
        self.wd_full = A("f_wd", [128, 22 * 1024], BF16)
        self.gw_sb = A("f_gw", [128, 8 * 8], BF16)
        self.sg_sb = A("f_sg", [128, 1024], BF16)
        self.lgall = A("f_lg", [128, 64], F32)
        self.cw = A("f_cw", [128, 64], F32)
        self.m1 = A("f_m1", [128, 8], F32)
        self.exin = A("f_exin", [128, 64], F32)
        self.ex = A("f_ex", [128, 64], F32)
        self.ismax = A("f_ismax", [128, 64], F32)
        self.notmax = A("f_nmx", [128, 64], F32)
        self.em2 = A("f_em2", [128, 64], F32)
        self.m2 = A("f_m2", [128, 8], F32)
        self.den8 = A("f_den", [128, 8], F32)
        self.rr8 = A("f_rr", [128, 8], F32)
        self.w28 = A("f_w2c", [128, 8], F32)
        self.ismax2 = A("f_ismax2", [128, 64], F32)
        self.c1 = A("f_c1", [128, 64], F32)
        self.tsb = A("f_tsb", [128, 1024], F32)
        self.bn6 = A("f_bn6", [128, 12], F32)
        self.mv2 = A("f_mv2", [128, 2], F32)
        self.lnr = A("f_lnr", [128, 1], F32)
        self.lnm = A("f_lnm", [128, 1], F32)
        self.xout_sb = A("f_xout", [128, 1024], F32)
        self.eps_col = A("f_eps", [128, 1], F32)
        self.ident_b = A("f_ident", [128, 128], BF16)
        self.fencea = A("f_fa", [128, 8], F32)
        self.fenceb = A("f_fb", [128, 8], F32)
        self.bigfa = A("f_bfa", [128, 256], BF16)
        self.bigfb = A("f_bfb", [128, 256], BF16)
        self.P, self.PT = pools if pools is not None else mk_pools(nc)

    def vf(self, v):
        self.ctx.inc(v.tensor_copy(self.fenceb[:], self.fencea[:]), "vS")

    def vF(self, v):
        self.ctx.inc(v.tensor_copy(self.bigfb[:], self.bigfa[:]), "vS")

    def af(self, a):
        self.ctx.inc(a.activation(self.fenceb[:], self.fencea[:], AF.Identity), "aS")

    def prologue(self):
        nc, ctx, io = self.nc, self.ctx, self.io
        with nc.Block("f_pro") as blk:
            def g_(g):
                ctx.inc(g.dma_start(self.ident_b[:], io["c_ident"]), "ld", 16)
                ctx.inc(g.dma_start(self.gw_sb[:].rearrange("p (cc e) -> p cc e", e=8),
                                    io["gate_w"].rearrange("(cc p) e -> p cc e", p=128)), "ld", 16)
            blk.gpsimd(g_)

            def v_(v):
                ctx.wait_all(v, "ld")
                v.memset(self.fencea[:], 0.0)
                v.memset(self.bigfa[:], 0.0)
                ctx.inc(v.memset(self.eps_col[:], 1e-5), "vS")
            blk.vector(v_)

    def chunk(self, ci, dbg=False):
        nc, ctx, io = self.nc, self.ctx, self.io
        b = ci // 2
        r0 = (ci % 2) * 1024
        TC = self.TC

        with nc.Block(f"f_a{ci}") as blk:
            def g_(g):
                ctx.wait_all(g, "st")
                for t in range(8):
                    ctx.inc(g.dma_start(self.xbf[:, t * 1024:(t + 1) * 1024],
                                        io["x_in"][b, r0 + t * 128: r0 + (t + 1) * 128, :]), "ld", 16)
            blk.gpsimd(g_)
            ld_t = ctx.cnt["ld"]
            for t in range(8):
                for cc in range(8):
                    slot = (t * 8 + cc) % 2
                    def t_(tt, t=t, cc=cc, slot=slot, ld_t=ld_t):
                        tt.wait_ge(ctx.sem("ld"), ld_t)
                        ctx.wait_all(tt, "vS")
                        ctx.inc(tt.transpose(self.PT[:, slot * 128:slot * 128 + 128],
                                             self.xbf[:, t * 1024 + cc * 128: t * 1024 + (cc + 1) * 128],
                                             self.ident_b[:]), "tS")
                    blk.tensor(t_)

                    def v_(v, t=t, cc=cc, slot=slot):
                        ctx.wait_all(v, "tS")
                        ctx.inc(v.tensor_copy(self.xT[:, cc * TC + t * 128: cc * TC + (t + 1) * 128],
                                              self.PT[:, slot * 128:slot * 128 + 128]), "vS")
                    blk.vector(v_)

        with nc.Block(f"f_g{ci}") as blk:
            for t in range(8):
                def tg_(tt, t=t):
                    ctx.wait_all(tt, "vS")
                    for cc in range(8):
                        ctx.inc(tt.matmul(self.P[4][:, t * 8:(t + 1) * 8],
                                          self.xT[:, cc * TC + t * 128: cc * TC + (t + 1) * 128],
                                          self.gw_sb[:, cc * 8:(cc + 1) * 8],
                                          start=(cc == 0), stop=(cc == 7)), "tS")
                blk.tensor(tg_)
            tS_g = ctx.cnt["tS"]

            def e8(x):
                return x.rearrange("p (t e) -> p t e", e=8)

            def b8(x):
                return x.rearrange("p (t o) -> p t o", o=1).broadcast_to((128, 8, 8))

            def vg_(v, tS_g=tS_g):
                v.wait_ge(ctx.sem("tS"), tS_g)
                self.vF(v)
                v.tensor_copy(self.lgall[:], self.P[4][:, 0:64])
                v.reduce_max(self.m1[:].rearrange("p (t o) -> p t o", o=1), e8(self.lgall[:]), axis=AX)
                self.vF(v)
                self.vF(v)
                v.tensor_tensor(e8(self.exin[:]), e8(self.lgall[:]), b8(self.m1[:]), OP.subtract)
                v.tensor_tensor(e8(self.ismax[:]), e8(self.lgall[:]), b8(self.m1[:]), OP.is_equal)
                self.vF(v)
                ctx.inc(v.tensor_copy(self.bigfa[:], self.bigfb[:]), "vS")
            blk.vector(vg_)
            vS_g = ctx.cnt["vS"]

            def ag_(a, vS_g=vS_g):
                pin_table(a, 6)
                a.wait_ge(ctx.sem("vS"), vS_g)
                a.activation(self.ex[:], self.exin[:], AF.Exp)
                self.af(a)
                self.af(a)
            blk.scalar(ag_)
            aS_g = ctx.cnt["aS"]

            def vg2_(v, aS_g=aS_g):
                v.wait_ge(ctx.sem("aS"), aS_g)
                v.tensor_scalar(self.notmax[:], self.ismax[:], -1.0, 1.0, OP.mult, OP.add)
                self.vF(v)
                v.tensor_tensor(self.em2[:], self.ex[:], self.notmax[:], OP.mult)
                self.vF(v)
                v.reduce_max(self.m2[:].rearrange("p (t o) -> p t o", o=1), e8(self.em2[:]), axis=AX)
                self.vF(v)
                self.vF(v)
                v.tensor_scalar(self.den8[:], self.m2[:], 1.0, None, OP.add)
                self.vF(v)
                v.reciprocal(self.rr8[:], self.den8[:])
                self.vF(v)
                self.vF(v)
                v.tensor_tensor(e8(self.ismax2[:]), e8(self.em2[:]), b8(self.m2[:]), OP.is_equal)
                v.tensor_tensor(self.w28[:], self.m2[:], self.rr8[:], OP.mult)
                self.vF(v)
                v.tensor_tensor(e8(self.c1[:]), e8(self.ismax[:]), b8(self.rr8[:]), OP.mult)
                v.tensor_tensor(e8(self.ismax2[:]), e8(self.ismax2[:]), b8(self.w28[:]), OP.mult)
                self.vF(v)
                v.tensor_tensor(self.cw[:], self.c1[:], self.ismax2[:], OP.add)
                self.vF(v)
                ctx.inc(v.tensor_copy(self.bigfa[:], self.bigfb[:]), "vS")
            blk.vector(vg2_)

        if dbg and ci == 0:
            with nc.Block("f_dcw") as blk:
                def gd_(g):
                    ctx.wait_all(g, "vS")
                    ctx.inc(g.dma_start(io["dbg_cw"], self.cw[:]), "st", 16)
                blk.gpsimd(gd_)

        gate_load_free = [0, 0]
        silu_done = [0, 0]
        last_silu = [0]
        last_mult = [0]
        mult_done = [0, 0]
        for m in range(9):
            wgu_d = io["base_wgu"] if m == 0 else io["exp_wgu"][m - 1]
            wd_d = io["base_wd"] if m == 0 else io["exp_wd"][m - 1]
            with nc.Block(f"f_e{ci}_{m}") as blk:
                def gw2_(g, wd_d=wd_d):
                    ctx.wait_all(g, "tS")
                    for i0 in range(0, 22, 4):
                        i1 = min(i0 + 4, 22)
                        ctx.inc(g.dma_start(self.wd_full[:, i0 * 1024:i1 * 1024].rearrange("p (i c) -> p i c", c=1024),
                                            wd_d[i0 * 128:i1 * 128, :].rearrange("(i p) c -> p i c", p=128)), "ld", 16)
                blk.gpsimd(gw2_)
                ld_d = ctx.cnt["ld"]

                for fp in range(22):
                    for half, fi in [(0, fp), (1, fp + 22)]:
                        def gw_(g, fi=fi, half=half, wgu_d=wgu_d, w=gate_load_free[half]):
                            if w:
                                g.wait_ge(ctx.sem("tS"), w)
                            ctx.inc(g.dma_start(
                                self.wgu_sb[:, half * 1024:(half + 1) * 1024].rearrange("p (cc f) -> p cc f", f=128),
                                wgu_d[:, fi * 128:(fi + 1) * 128].rearrange("(cc p) f -> p cc f", p=128)), "ld", 16)
                        blk.gpsimd(gw_)
                        ld_w = ctx.cnt["ld"]
                        for th in range(2):
                            pg = self.P[half]
                            wait_sem = ("aS", last_silu[0]) if half == 0 else ("vS", last_mult[0])
                            def tm_(t, half=half, th=th, pg=pg, ld_w=ld_w, wait_sem=wait_sem):
                                t.wait_ge(ctx.sem("ld"), ld_w)
                                if wait_sem[1]:
                                    t.wait_ge(ctx.sem(wait_sem[0]), wait_sem[1])
                                for cc in range(8):
                                    ctx.inc(t.matmul(pg[:], self.wgu_sb[:, half * 1024 + cc * 128: half * 1024 + (cc + 1) * 128],
                                                     self.xT[:, cc * TC + th * 512: cc * TC + (th + 1) * 512],
                                                     start=(cc == 0), stop=(cc == 7)), "tS")
                            blk.tensor(tm_)
                            tS_m = ctx.cnt["tS"]
                            if half == 0:
                                def as_(a, th=th, pg=pg, tS_m=tS_m, pin=(fp == 0 and th == 0)):
                                    if pin:
                                        pin_table(a, 18)
                                    a.wait_ge(ctx.sem("tS"), tS_m)
                                    ctx.inc(a.activation(self.sg_sb[:, th * 512:(th + 1) * 512], pg[:], AF.Silu), "aS")
                                blk.scalar(as_)
                                silu_done[th] = ctx.cnt["aS"]
                                last_silu[0] = ctx.cnt["aS"]
                            else:
                                aS_m = silu_done[th]
                                def vm_(v, fp=fp, th=th, pg=pg, tS_m=tS_m, aS_m=aS_m):
                                    v.wait_ge(ctx.sem("tS"), tS_m)
                                    v.wait_ge(ctx.sem("aS"), aS_m)
                                    ctx.inc(v.tensor_tensor(self.hT[:, fp * TC + th * 512: fp * TC + (th + 1) * 512],
                                                            pg[:], self.sg_sb[:, th * 512:(th + 1) * 512], OP.mult), "vS")
                                blk.vector(vm_)
                                mult_done[th] = ctx.cnt["vS"]
                                last_mult[0] = ctx.cnt["vS"]
                        gate_load_free[half] = tS_m

                ev_done = []
                for tt_i in range(8):
                    for cc2 in range(2):
                        po = self.P[2 + (tt_i % 2) * 2 + cc2]
                        wait_ev = ev_done[(tt_i - 2) * 2 + cc2] if tt_i >= 2 else 0
                        def tw_(t, tt_i=tt_i, cc2=cc2, po=po, ld_d=ld_d, wait_ev=wait_ev, md=mult_done[1]):
                            t.wait_ge(ctx.sem("ld"), ld_d)
                            if wait_ev:
                                t.wait_ge(ctx.sem("vS"), wait_ev)
                            else:
                                t.wait_ge(ctx.sem("vS"), md)
                            for ic in range(22):
                                ctx.inc(t.matmul(po[:], self.hT[:, ic * TC + tt_i * 128: ic * TC + (tt_i + 1) * 128],
                                                 self.wd_full[:, ic * 1024 + cc2 * 512: ic * 1024 + (cc2 + 1) * 512],
                                                 start=(ic == 0), stop=(ic == 21)), "tS")
                        blk.tensor(tw_)
                        tS_wd = ctx.cnt["tS"]

                        def vev_(v, m=m, tt_i=tt_i, cc2=cc2, po=po, tS_wd=tS_wd):
                            v.wait_ge(ctx.sem("tS"), tS_wd)
                            dst = self.acc[:, tt_i * 1024 + cc2 * 512: tt_i * 1024 + (cc2 + 1) * 512]
                            if m == 0:
                                ctx.inc(v.tensor_copy(dst, po[:]), "vS")
                            else:
                                ctx.inc(v.scalar_tensor_tensor(dst, po[:], self.cw[:, tt_i * 8 + (m - 1):tt_i * 8 + m],
                                                               dst, OP.mult, OP.add), "vS")
                        blk.vector(vev_)
                        ev_done.append(ctx.cnt["vS"])

        with nc.Block(f"f_l{ci}") as blk:
            for tt_i in range(8):
                vS_pre = ctx.cnt["vS"]
                def v2_(v, tt_i=tt_i, vS_pre=vS_pre):
                    v.wait_ge(ctx.sem("vS"), vS_pre)
                    ctx.wait_all(v, "aS")
                    v.tensor_tensor(self.tsb[:], self.acc[:, tt_i * 1024:(tt_i + 1) * 1024],
                                    self.xbf[:, tt_i * 1024:(tt_i + 1) * 1024], OP.add)
                    v.bn_stats(self.bn6[:, 0:6], self.tsb[:, 0:512])
                    v.bn_stats(self.bn6[:, 6:12], self.tsb[:, 512:1024])
                    self.vf(v)
                    self.vf(v)
                    v.bn_aggr(self.mv2[:], self.bn6[:])
                    self.vf(v)
                    self.vf(v)
                    v.tensor_scalar(self.lnm[:], self.mv2[:, 0:1], -1.0, None, OP.mult)
                    self.vf(v)
                    ctx.inc(v.tensor_copy(self.fencea[:], self.fenceb[:]), "vS")
                blk.vector(v2_)
                vS_ln = ctx.cnt["vS"]

                def a3_(a, vS_ln=vS_ln, pin=(tt_i == 0)):
                    if pin:
                        pin_table(a, 6)
                    a.wait_ge(ctx.sem("vS"), vS_ln)
                    a.activation(self.lnr[:], self.mv2[:, 1:2], AF.Ln, bias=self.eps_col[:])
                    self.af(a)
                    a.activation(self.lnr[:], self.lnr[:], AF.Exp, scale=-0.5)
                    self.af(a)
                    self.af(a)
                blk.scalar(a3_)
                aS_ln = ctx.cnt["aS"]

                def v3_(v, aS_ln=aS_ln):
                    v.wait_ge(ctx.sem("aS"), aS_ln)
                    v.tensor_tensor(self.lnm[:], self.lnm[:], self.lnr[:], OP.mult)
                    self.vf(v)
                    ctx.inc(v.tensor_copy(self.fencea[:], self.fenceb[:]), "vS")
                blk.vector(v3_)
                vS_nm = ctx.cnt["vS"]

                def a4_(a, vS_nm=vS_nm):
                    a.wait_ge(ctx.sem("vS"), vS_nm)
                    ctx.inc(a.activation(self.xout_sb[:], self.tsb[:], AF.Identity,
                                         bias=self.lnm[:], scale=self.lnr[:]), "aS")
                blk.scalar(a4_)
                aS_x = ctx.cnt["aS"]

                def s_(s, b=b, r0=r0, tt_i=tt_i, aS_x=aS_x):
                    s.wait_ge(ctx.sem("aS"), aS_x)
                    ctx.inc(s.dma_start(io["x_out"][b, r0 + tt_i * 128: r0 + (tt_i + 1) * 128, :], self.xout_sb[:]), "st", 16)
                blk.sync(s_)

    def close(self):
        self._es.close()

    def finish(self):
        nc, ctx = self.nc, self.ctx
        with nc.Block("f_fin") as blk:
            def g_(g):
                ctx.wait_all(g, "st")
            blk.gpsimd(g_)


def build_l0ffn_launch(dbg=False):
    nc = bass.Bass("TRN2", target_bir_lowering=False)
    io = {}
    io["x_in"] = nc.dram_tensor("x_in", [2, N, C], F32, kind="ExternalInput").ap()
    io["x_out"] = nc.dram_tensor("x_out", [2, N, C], F32, kind="ExternalOutput").ap()
    io["gate_w"] = nc.dram_tensor("gate_w", [C, 8], F32, kind="ExternalInput").ap()
    io["base_wgu"] = nc.dram_tensor("base_wgu", [C, 5632], F32, kind="ExternalInput").ap()
    io["base_wd"] = nc.dram_tensor("base_wd", [2816, C], F32, kind="ExternalInput").ap()
    io["exp_wgu"] = nc.dram_tensor("exp_wgu", [8, C, 5632], F32, kind="ExternalInput").ap()
    io["exp_wd"] = nc.dram_tensor("exp_wd", [8, 2816, C], F32, kind="ExternalInput").ap()
    io["c_ident"] = nc.dram_tensor("c_ident", [128, 128], F32, kind="ExternalInput").ap()
    if dbg:
        io["dbg_cw"] = nc.dram_tensor("dbg_cw", [128, 64], F32, kind="ExternalOutput").ap()
        io["dbg_acc"] = nc.dram_tensor("dbg_acc", [128, 8192], F32, kind="ExternalOutput").ap()
    ctx = Ctx(nc)
    f = L0FFN(nc, ctx, io)
    f.prologue()
    for ci in range(4):
        f.chunk(ci, dbg=dbg)
    f.finish()
    nc.finalize()
    return nc


class L1Attn:

    def __init__(self, nc, ctx, io, pools=None):
        self.nc, self.ctx, self.io = nc, ctx, io
        self._es = ExitStack()
        A = lambda n, s, d: self._es.enter_context(nc.sbuf_tensor(n, s, d))
        self.xbf = A("a_xbf", [128, 16 * 1024], BF16)
        self.xT = A("a_xT", [128, 8 * N], BF16)
        self.qT = A("a_qT", [128, 8 * N], BF16)
        self.kT = A("a_kT", [128, 2 * N], BF16)
        self.klat = A("a_klat", [32, 2 * N], BF16)
        self.vlat = A("a_vlat", [32, 2 * N], BF16)
        self.vsl = A("a_vsl", [128, 16 * 2 * 130], BF16)
        self.ynT = A("a_ynT", [128, 8 * N], BF16)
        self.wq_sb = A("a_wq", [128, 2 * 1024], BF16)
        self.wkv_sb = A("a_wkv", [128, 512], BF16)
        self.wup_sb = A("a_wup", [32, 256], BF16)
        self.wo_sb = A("a_wo", [128, 8 * 1024], BF16)
        self.masks = A("a_masks", [128, 4 * 512], BF16)
        self.e_sb = A("a_e", [128, 512], BF16)
        self.r1st = A("a_r1st", [2, 512], F32)
        self.rb_sb = A("a_rb", [128, 512], F32)
        self.e2_f = A("a_e2f", [2, 128], F32)
        self.ident_b = A("a_ident", [128, 128], BF16)
        self.tsb = A("a_tsb", [128, 1024], F32)
        self.bn6 = A("a_bn6", [128, 12], F32)
        self.mv2 = A("a_mv2", [128, 2], F32)
        self.lnr = A("a_lnr", [128, 1], F32)
        self.lnm = A("a_lnm", [128, 1], F32)
        self.xout_sb = A("a_xout", [128, 1024], F32)
        self.eps_col = A("a_eps", [128, 1], F32)
        self.fencea = A("a_fa", [128, 8], F32)
        self.fenceb = A("a_fb", [128, 8], F32)
        self.bigfa = A("a_bfa", [128, 256], BF16)
        self.bigfb = A("a_bfb", [128, 256], BF16)
        self.P, self.PT = pools if pools is not None else mk_pools(nc)

    def vf(self, v):
        self.ctx.inc(v.tensor_copy(self.fenceb[:], self.fencea[:]), "vS")

    def vF(self, v):
        self.ctx.inc(v.tensor_copy(self.bigfb[:], self.bigfa[:]), "vS")

    def af(self, a):
        self.ctx.inc(a.activation(self.fenceb[:], self.fencea[:], AF.Identity), "aS")

    def prologue(self):
        nc, ctx, io = self.nc, self.ctx, self.io
        with nc.Block("a_pro") as blk:
            def g_(g):
                ctx.inc(g.dma_start(self.ident_b[:], io["c_ident"]), "ld", 16)
                ctx.inc(g.dma_start(self.masks[:].rearrange("p (m q) -> p m q", q=512),
                                    io["c_masks"].rearrange("m p q -> p m q")), "ld", 16)
                ctx.inc(g.dma_start(self.wup_sb[:, 0:128], io["wk_up"]), "ld", 16)
                ctx.inc(g.dma_start(self.wup_sb[:, 128:256], io["wv_up"]), "ld", 16)
                for jt2 in range(0, 8, 2):
                    ctx.inc(g.dma_start(self.wo_sb[:, jt2 * 1024:(jt2 + 2) * 1024].rearrange("p (jt c) -> p jt c", c=1024),
                                        io["wo"][jt2 * 128:(jt2 + 2) * 128, :].rearrange("(jt p) c -> p jt c", p=128)), "ld", 16)
            blk.gpsimd(g_)

            def s_(s):
                ctx.inc(s.dma_start(self.e2_f[:], io["c_e2"]), "ld", 16)
            blk.sync(s_)

            def v_(v):
                ctx.wait_all(v, "ld")
                v.memset(self.fencea[:], 0.0)
                v.memset(self.bigfa[:], 0.0)
                v.memset(self.r1st[:], 0.0)
                view = self.vsl[:].rearrange("p (s d) -> p s d", d=130)
                v.memset(view[:, :, 64:65], 1.0)
                v.memset(view[:, :, 129:130], 1.0)
                ctx.inc(v.memset(self.eps_col[:], 1e-5), "vS")
            blk.vector(v_)

    def batch(self, b, upto="d"):
        nc, ctx, io = self.nc, self.ctx, self.io
        SC = 0.08838834764831845

        with nc.Block(f"a_a{b}") as blk:
            def g_(g):
                ctx.wait_all(g, "st")
                for t in range(NT):
                    ctx.inc(g.dma_start(self.xbf[:, t * 1024:(t + 1) * 1024],
                                        io["x_in"][b, t * 128:(t + 1) * 128, :]), "ld", 16)
            blk.gpsimd(g_)
            ld_t = ctx.cnt["ld"]
            for t in range(NT):
                for cc in range(8):
                    slot = (t * 8 + cc) % 2
                    def t_(tt, t=t, cc=cc, slot=slot, ld_t=ld_t):
                        tt.wait_ge(ctx.sem("ld"), ld_t)
                        ctx.wait_all(tt, "vS")
                        ctx.inc(tt.transpose(self.PT[:, slot * 128:slot * 128 + 128],
                                             self.xbf[:, t * 1024 + cc * 128: t * 1024 + (cc + 1) * 128],
                                             self.ident_b[:]), "tS")
                    blk.tensor(t_)

                    def v_(v, t=t, cc=cc, slot=slot):
                        ctx.wait_all(v, "tS")
                        ctx.inc(v.tensor_copy(self.xT[:, cc * N + t * 128: cc * N + (t + 1) * 128],
                                              self.PT[:, slot * 128:slot * 128 + 128]), "vS")
                    blk.vector(v_)

        if upto == "a":
            return
        with nc.Block(f"a_b{b}") as blk:
            for jt in range(8):
                slot = jt % 2
                def gw_(g, jt=jt, slot=slot):
                    ctx.wait_all(g, "tS")
                    ctx.inc(g.dma_start(self.wq_sb[:, slot * 1024:(slot + 1) * 1024]
                                        .rearrange("p (cc j) -> p cc j", j=128),
                                        io["wq"][:, jt * 128:(jt + 1) * 128]
                                        .rearrange("(cc p) j -> p cc j", p=128)), "ld", 16)
                blk.gpsimd(gw_)
                ld_w = ctx.cnt["ld"]
                for ch in range(4):
                    pq = self.P[ch % 2]
                    def t_(t, jt=jt, ch=ch, pq=pq, slot=slot, ld_w=ld_w):
                        t.wait_ge(ctx.sem("ld"), ld_w)
                        ctx.wait_all(t, "aS")
                        for cc in range(8):
                            ctx.inc(t.matmul(pq[:], self.wq_sb[:, slot * 1024 + cc * 128: slot * 1024 + (cc + 1) * 128],
                                             self.xT[:, cc * N + ch * 512: cc * N + (ch + 1) * 512],
                                             start=(cc == 0), stop=(cc == 7)), "tS")
                    blk.tensor(t_)
                    tq = ctx.cnt["tS"]

                    def a_(a, jt=jt, ch=ch, pq=pq, tq=tq, pin=(jt == 0 and ch == 0)):
                        if pin:
                            pin_table(a, 6)
                        a.wait_ge(ctx.sem("tS"), tq)
                        ctx.inc(a.activation(self.qT[:, jt * N + ch * 512: jt * N + (ch + 1) * 512],
                                             pq[:], AF.Identity, scale=SC), "aS")
                    blk.scalar(a_)

            for which, dst in [("wk_down", self.klat), ("wv_down", self.vlat)]:
                def gw2_(g, which=which):
                    ctx.wait_all(g, "tS")
                    ctx.inc(g.dma_start(self.wkv_sb[:].rearrange("p (cc l) -> p cc l", l=64),
                                        io[which].rearrange("(cc p) l -> p cc l", p=128)), "ld", 16)
                blk.gpsimd(gw2_)
                ld_w = ctx.cnt["ld"]
                for ch in range(4):
                    pl = self.P[2]
                    def t2_(t, ch=ch, pl=pl, ld_w=ld_w):
                        t.wait_ge(ctx.sem("ld"), ld_w)
                        ctx.wait_all(t, "aS")
                        for cc in range(8):
                            ctx.inc(t.matmul(pl[0:64, 0:512], self.wkv_sb[:, cc * 64:(cc + 1) * 64],
                                             self.xT[:, cc * N + ch * 512: cc * N + (ch + 1) * 512],
                                             start=(cc == 0), stop=(cc == 7)), "tS")
                    blk.tensor(t2_)
                    tl = ctx.cnt["tS"]

                    def a2_(a, ch=ch, dst=dst, pl=pl, tl=tl):
                        a.wait_ge(ctx.sem("tS"), tl)
                        a.activation(dst[0:32, ch * 512:(ch + 1) * 512], pl[0:32, 0:512], AF.Identity)
                        ctx.inc(a.activation(dst[0:32, N + ch * 512: N + (ch + 1) * 512], pl[32:64, 0:512], AF.Identity), "aS")
                    blk.scalar(a2_)

            for g_i in range(2):
                for ch in range(4):
                    pk = self.P[3]
                    def t3_(t, g_i=g_i, ch=ch, pk=pk):
                        ctx.wait_all(t, "aS")
                        ctx.inc(t.matmul(pk[:, 0:512], self.wup_sb[0:32, 0:128],
                                         self.klat[0:32, g_i * N + ch * 512: g_i * N + (ch + 1) * 512],
                                         start=True, stop=True), "tS")
                    blk.tensor(t3_)
                    tk = ctx.cnt["tS"]

                    def a3_(a, g_i=g_i, ch=ch, pk=pk, tk=tk):
                        a.wait_ge(ctx.sem("tS"), tk)
                        ctx.inc(a.activation(self.kT[:, g_i * N + ch * 512: g_i * N + (ch + 1) * 512],
                                             pk[:, 0:512], AF.Identity), "aS")
                    blk.scalar(a3_)

                for tt_i in range(NT):
                    pv = self.P[4]
                    def t4_(t, g_i=g_i, tt_i=tt_i, pv=pv):
                        ctx.wait_all(t, "aS")
                        ctx.inc(t.matmul(pv[:, 0:128], self.vlat[0:32, g_i * N + tt_i * 128: g_i * N + (tt_i + 1) * 128],
                                         self.wup_sb[0:32, 128:256],
                                         start=True, stop=True), "tS")
                    blk.tensor(t4_)
                    tv = ctx.cnt["tS"]

                    def a4_(a, g_i=g_i, tt_i=tt_i, pv=pv, tv=tv):
                        a.wait_ge(ctx.sem("tS"), tv)
                        base = (tt_i * 2 + g_i) * 130
                        src = pv[:, 0:128].rearrange("p (u d) -> p u d", u=2)
                        dst = self.vsl[:, base:base + 130].rearrange("p (u d) -> p u d", d=65)[:, :, 0:64]
                        ctx.inc(a.activation(dst, src, AF.Identity), "aS")
                    blk.scalar(a4_)

        if upto == "b":
            return
        with nc.Block(f"a_c{b}") as blk:
            for h in range(H):
                g_i = h // 4
                for qg in range(4):
                    ya = self.P[3][0:65, 0:512]
                    yb = self.P[4][0:65, 0:512]
                    nkt = qg * 4 + 4
                    exp_done = []
                    av_tS = []
                    for kt in range(nkt):
                        ps = self.P[kt % 2][:, 0:512]
                        wait_e = exp_done[kt - 2] if kt >= 2 else 0

                        def t_(t, h=h, g_i=g_i, qg=qg, kt=kt, ps=ps, wait_e=wait_e):
                            if wait_e:
                                t.wait_ge(ctx.sem("aS"), wait_e)
                            else:
                                ctx.wait_all(t, "vS")
                                ctx.wait_all(t, "aS")
                            ctx.inc(t.matmul(ps, self.kT[:, g_i * N + kt * 128: g_i * N + (kt + 1) * 128],
                                             self.qT[:, h * N + qg * 512: h * N + (qg + 1) * 512],
                                             start=True, stop=True), "tS")
                        blk.tensor(t_)
                        tS_s = ctx.cnt["tS"]
                        wait_av = av_tS[kt - 1] if kt >= 1 else 0

                        def a_(a, kt=kt, qg=qg, ps=ps, tS_s=tS_s, wait_av=wait_av):
                            a.wait_ge(ctx.sem("tS"), max(tS_s, wait_av))
                            ctx.inc(a.activation(self.e_sb[:], ps, AF.Exp), "aS")
                        blk.scalar(a_)
                        exp_done.append(ctx.cnt["aS"])
                        aS_e = ctx.cnt["aS"]

                        m = kt - qg * 4
                        if m >= 0:
                            def vm_(v, m=m, aS_e=aS_e):
                                v.wait_ge(ctx.sem("aS"), aS_e)
                                ctx.inc(v.tensor_tensor(self.e_sb[:], self.e_sb[:],
                                                        self.masks[:, m * 512:(m + 1) * 512], OP.mult), "vS")
                            blk.vector(vm_)
                            gate_sem, gate_cnt = "vS", ctx.cnt["vS"]
                        else:
                            gate_sem, gate_cnt = "aS", aS_e

                        def t2_(t, g_i=g_i, kt=kt, qg=qg, ya=ya, yb=yb, gate_sem=gate_sem, gate_cnt=gate_cnt, nkt=nkt):
                            t.wait_ge(ctx.sem(gate_sem), gate_cnt)
                            base = (kt * 2 + g_i) * 130
                            st_ = (kt == 0); sp = (kt == nkt - 1)
                            ctx.inc(t.matmul(ya, self.vsl[:, base:base + 65], self.e_sb[:], start=st_, stop=sp), "tS")
                            ctx.inc(t.matmul(yb, self.vsl[:, base + 65:base + 130], self.e_sb[:], start=st_, stop=sp), "tS")
                        blk.tensor(t2_)
                        av_tS.append(ctx.cnt["tS"])

                    tS_av = ctx.cnt["tS"]

                    def ve_(v, ya=ya, tS_av=tS_av):
                        v.wait_ge(ctx.sem("tS"), tS_av)
                        self.vF(v)
                        v.reciprocal(self.r1st[0:1, :], ya[64:65, :])
                        self.vF(v)
                        ctx.inc(v.tensor_copy(self.bigfa[:], self.bigfb[:]), "vS")
                    blk.vector(ve_)
                    vS_r = ctx.cnt["vS"]

                    def te_(t, vS_r=vS_r):
                        t.wait_ge(ctx.sem("vS"), vS_r)
                        ctx.inc(t.matmul(self.P[5][:, 0:512], self.e2_f[:], self.r1st[:], start=True, stop=True), "tS")
                    blk.tensor(te_)
                    tS_b = ctx.cnt["tS"]

                    def ve2_(v, h=h, qg=qg, ya=ya, yb=yb, tS_b=tS_b):
                        v.wait_ge(ctx.sem("tS"), tS_b)
                        v.tensor_copy(self.rb_sb[:], self.P[5][:, 0:512])
                        qoff = h * N + qg * 512
                        ctx.inc(v.tensor_tensor(self.ynT[0:64, qoff:qoff + 512], ya[0:64, :],
                                                self.rb_sb[0:64, :], OP.mult), "vS")
                        ctx.inc(v.tensor_tensor(self.ynT[64:128, qoff:qoff + 512], yb[0:64, :],
                                                self.rb_sb[64:128, :], OP.mult), "vS")
                    blk.vector(ve2_)

        if upto == "c":
            return
        with nc.Block(f"a_d{b}") as blk:
            for tt_i in range(NT):
                vS_pre = ctx.cnt["vS"]
                aS_pre = ctx.cnt["aS"]

                def t3_(t, tt_i=tt_i, vS_pre=vS_pre, aS_pre=aS_pre):
                    t.wait_ge(ctx.sem("vS"), vS_pre)
                    t.wait_ge(ctx.sem("aS"), aS_pre)
                    for cc2 in range(2):
                        po = self.P[cc2]
                        for h in range(H):
                            ctx.inc(t.matmul(po[:], self.ynT[:, h * N + tt_i * 128: h * N + (tt_i + 1) * 128],
                                             self.wo_sb[:, h * 1024 + cc2 * 512: h * 1024 + (cc2 + 1) * 512],
                                             start=(h == 0), stop=(h == 7)), "tS")
                blk.tensor(t3_)
                tS_o = ctx.cnt["tS"]

                def v2_(v, tt_i=tt_i, tS_o=tS_o):
                    v.wait_ge(ctx.sem("tS"), tS_o)
                    v.tensor_tensor(self.tsb[:, 0:512], self.P[0][:],
                                    self.xbf[:, tt_i * 1024: tt_i * 1024 + 512], OP.add)
                    v.tensor_tensor(self.tsb[:, 512:1024], self.P[1][:],
                                    self.xbf[:, tt_i * 1024 + 512: (tt_i + 1) * 1024], OP.add)
                    v.bn_stats(self.bn6[:, 0:6], self.tsb[:, 0:512])
                    v.bn_stats(self.bn6[:, 6:12], self.tsb[:, 512:1024])
                    self.vf(v)
                    self.vf(v)
                    v.bn_aggr(self.mv2[:], self.bn6[:])
                    self.vf(v)
                    self.vf(v)
                    v.tensor_scalar(self.lnm[:], self.mv2[:, 0:1], -1.0, None, OP.mult)
                    self.vf(v)
                    ctx.inc(v.tensor_copy(self.fencea[:], self.fenceb[:]), "vS")
                blk.vector(v2_)
                vS_ln = ctx.cnt["vS"]

                def a3_(a, vS_ln=vS_ln, pin=(tt_i == 0)):
                    if pin:
                        pin_table(a, 6)
                    a.wait_ge(ctx.sem("vS"), vS_ln)
                    a.activation(self.lnr[:], self.mv2[:, 1:2], AF.Ln, bias=self.eps_col[:])
                    self.af(a)
                    a.activation(self.lnr[:], self.lnr[:], AF.Exp, scale=-0.5)
                    self.af(a)
                    self.af(a)
                blk.scalar(a3_)
                aS_ln = ctx.cnt["aS"]

                def v3_(v, aS_ln=aS_ln):
                    v.wait_ge(ctx.sem("aS"), aS_ln)
                    v.tensor_tensor(self.lnm[:], self.lnm[:], self.lnr[:], OP.mult)
                    self.vf(v)
                    ctx.inc(v.tensor_copy(self.fencea[:], self.fenceb[:]), "vS")
                blk.vector(v3_)
                vS_nm = ctx.cnt["vS"]

                def a4_(a, vS_nm=vS_nm):
                    a.wait_ge(ctx.sem("vS"), vS_nm)
                    ctx.inc(a.activation(self.xout_sb[:], self.tsb[:], AF.Identity,
                                         bias=self.lnm[:], scale=self.lnr[:]), "aS")
                blk.scalar(a4_)
                aS_x = ctx.cnt["aS"]

                def s_(s, b=b, tt_i=tt_i, aS_x=aS_x):
                    s.wait_ge(ctx.sem("aS"), aS_x)
                    ctx.inc(s.dma_start(io["x_out"][b, tt_i * 128:(tt_i + 1) * 128, :], self.xout_sb[:]), "st", 16)
                blk.sync(s_)

    def close(self):
        self._es.close()

    def finish(self):
        nc, ctx = self.nc, self.ctx
        with nc.Block("a_fin") as blk:
            def g_(g):
                ctx.wait_all(g, "st")
            blk.gpsimd(g_)


def l1_masks():
    masks = np.zeros((4, 128, 512), np.float32)
    for m in range(4):
        for k in range(128):
            q0 = m * 128 + k
            if q0 < 512:
                masks[m, k, q0:] = 1.0
    return masks


def build_l1attn_launch(upto="d"):
    nc = bass.Bass("TRN2", target_bir_lowering=False)
    io = {}
    io["x_in"] = nc.dram_tensor("x_in", [2, N, C], F32, kind="ExternalInput").ap()
    io["x_out"] = nc.dram_tensor("x_out", [2, N, C], F32, kind="ExternalOutput").ap()
    io["wq"] = nc.dram_tensor("wq", [C, C], F32, kind="ExternalInput").ap()
    io["wk_down"] = nc.dram_tensor("wk_down", [C, 64], F32, kind="ExternalInput").ap()
    io["wv_down"] = nc.dram_tensor("wv_down", [C, 64], F32, kind="ExternalInput").ap()
    io["wk_up"] = nc.dram_tensor("wk_up", [32, 128], F32, kind="ExternalInput").ap()
    io["wv_up"] = nc.dram_tensor("wv_up", [32, 128], F32, kind="ExternalInput").ap()
    io["wo"] = nc.dram_tensor("wo", [C, C], F32, kind="ExternalInput").ap()
    io["c_ident"] = nc.dram_tensor("c_ident", [128, 128], F32, kind="ExternalInput").ap()
    io["c_e2"] = nc.dram_tensor("c_e2", [2, 128], F32, kind="ExternalInput").ap()
    io["c_masks"] = nc.dram_tensor("c_masks", [4, 128, 512], F32, kind="ExternalInput").ap()
    ctx = Ctx(nc)
    a_ = L1Attn(nc, ctx, io)
    a_.prologue()
    for b in range(2):
        a_.batch(b, upto=upto)
    a_.finish()
    nc.finalize()
    return nc


class L1FFN:
    CAP = 5120.0

    def __init__(self, nc, ctx, io, totals_only=False, pools=None, fused=False):
        self.nc, self.ctx, self.io = nc, ctx, io
        self.totals_only = totals_only
        self.fused = fused
        self._es = ExitStack()
        A = lambda n, s, d: self._es.enter_context(nc.sbuf_tensor(n, s, d))
        TC = 1024
        self.TC = TC
        self.xbf = A("m_xbf", [128, 8 * 1024], BF16)
        self.xT = A("m_xT", [128, 8 * TC], BF16)
        self.wr_sb = A("m_wr", [128, 64], BF16)
        self.lgall = A("m_lg", [128, 64], F32)
        self.m1 = A("m_m1", [128, 8], F32)
        self.exin = A("m_exin", [128, 64], F32)
        self.ex = A("m_ex", [128, 64], F32)
        self.oh = A("m_oh", [128, 64], F32)
        self.ohb = A("m_ohb", [128, 64], BF16)
        self.sume = A("m_sume", [128, 8], F32)
        self.pmax = A("m_pmax", [128, 8], F32)
        self.ident_b = A("m_ident", [128, 128], BF16)
        self.ones_col = A("m_ones", [128, 1], BF16)
        self.fencea = A("m_fa", [128, 8], F32)
        self.fenceb = A("m_fb", [128, 8], F32)
        self.bigfa = A("m_bfa", [128, 256], BF16)
        self.bigfb = A("m_bfb", [128, 256], BF16)
        self.cnt_sb = A("m_cnt", [1, 8], F32)
        if not totals_only:
            self.hT = A("m_hT", [128, 32 * TC], BF16)
            self.acc = A("m_acc", [128, 8 * 1024], F32)
            self.w1_sb = A("m_w1", [128, 2 * 1024], BF16)
            self.w2_sb = A("m_w2", [128, 8 * 1024], BF16)
            self.ohT = A("m_ohT", [8, TC], F32)
            self.posT = A("m_posT", [8, TC], F32)
            self.runc = A("m_runc", [8, 1], F32)
            self.prodT = A("m_prod", [8, TC], F32)
            self.poT = A("m_poT", [1, TC], F32)
            self.kept2 = A("m_kept2", [2, TC], F32)
            self.kb2 = A("m_kb2", [2, TC], BF16)
            self.keptcol = A("m_keptcol", [128, 1], F32)
            self.tmpcol = A("m_tmpcol", [128, 1], F32)
            self.cwall = A("m_cwall", [128, 64], F32)
            self.passall = A("m_pass", [128, 8], F32)
            self.sel_sb = A("m_sel", [8, 1], F32)
            self.atot_sb = A("m_atot", [8, 8], F32)
            self.off_row = A("m_offr", [1, 8], F32)
            self.off_col = A("m_offc", [8, 1], F32)
            self.ohSave = A("m_ohsave", [128, 4 * 64], F32)
            self.pmaxSave = A("m_pmsave", [128, 4 * 8], F32)
            self.ohTSave = A("m_ohtsave", [8, 4 * 1024], F32)
            self.prodo = A("m_prodo", [8, 8], F32)
            self.tsb = A("m_tsb", [128, 1024], F32)
            self.bn6 = A("m_bn6", [128, 12], F32)
            self.mv2 = A("m_mv2", [128, 2], F32)
            self.lnr = A("m_lnr", [128, 1], F32)
            self.lnm = A("m_lnm", [128, 1], F32)
            self.xout_sb = A("m_xout", [128, 1024], F32)
            self.eps_col = A("m_eps", [128, 1], F32)
        self.P, self.PT = pools if pools is not None else mk_pools(nc)

    def vf(self, v):
        self.ctx.inc(v.tensor_copy(self.fenceb[:], self.fencea[:]), "vS")

    def vF(self, v):
        self.ctx.inc(v.tensor_copy(self.bigfb[:], self.bigfa[:]), "vS")

    def af(self, a):
        self.ctx.inc(a.activation(self.fenceb[:], self.fencea[:], AF.Identity), "aS")

    def prologue(self):
        nc, ctx, io = self.nc, self.ctx, self.io
        with nc.Block("m_pro") as blk:
            def g_(g):
                ctx.inc(g.dma_start(self.ident_b[:], io["c_ident"]), "ld", 16)
                ctx.inc(g.dma_start(self.wr_sb[:].rearrange("p (cc e) -> p cc e", e=8),
                                    io["Wr"].rearrange("(cc p) e -> p cc e", p=128)), "ld", 16)
                if not self.totals_only:
                    ctx.inc(g.dma_start(self.sel_sb[:], io["c_sel"]), "ld", 16)
                    if not self.fused:
                        ctx.inc(g.dma_start(self.atot_sb[:], io["all_totals"]), "ld", 16)
            blk.gpsimd(g_)

            def v_(v):
                ctx.wait_all(v, "ld")
                v.memset(self.fencea[:], 0.0)
                v.memset(self.bigfa[:], 0.0)
                v.memset(self.ones_col[:], 1.0)
                if not self.totals_only:
                    v.memset(self.eps_col[:], 1e-5)
                ctx.inc(v.memset(self.cnt_sb[:], 0.0), "vS")
            blk.vector(v_)

        if not self.totals_only and not self.fused:
            self.emit_offsets()

    def emit_offsets(self):
        nc, ctx, io = self.nc, self.ctx, self.io
        with nc.Block("m_off") as blk:
            def v1_(v):
                ctx.wait_all(v, "ld")
                ctx.wait_all(v, "gS")
                v.tensor_scalar(self.prodo[:], self.atot_sb[:], self.sel_sb[:], None, OP.mult)
                self.vF(v)
                ctx.inc(v.tensor_copy(self.bigfa[:], self.bigfb[:]), "vS")
            blk.vector(v1_)

            def g1_(g):
                ctx.wait_all(g, "vS")
                ctx.inc(g.reduce_sum(self.off_row[:], self.prodo[:], axis=mybir.AxisListType.C), "gS")
            blk.gpsimd(g1_)

            def v2_(v):
                ctx.wait_all(v, "gS")
                self.vF(v)
                v.tensor_scalar(self.off_row[:], self.off_row[:], 1.0, None, OP.subtract)
                self.vF(v)
                ctx.inc(v.tensor_copy(self.bigfa[:], self.bigfb[:]), "vS")
            blk.vector(v2_)

            def s_(s):
                ctx.wait_all(s, "vS")
                ctx.inc(s.dma_start(io["scr8"][None, :], self.off_row[0:1, :]), "st", 16)
            blk.sync(s_)

            def g2_(g):
                ctx.wait_all(g, "st")
                ctx.inc(g.dma_start(self.runc[:], io["scr8"][:, None]), "ld", 16)
            blk.gpsimd(g2_)

    def fused_counts_exchange(self, replica_groups):
        nc, ctx, io = self.nc, self.ctx, self.io
        with nc.Block("m_cc") as blk:
            def vt_(v):
                ctx.wait_all(v, "tS")
                self.vF(v)
                v.tensor_copy(self.cnt_sb[:], self.P[5][0:1, 8:16])
                self.vF(v)
                ctx.inc(v.tensor_copy(self.bigfa[:], self.bigfb[:]), "vS")
            blk.vector(vt_)

            def g_(g):
                ctx.wait_all(g, "vS")
                ctx.inc(g.dma_start(io["cc_in"], self.cnt_sb[0:1, :]), "st", 16)
                ctx.wait_all(g, "st")
                ctx.inc(g.collective_compute("AllGather", OP.bypass, replica_groups=replica_groups,
                                             ins=[io["cc_in"].opt()], outs=[io["cc_out"].opt()]), "gS")
                ctx.wait_all(g, "gS")
                ctx.inc(g.dma_start(self.atot_sb[:], io["cc_out"]), "ld", 16)
            blk.gpsimd(g_)
        self.emit_offsets()

    def chunk(self, ci, phase="all"):
        nc, ctx, io = self.nc, self.ctx, self.io
        b = ci // 2
        r0 = (ci % 2) * 1024
        TC = self.TC

        with nc.Block(f"m_a{ci}_{phase}") as blk:
            def g_(g):
                ctx.wait_all(g, "st")
                for t in range(8):
                    ctx.inc(g.dma_start(self.xbf[:, t * 1024:(t + 1) * 1024],
                                        io["x_in"][b, r0 + t * 128: r0 + (t + 1) * 128, :]), "ld", 16)
            blk.gpsimd(g_)
            ld_t = ctx.cnt["ld"]
            for t in range(8):
                for cc in range(8):
                    slot = (t * 8 + cc) % 2
                    def t_(tt, t=t, cc=cc, slot=slot, ld_t=ld_t):
                        tt.wait_ge(ctx.sem("ld"), ld_t)
                        ctx.wait_all(tt, "vS")
                        ctx.inc(tt.transpose(self.PT[:, slot * 128:slot * 128 + 128],
                                             self.xbf[:, t * 1024 + cc * 128: t * 1024 + (cc + 1) * 128],
                                             self.ident_b[:]), "tS")
                    blk.tensor(t_)

                    def v_(v, t=t, cc=cc, slot=slot):
                        ctx.wait_all(v, "tS")
                        ctx.inc(v.tensor_copy(self.xT[:, cc * TC + t * 128: cc * TC + (t + 1) * 128],
                                              self.PT[:, slot * 128:slot * 128 + 128]), "vS")
                    blk.vector(v_)

        if phase == "compute":
            with nc.Block(f"m_rs{ci}") as blk:
                def vr_(v, ci=ci):
                    ctx.wait_all(v, "vS")
                    v.tensor_copy(self.oh[:], self.ohSave[:, ci * 64:(ci + 1) * 64])
                    v.tensor_copy(self.pmax[:], self.pmaxSave[:, ci * 8:(ci + 1) * 8])
                    v.tensor_copy(self.ohT[:], self.ohTSave[:, ci * 1024:(ci + 1) * 1024])
                    self.vF(v)
                    ctx.inc(v.tensor_copy(self.bigfa[:], self.bigfb[:]), "vS")
                blk.vector(vr_)
        if phase != "compute":
         with nc.Block(f"m_r{ci}") as blk:
            for t in range(8):
                def tg_(tt, t=t):
                    ctx.wait_all(tt, "vS")
                    for cc in range(8):
                        ctx.inc(tt.matmul(self.P[4][:, t * 8:(t + 1) * 8],
                                          self.xT[:, cc * TC + t * 128: cc * TC + (t + 1) * 128],
                                          self.wr_sb[:, cc * 8:(cc + 1) * 8],
                                          start=(cc == 0), stop=(cc == 7)), "tS")
                blk.tensor(tg_)
            tS_g = ctx.cnt["tS"]

            def e8(x):
                return x.rearrange("p (t e) -> p t e", e=8)

            def b8(x):
                return x.rearrange("p (t o) -> p t o", o=1).broadcast_to((128, 8, 8))

            def vg_(v, tS_g=tS_g):
                v.wait_ge(ctx.sem("tS"), tS_g)
                self.vF(v)
                v.tensor_copy(self.lgall[:], self.P[4][:, 0:64])
                v.reduce_max(self.m1[:].rearrange("p (t o) -> p t o", o=1), e8(self.lgall[:]), axis=AX)
                self.vF(v)
                self.vF(v)
                v.tensor_tensor(e8(self.exin[:]), e8(self.lgall[:]), b8(self.m1[:]), OP.subtract)
                v.tensor_tensor(e8(self.oh[:]), e8(self.lgall[:]), b8(self.m1[:]), OP.is_equal)
                self.vF(v)
                v.tensor_copy(self.ohb[:], self.oh[:])
                ctx.inc(v.tensor_copy(self.bigfa[:], self.bigfb[:]), "vS")
            blk.vector(vg_)
            vS_g = ctx.cnt["vS"]

            def ag_(a, vS_g=vS_g):
                pin_table(a, 6)
                a.wait_ge(ctx.sem("vS"), vS_g)
                a.activation(self.ex[:], self.exin[:], AF.Exp)
                self.af(a)
                self.af(a)
            blk.scalar(ag_)
            aS_g = ctx.cnt["aS"]

            def vg2_(v, aS_g=aS_g):
                v.wait_ge(ctx.sem("aS"), aS_g)
                v.reduce_sum(self.sume[:].rearrange("p (t o) -> p t o", o=1), e8(self.ex[:]), axis=AX)
                self.vF(v)
                self.vF(v)
                v.reciprocal(self.pmax[:], self.sume[:])
                self.vF(v)
                self.vF(v)
                ctx.inc(v.tensor_copy(self.bigfa[:], self.bigfb[:]), "vS")
            blk.vector(vg2_)
            vS_p = ctx.cnt["vS"]

            def tc_(t, ci=ci, vS_p=vS_p):
                t.wait_ge(ctx.sem("vS"), vS_p)
                for t_i in range(8):
                    ctx.inc(t.matmul(self.P[5][0:1, 8:16], self.ones_col[:], self.ohb[:, t_i * 8:(t_i + 1) * 8],
                                     start=(ci == 0 and t_i == 0), stop=(ci == 3 and t_i == 7)), "tS")
            blk.tensor(tc_)

        if phase == "route":
            with nc.Block(f"m_sv{ci}") as blk:
                def vsv_(v, ci=ci):
                    ctx.wait_all(v, "aS")
                    v.tensor_copy(self.ohSave[:, ci * 64:(ci + 1) * 64], self.oh[:])
                    v.tensor_copy(self.pmaxSave[:, ci * 8:(ci + 1) * 8], self.pmax[:])
                    self.vF(v)
                    ctx.inc(v.tensor_copy(self.bigfa[:], self.bigfb[:]), "vS")
                blk.vector(vsv_)
                for t in range(8):
                    slot = t % 2
                    def tt_(tt, t=t, slot=slot):
                        ctx.wait_all(tt, "vS")
                        ctx.inc(tt.transpose(self.PT[0:8, slot * 128:slot * 128 + 128],
                                             self.ohb[:, t * 8:(t + 1) * 8], self.ident_b[:]), "tS")
                    blk.tensor(tt_)
                    tS_t = ctx.cnt["tS"]

                    def vt_(v, t=t, ci=ci, slot=slot, tS_t=tS_t):
                        v.wait_ge(ctx.sem("tS"), tS_t)
                        ctx.inc(v.tensor_copy(self.ohTSave[:, ci * 1024 + t * 128: ci * 1024 + (t + 1) * 128],
                                              self.PT[0:8, slot * 128:slot * 128 + 128]), "vS")
                    blk.vector(vt_)
            return

        if self.totals_only:
            if ci == 3:
                with nc.Block("m_tot") as blk:
                    def vt_(v):
                        ctx.wait_all(v, "tS")
                        self.vF(v)
                        v.tensor_copy(self.cnt_sb[:], self.P[5][0:1, 8:16])
                        self.vF(v)
                        ctx.inc(v.tensor_copy(self.bigfa[:], self.bigfb[:]), "vS")
                    blk.vector(vt_)

                    def st_(s):
                        ctx.wait_all(s, "vS")
                        ctx.inc(s.dma_start(io["totals"][None, :], self.cnt_sb[0:1, :]), "st", 16)
                    blk.sync(st_)
            return

        with nc.Block(f"m_p{ci}") as blk:
            if phase != "compute":
                for t in range(8):
                    slot = t % 2
                    def tt_(tt, t=t, slot=slot):
                        ctx.wait_all(tt, "vS")
                        ctx.inc(tt.transpose(self.PT[0:8, slot * 128:slot * 128 + 128],
                                             self.ohb[:, t * 8:(t + 1) * 8], self.ident_b[:]), "tS")
                    blk.tensor(tt_)
                    tS_t = ctx.cnt["tS"]

                    def vt_(v, t=t, slot=slot, tS_t=tS_t):
                        v.wait_ge(ctx.sem("tS"), tS_t)
                        ctx.inc(v.tensor_copy(self.ohT[:, t * 128:(t + 1) * 128],
                                              self.PT[0:8, slot * 128:slot * 128 + 128]), "vS")
                    blk.vector(vt_)

            def vs_(v):
                ctx.wait_all(v, "vS")
                v.tensor_tensor_scan(self.posT[:], self.ohT[:], self.ohT[:], self.runc[:], OP.add, OP.bypass)
                self.vF(v)
                self.vF(v)
                v.tensor_copy(self.runc[:], self.posT[:, TC - 1:TC])
                v.tensor_tensor(self.prodT[:], self.posT[:], self.ohT[:], OP.mult)
                v.memset(self.kept2[:], 0.0)
                self.vF(v)
                ctx.inc(v.tensor_copy(self.bigfa[:], self.bigfb[:]), "vS")
            blk.vector(vs_)
            vS_s = ctx.cnt["vS"]

            def gpo_(g, vS_s=vS_s):
                g.wait_ge(ctx.sem("vS"), vS_s)
                ctx.inc(g.reduce_sum(self.poT[:], self.prodT[:], axis=mybir.AxisListType.C), "gS")
            blk.gpsimd(gpo_)
            gS_po = ctx.cnt["gS"]

            def vko_(v, gS_po=gS_po):
                v.wait_ge(ctx.sem("gS"), gS_po)
                self.vF(v)
                v.tensor_scalar(self.kept2[0:1, :], self.poT[:], L1FFN.CAP, None, OP.is_lt)
                self.vF(v)
                ctx.inc(v.tensor_copy(self.bigfa[:], self.bigfb[:]), "vS")
            blk.vector(vko_)

            def vkb_(v):
                ctx.wait_all(v, "vS")
                v.tensor_copy(self.kb2[:], self.kept2[:])
                self.vF(v)
                ctx.inc(v.tensor_copy(self.bigfa[:], self.bigfb[:]), "vS")
            blk.vector(vkb_)
            vS_kb = ctx.cnt["vS"]

            for t_i in range(8):
                def tkt_(t, t_i=t_i, vS_kb=vS_kb):
                    t.wait_ge(ctx.sem("vS"), vS_kb)
                    ctx.wait_all(t, "vS")
                    ctx.inc(t.transpose(self.PT[:, 0:2], self.kb2[:, t_i * 128:(t_i + 1) * 128],
                                        self.ident_b[0:2, 0:2]), "tS")
                blk.tensor(tkt_)
                tS_kt = ctx.cnt["tS"]

                def vcw_(v, t_i=t_i, tS_kt=tS_kt):
                    v.wait_ge(ctx.sem("tS"), tS_kt)
                    self.vF(v)
                    v.tensor_copy(self.keptcol[:], self.PT[:, 0:1])
                    self.vF(v)
                    v.tensor_tensor(self.tmpcol[:], self.keptcol[:], self.pmax[:, t_i:t_i + 1], OP.mult)
                    v.tensor_scalar(self.passall[:, t_i:t_i + 1], self.keptcol[:], -1.0, 1.0, OP.mult, OP.add)
                    self.vF(v)
                    v.tensor_scalar(self.cwall[:, t_i * 8:(t_i + 1) * 8], self.oh[:, t_i * 8:(t_i + 1) * 8],
                                    self.tmpcol[:], None, OP.mult)
                    self.vF(v)
                    ctx.inc(v.tensor_copy(self.bigfa[:], self.bigfb[:]), "vS")
                blk.vector(vcw_)

            def vi_(v):
                for t in range(8):
                    ctx.inc(v.tensor_scalar(self.acc[:, t * 1024:(t + 1) * 1024],
                                            self.xbf[:, t * 1024:(t + 1) * 1024],
                                            self.passall[:, t:t + 1], None, OP.mult), "vS")
            blk.vector(vi_)

        w1_free = [0, 0]
        last_relu = [0]
        for e in range(8):
            with nc.Block(f"m_e{ci}_{e}") as blk:
                def gw20_(g, e=e):
                    ctx.wait_all(g, "tS")
                    for q in range(2):
                        ctx.inc(g.dma_start(self.w2_sb[:, q * 4096:(q + 1) * 4096].rearrange("p (i c) -> p i c", c=1024),
                                            io["W2"][e][q * 512:(q + 1) * 512, :]
                                            .rearrange("(i p) c -> p i c", p=128)), "ld", 16)
                blk.gpsimd(gw20_)
                ld_q = [ctx.cnt["ld"], 0, 0, 0]

                for fi in range(32):
                    slot = fi % 2
                    def gw_(g, fi=fi, e=e, slot=slot, w=w1_free[fi % 2]):
                        if w:
                            g.wait_ge(ctx.sem("tS"), w)
                        ctx.inc(g.dma_start(
                            self.w1_sb[:, slot * 1024:(slot + 1) * 1024].rearrange("p (cc f) -> p cc f", f=128),
                            io["W1"][e][:, fi * 128:(fi + 1) * 128].rearrange("(cc p) f -> p cc f", p=128)), "ld", 16)
                    blk.gpsimd(gw_)
                    ld_w = ctx.cnt["ld"]
                    for th in range(2):
                        pg = self.P[th]
                        def tm_(t, fi=fi, th=th, pg=pg, ld_w=ld_w, slot=slot, w=last_relu[0]):
                            t.wait_ge(ctx.sem("ld"), ld_w)
                            if w:
                                t.wait_ge(ctx.sem("aS"), w)
                            for cc in range(8):
                                ctx.inc(t.matmul(pg[:], self.w1_sb[:, slot * 1024 + cc * 128: slot * 1024 + (cc + 1) * 128],
                                                 self.xT[:, cc * TC + th * 512: cc * TC + (th + 1) * 512],
                                                 start=(cc == 0), stop=(cc == 7)), "tS")
                        blk.tensor(tm_)
                        tS_m = ctx.cnt["tS"]

                        def am_(a, fi=fi, th=th, pg=pg, tS_m=tS_m):
                            a.wait_ge(ctx.sem("tS"), tS_m)
                            ctx.inc(a.activation(self.hT[:, fi * TC + th * 512: fi * TC + (th + 1) * 512],
                                                 pg[:], AF.Relu), "aS")
                        blk.scalar(am_)
                        last_relu[0] = ctx.cnt["aS"]
                    w1_free[slot] = tS_m

                relu_all = ctx.cnt["aS"]
                for ih in range(4):
                    if ih > 0:
                        def gw2_(g, e=e, ih=ih, w=ctx.cnt["tS"]):
                            g.wait_ge(ctx.sem("tS"), w)
                            for q in range(2):
                                ctx.inc(g.dma_start(self.w2_sb[:, q * 4096:(q + 1) * 4096].rearrange("p (i c) -> p i c", c=1024),
                                                    io["W2"][e][ih * 1024 + q * 512: ih * 1024 + (q + 1) * 512, :]
                                                    .rearrange("(i p) c -> p i c", p=128)), "ld", 16)
                        blk.gpsimd(gw2_)
                        ld_q[ih] = ctx.cnt["ld"]
                    ld_2 = ld_q[ih]
                    ev_done = []
                    for tt_i in range(8):
                        for cc2 in range(2):
                            po = self.P[2 + (tt_i % 2) * 2 + cc2]
                            wait_ev = ev_done[(tt_i - 2) * 2 + cc2] if tt_i >= 2 else 0
                            def tw_(t, ih=ih, tt_i=tt_i, cc2=cc2, po=po, ld_2=ld_2, wait_ev=wait_ev, ra=relu_all):
                                t.wait_ge(ctx.sem("ld"), ld_2)
                                t.wait_ge(ctx.sem("aS"), ra)
                                if wait_ev:
                                    t.wait_ge(ctx.sem("vS"), wait_ev)
                                else:
                                    ctx.wait_all(t, "vS")
                                for ic in range(8):
                                    fi_g = ih * 8 + ic
                                    ctx.inc(t.matmul(po[:], self.hT[:, fi_g * TC + tt_i * 128: fi_g * TC + (tt_i + 1) * 128],
                                                     self.w2_sb[:, ic * 1024 + cc2 * 512: ic * 1024 + (cc2 + 1) * 512],
                                                     start=(ic == 0), stop=(ic == 7)), "tS")
                            blk.tensor(tw_)
                            tS_wd = ctx.cnt["tS"]

                            def vev_(v, e=e, tt_i=tt_i, cc2=cc2, po=po, tS_wd=tS_wd):
                                v.wait_ge(ctx.sem("tS"), tS_wd)
                                dst = self.acc[:, tt_i * 1024 + cc2 * 512: tt_i * 1024 + (cc2 + 1) * 512]
                                ctx.inc(v.scalar_tensor_tensor(dst, po[:], self.cwall[:, tt_i * 8 + e:tt_i * 8 + e + 1],
                                                               dst, OP.mult, OP.add), "vS")
                            blk.vector(vev_)
                            ev_done.append(ctx.cnt["vS"])

        with nc.Block(f"m_l{ci}") as blk:
            for tt_i in range(8):
                vS_pre = ctx.cnt["vS"]
                def v2_(v, tt_i=tt_i, vS_pre=vS_pre):
                    v.wait_ge(ctx.sem("vS"), vS_pre)
                    ctx.wait_all(v, "aS")
                    v.tensor_tensor(self.tsb[:], self.acc[:, tt_i * 1024:(tt_i + 1) * 1024],
                                    self.xbf[:, tt_i * 1024:(tt_i + 1) * 1024], OP.add)
                    v.bn_stats(self.bn6[:, 0:6], self.tsb[:, 0:512])
                    v.bn_stats(self.bn6[:, 6:12], self.tsb[:, 512:1024])
                    self.vf(v)
                    self.vf(v)
                    v.bn_aggr(self.mv2[:], self.bn6[:])
                    self.vf(v)
                    self.vf(v)
                    v.tensor_scalar(self.lnm[:], self.mv2[:, 0:1], -1.0, None, OP.mult)
                    self.vf(v)
                    ctx.inc(v.tensor_copy(self.fencea[:], self.fenceb[:]), "vS")
                blk.vector(v2_)
                vS_ln = ctx.cnt["vS"]

                def a3_(a, vS_ln=vS_ln, pin=(tt_i == 0)):
                    if pin:
                        pin_table(a, 6)
                    a.wait_ge(ctx.sem("vS"), vS_ln)
                    a.activation(self.lnr[:], self.mv2[:, 1:2], AF.Ln, bias=self.eps_col[:])
                    self.af(a)
                    a.activation(self.lnr[:], self.lnr[:], AF.Exp, scale=-0.5)
                    self.af(a)
                    self.af(a)
                blk.scalar(a3_)
                aS_ln = ctx.cnt["aS"]

                def v3_(v, aS_ln=aS_ln):
                    v.wait_ge(ctx.sem("aS"), aS_ln)
                    v.tensor_tensor(self.lnm[:], self.lnm[:], self.lnr[:], OP.mult)
                    self.vf(v)
                    ctx.inc(v.tensor_copy(self.fencea[:], self.fenceb[:]), "vS")
                blk.vector(v3_)
                vS_nm = ctx.cnt["vS"]

                def a4_(a, vS_nm=vS_nm):
                    a.wait_ge(ctx.sem("vS"), vS_nm)
                    ctx.inc(a.activation(self.xout_sb[:], self.tsb[:], AF.Identity,
                                         bias=self.lnm[:], scale=self.lnr[:]), "aS")
                blk.scalar(a4_)
                aS_x = ctx.cnt["aS"]

                def s_(s, b=b, r0=r0, tt_i=tt_i, aS_x=aS_x):
                    s.wait_ge(ctx.sem("aS"), aS_x)
                    ctx.inc(s.dma_start(io["x_out"][b, r0 + tt_i * 128: r0 + (tt_i + 1) * 128, :], self.xout_sb[:]), "st", 16)
                blk.sync(s_)

    def close(self):
        self._es.close()

    def finish(self):
        nc, ctx = self.nc, self.ctx
        with nc.Block("m_fin") as blk:
            def g_(g):
                ctx.wait_all(g, "st")
            blk.gpsimd(g_)


def build_l1ffn_launch(totals_only=False):
    nc = bass.Bass("TRN2", target_bir_lowering=False)
    io = {}
    io["x_in"] = nc.dram_tensor("x_in", [2, N, C], F32, kind="ExternalInput").ap()
    io["Wr"] = nc.dram_tensor("Wr", [C, 8], F32, kind="ExternalInput").ap()
    io["c_ident"] = nc.dram_tensor("c_ident", [128, 128], F32, kind="ExternalInput").ap()
    if totals_only:
        io["totals"] = nc.dram_tensor("totals", [8], F32, kind="ExternalOutput").ap()
    else:
        io["x_out"] = nc.dram_tensor("x_out", [2, N, C], F32, kind="ExternalOutput").ap()
        io["W1"] = nc.dram_tensor("W1", [8, C, 4096], F32, kind="ExternalInput").ap()
        io["W2"] = nc.dram_tensor("W2", [8, 4096, C], F32, kind="ExternalInput").ap()
        io["c_sel"] = nc.dram_tensor("c_sel", [8, 1], F32, kind="ExternalInput").ap()
        io["all_totals"] = nc.dram_tensor("all_totals", [8, 8], F32, kind="ExternalInput").ap()
        io["scr8"] = nc.dram_tensor("scr8", [8], F32, kind="Internal").ap()
    ctx = Ctx(nc)
    f = L1FFN(nc, ctx, io, totals_only=totals_only)
    f.prologue()
    for ci in range(4):
        f.chunk(ci)
    f.finish()
    nc.finalize()
    return nc


def build_fused_launch():
    nc = bass.Bass("TRN2", target_bir_lowering=False)
    io = {}
    io["x0"] = nc.dram_tensor("x0", [2, N, C], F32, kind="ExternalInput").ap()
    io["x_out_f"] = nc.dram_tensor("x_out_f", [2, N, C], F32, kind="ExternalOutput").ap()
    for nm in ["x1s", "x2s", "x3s"]:
        io[nm] = nc.dram_tensor(nm, [2, N, C], F32, kind="Internal").ap()
    for w in ["Wq", "Wk", "Wv", "Wo"]:
        io[w] = nc.dram_tensor(w, [C, C], F32, kind="ExternalInput").ap()
    io["bo"] = nc.dram_tensor("bo", [1, C], F32, kind="ExternalInput").ap()
    for nm in ["lambda_q1", "lambda_k1", "lambda_q2", "lambda_k2"]:
        io[nm] = nc.dram_tensor(nm, [1, 64], F32, kind="ExternalInput").ap()
    io["delta_gain"] = nc.dram_tensor("delta_gain", [1, 8], F32, kind="ExternalInput").ap()
    io["cos_head_delta"] = nc.dram_tensor("cos_head_delta", [1, 8], F32, kind="ExternalInput").ap()
    io["cos_raw"] = nc.dram_tensor("cos_raw", [1, 1], F32, kind="ExternalInput").ap()
    io["subln"] = nc.dram_tensor("subln", [128, 1], F32, kind="ExternalInput").ap()
    io["wT_scr"] = nc.dram_tensor("wT_scr", [3, 8, 128, 1024], BF16, kind="Internal").ap()
    io["woT_scr"] = nc.dram_tensor("woT_scr", [8, 8, 128, 128], BF16, kind="Internal").ap()
    io["gate_w"] = nc.dram_tensor("gate_w", [C, 8], F32, kind="ExternalInput").ap()
    io["base_wgu"] = nc.dram_tensor("base_wgu", [C, 5632], F32, kind="ExternalInput").ap()
    io["base_wd"] = nc.dram_tensor("base_wd", [2816, C], F32, kind="ExternalInput").ap()
    io["exp_wgu"] = nc.dram_tensor("exp_wgu", [8, C, 5632], F32, kind="ExternalInput").ap()
    io["exp_wd"] = nc.dram_tensor("exp_wd", [8, 2816, C], F32, kind="ExternalInput").ap()
    io["wq"] = nc.dram_tensor("wq", [C, C], F32, kind="ExternalInput").ap()
    io["wk_down"] = nc.dram_tensor("wk_down", [C, 64], F32, kind="ExternalInput").ap()
    io["wv_down"] = nc.dram_tensor("wv_down", [C, 64], F32, kind="ExternalInput").ap()
    io["wk_up"] = nc.dram_tensor("wk_up", [32, 128], F32, kind="ExternalInput").ap()
    io["wv_up"] = nc.dram_tensor("wv_up", [32, 128], F32, kind="ExternalInput").ap()
    io["wo"] = nc.dram_tensor("wo", [C, C], F32, kind="ExternalInput").ap()
    io["Wr"] = nc.dram_tensor("Wr", [C, 8], F32, kind="ExternalInput").ap()
    io["W1"] = nc.dram_tensor("W1", [8, C, 4096], F32, kind="ExternalInput").ap()
    io["W2"] = nc.dram_tensor("W2", [8, 4096, C], F32, kind="ExternalInput").ap()
    io["c_sel"] = nc.dram_tensor("c_sel", [8, 1], F32, kind="ExternalInput").ap()
    io["scr8"] = nc.dram_tensor("scr8", [8], F32, kind="Internal").ap()
    rg = [list(range(8))]
    io["cc_in"] = nc.dram_tensor("cc_in", [1, 8], F32, kind="Internal").ap()
    io["cc_out"] = nc.dram_tensor("cc_out", [8, 8], F32, kind="Internal").ap()
    for k, v in host_consts().items():
        io[k] = nc.dram_tensor(k, list(v.shape), F32, kind="ExternalInput").ap()
    io["c_masks"] = nc.dram_tensor("c_masks", [4, 128, 512], F32, kind="ExternalInput").ap()

    pools = mk_pools(nc)
    ctx = Ctx(nc)

    io_a = dict(io); io_a["x_in"] = io["x0"]; io_a["x_out"] = io["x1s"]
    b1 = L0Attn(nc, ctx, io_a, pools=pools)
    b1.p0_consts(); b1.p0_weights()
    for b in range(2):
        b1.phase_a(b); b1.phase_b(b); b1.phase_c(b); b1.phase_d(b)
    b1.close()

    io_f = dict(io); io_f["x_in"] = io["x1s"]; io_f["x_out"] = io["x2s"]
    b2 = L0FFN(nc, ctx, io_f, pools=pools)
    b2.prologue()
    for ci in range(4):
        b2.chunk(ci)
    b2.close()

    io_l = dict(io); io_l["x_in"] = io["x2s"]; io_l["x_out"] = io["x3s"]
    b3 = L1Attn(nc, ctx, io_l, pools=pools)
    b3.prologue()
    for b in range(2):
        b3.batch(b)
    b3.close()

    io_m = dict(io); io_m["x_in"] = io["x3s"]; io_m["x_out"] = io["x_out_f"]
    b4 = L1FFN(nc, ctx, io_m, pools=pools, fused=True)
    b4.prologue()
    for ci in range(4):
        b4.chunk(ci, phase="route")
    b4.fused_counts_exchange(rg)
    for ci in range(4):
        b4.chunk(ci, phase="compute")
    b4.finish()
    b4.close()
    nc.finalize()
    return nc


def fused_in_map(xb, inp, core_id):
    consts = host_consts()
    sel = np.zeros((8, 1), np.float32); sel[:core_id] = 1.0
    m = {"x0": xb,
         "Wq": inp["L0_attn_Wq"], "Wk": inp["L0_attn_Wk"], "Wv": inp["L0_attn_Wv"], "Wo": inp["L0_attn_Wo"],
         "bo": inp["L0_attn_bo"].reshape(1, -1),
         "lambda_q1": inp["L0_attn_lambda_q1"].reshape(1, -1), "lambda_k1": inp["L0_attn_lambda_k1"].reshape(1, -1),
         "lambda_q2": inp["L0_attn_lambda_q2"].reshape(1, -1), "lambda_k2": inp["L0_attn_lambda_k2"].reshape(1, -1),
         "delta_gain": inp["L0_attn_delta_gain"].reshape(1, -1),
         "cos_head_delta": inp["L0_attn_cos_head_delta"].reshape(1, -1),
         "cos_raw": np.asarray(inp["L0_attn_cos_logit_scale_raw"]).reshape(1, 1),
         "subln": inp["L0_attn_subln_weight"].reshape(-1, 1),
         "gate_w": inp["L0_ffn_gate_w"], "base_wgu": inp["L0_ffn_base_wgu"], "base_wd": inp["L0_ffn_base_wd"],
         "exp_wgu": inp["L0_ffn_exp_wgu"], "exp_wd": inp["L0_ffn_exp_wd"],
         "wq": inp["L1_attn_wq"], "wk_down": inp["L1_attn_wk_down"], "wv_down": inp["L1_attn_wv_down"],
         "wk_up": inp["L1_attn_wk_up"], "wv_up": inp["L1_attn_wv_up"], "wo": inp["L1_attn_wo"],
         "Wr": inp["L1_ffn_Wr"], "W1": inp["L1_ffn_W1"], "W2": inp["L1_ffn_W2"],
         "c_sel": sel, "c_masks": l1_masks()}
    m.update(consts)
    return m


_CACHE = {}


def kernel(**inp):
    n_cores = 8
    if "fused" not in _CACHE:
        _CACHE["fused"] = build_fused_launch()
    nc = _CACHE["fused"]
    x = np.ascontiguousarray(inp["L0_attn_x"])
    maps = [fused_in_map(x[2 * c:2 * c + 2], inp, c) for c in range(n_cores)]
    res = run_bass_kernel_spmd(nc, maps, core_ids=list(range(n_cores)))
    x4 = np.concatenate([np.asarray(r["x_out_f"]) for r in res.results], axis=0)
    return x4.astype(np.float32)
